# version 09
# baseline (speedup 1.0000x reference)
import os

import numpy as np

import concourse.bass as bass
import concourse.mybir as mybir
from concourse.bass_utils import run_bass_kernel_spmd

F32 = mybir.dt.float32
BF16 = mybir.dt.bfloat16
U32 = mybir.dt.uint32
I32 = mybir.dt.int32
AL = mybir.AluOpType
AF = mybir.ActivationFunctionType
AX = mybir.AxisListType

MAGIC = 12582912.0
EPS = 1e-5
QB = 127.0

D = 2048
S = 4096
T = 2048
HLOC = 8
NSUB = 4
SUBT = 1024
HID2 = 8192
DBG = os.environ.get("MK_TAPS", "")
SERIAL = os.environ.get("MK_SERIAL", "")

ENGS = ("te", "ve", "sc", "gp", "sy")


class Prog:
    def __init__(self, nc):
        self.nc = nc
        self.ops = {e: [] for e in ENGS}
        self.cnt = {}
        self.sems = {}
        self.wm = {e: {} for e in ENGS}
        self._stack = []

    def sem(self, name):
        if name not in self.sems:
            cm = self.nc.semaphore(name)
            self.sems[name] = cm.__enter__()
            self._stack.append(cm)
            self.cnt[name] = 0
        return self.sems[name]

    def _waits(self, eng, deps):
        w = []
        wm = self.wm[eng]
        for d in deps:
            if d is None:
                continue
            s, v = d
            if v > 0 and wm.get(s, 0) < v:
                w.append((self.sems[s], v))
                wm[s] = v
        return w

    def _serial_deps(self):
        return tuple((k, v) for k, v in self.cnt.items()
                     if v > 0 and k != "d_poll")

    def comp(self, eng, fn, deps=(), sem=None):
        if SERIAL:
            deps = tuple(deps) + self._serial_deps()
        sname = sem or ("e_" + eng)
        self.sem(sname)
        self.cnt[sname] += 1
        tok = (sname, self.cnt[sname])
        self.ops[eng].append((self._waits(eng, deps), fn, (self.sems[sname], 1)))
        return tok

    def dma(self, eng, sname, fn, deps=()):
        if SERIAL:
            deps = tuple(deps) + self._serial_deps()
        self.sem(sname)
        self.cnt[sname] += 16
        tok = (sname, self.cnt[sname])
        self.ops[eng].append((self._waits(eng, deps), fn, (self.sems[sname], 16)))
        return tok

    def raw(self, eng, fn, deps=()):
        self.ops[eng].append((self._waits(eng, deps), fn, None))

    def tok(self, sname):
        return (sname, self.cnt.get(sname, 0))

    def emit(self, eng, e):
        for waits, fn, inc in self.ops[eng]:
            for s, v in waits:
                e.wait_ge(s, v)
            if inc is None:
                fn(e, self)
            else:
                fn(e).then_inc(inc[0], inc[1])


def build_program():
    nc = bass.Bass("TRN2", target_bir_lowering=False)

    def inp(name, shape, dt=F32):
        return nc.dram_tensor(name, shape, dt, kind="ExternalInput").ap()

    x_full = inp("x_full", [S, D])
    x_half = inp("x_half", [T, D])
    wq_h = inp("wq_h", [D, 1024])
    wk_h = inp("wk_h", [D, 1024])
    wv_h = inp("wv_h", [D, 1024])
    wg_h = inp("wg_h", [D, 1024])
    wo_h = inp("wo_h", [1024, D])
    bq_h = inp("bq_h", [1, 1024])
    bk_h = inp("bk_h", [1, 1024])
    bv_h = inp("bv_h", [1, 1024])
    bg_h = inp("bg_h", [1, 1024])
    bo_f = inp("bo_f", [1, D])
    gnw_h = inp("gnw_h", [1, 1024])
    gnb_h = inp("gnb_h", [1, 1024])
    gam_h = inp("gam_h", [HLOC, 1])
    wnorm_f = inp("wnorm_f", [1, D])
    wgv_hf = inp("wgv_hf", [HID2, D])
    wout_hf = inp("wout_hf", [D, 4096])
    par_h = inp("par_h", [1, 1], U32)
    non_h = inp("non_h", [1, 1], U32)

    y_out = nc.dram_tensor("y_out", [T, D], F32, kind="ExternalOutput").ap()

    def shd(name, shape, dt=F32):
        return nc.dram_tensor(name, shape, dt, kind="Internal", addr_space="Shared").ap()

    sh_wgv = shd("sh_wgv", [2 * HID2, D], BF16)
    sh_woutT = shd("sh_woutT", [T, HID2], BF16)
    sh_ye = shd("sh_ye", [S, D])
    sh_yo = shd("sh_yo", [S, D])
    sh_stat_e = shd("sh_stat_e", [1, 2])
    sh_stat_o = shd("sh_stat_o", [1, 2])
    sh_flags = shd("sh_flags", [1, 8], U32)

    def loc(name, shape, dt=F32):
        return nc.dram_tensor(name, shape, dt, kind="Internal").ap()

    scr_tab = loc("scr_tab", [24, 132])
    scr_gc = loc("scr_gc", [1, 8])
    scr_wm = loc("scr_wm", [1, 4])
    scr_sc2 = loc("scr_sc2", [1, T])
    scr_gn = loc("scr_gn", [2, 1024])
    ln0f_sp = loc("ln0f_sp", [D, S], BF16)
    ret_sp = loc("ret_sp", [S, 1024])
    u_sp = loc("u_sp", [T, HID2])
    x1_sp = loc("x1_sp", [T, D])
    fl_sp = loc("fl_sp", [T, D])
    wstage_out = loc("wstage_out", [D, 4096], BF16)

    taps = {}
    if DBG:
        for nm, shape, dt in [("tap_ret", [S, 1024], F32), ("tap_ln0f", [D, S], BF16),
                              ("tap_y", [S, D], F32), ("tap_x1", [T, D], F32),
                              ("tap_u", [T, HID2], F32), ("tap_fl", [T, D], F32),
                              ("tap_gn", [2, 1024], F32), ("tap_wm", [1, 4], F32),
                              ("tap_sc2", [1, T], F32)]:
            taps[nm] = nc.dram_tensor(nm, shape, dt, kind="ExternalOutput").ap()

    P = Prog(nc)
    cms = []
    bdep = {}
    pdep = [None] * 8

    def bd(key):
        return tuple(bdep.get(key) or ())

    def setb(key, *toks):
        bdep[key] = [t for t in toks if t is not None]

    def pd(i):
        return tuple(pdep[i] or ())

    def setp(i, *toks):
        pdep[i] = [t for t in toks if t is not None]

    def sb(name, shape, dt):
        cm = nc.sbuf_tensor(name, shape, dt)
        t = cm.__enter__()
        cms.append(cm)
        return t.ap() if hasattr(t, "ap") else t

    def close_scope(mark):
        while len(cms) > mark:
            cms.pop().__exit__(None, None, None)

    ps = []
    for i in range(8):
        cm = nc.psum_tensor("pb%d" % i, [128, 512], F32)
        t = cm.__enter__()
        cms.append(cm)
        ps.append(t.ap() if hasattr(t, "ap") else t)

    par_s = sb("par_s", [1, 1], U32)
    non_s = sb("non_s", [1, 1], U32)
    flg_s = sb("flg_s", [1, 8], U32)
    ident = sb("ident", [128, 128], BF16)
    idf = sb("idf", [128, 128], F32)
    ones_f = sb("ones_f", [128, 128], F32)
    col_a = sb("col_a", [128, 1], F32)
    col_b = sb("col_b", [128, 1], F32)
    col_c = sb("col_c", [128, 1], F32)
    col_d = sb("col_d", [128, 1], F32)
    col_e = sb("col_e", [128, 1], F32)
    col_f = sb("col_f", [128, 1], F32)
    m2c = sb("m2c", [128, 16], F32)
    rs2c = sb("rs2c", [128, 16], F32)
    srsc = sb("srsc", [128, 16], F32)
    scgvc = sb("scgvc", [128, 16], F32)
    wmcols = sb("wmcols", [128, 4], F32)
    sc_scr = sb("sc_scr", [128, D], F32)

    t0 = P.dma("gp", "d_c", lambda e: e.dma_start(out=par_s[:, :], in_=par_h))
    t0b = P.dma("gp", "d_c", lambda e: e.dma_start(out=non_s[:, :], in_=non_h))

    state = {}

    def sy_prologue(e, Pr):
        state["rp"] = e.alloc_register("rp")
        state["rt"] = e.alloc_register("rt")
        pv = e.value_load(par_s[0:1, 0:1])
        e.reg_mov(state["rp"], pv)
        e.reg_mov(state["rt"], 0)

    P.raw("sy", sy_prologue, deps=(t0, t0b))

    def gp_prologue(e, Pr):
        state["rp_gp"] = e.alloc_register("rp")
        pv = e.value_load(par_s[0:1, 0:1])
        e.reg_mov(state["rp_gp"], pv)

    P.raw("gp", gp_prologue, deps=(t0, t0b))
    psem = P.sem("d_poll")

    def branch_dma(sname, fn_e, fn_o, deps=()):
        P.sem(sname)
        P.cnt[sname] += 16
        tok = (sname, P.cnt[sname])
        s = P.sems[sname]

        def cl(e, Pr):
            with e.If_eq(state["rp"], 0):
                fn_e(e).then_inc(s, 16)
            with e.Else():
                fn_o(e).then_inc(s, 16)
        P.raw("sy", cl, deps=deps)
        return tok

    def branch_dma_gp(sname, fn_e, fn_o, deps=()):
        P.sem(sname)
        P.cnt[sname] += 16
        tok = (sname, P.cnt[sname])
        s = P.sems[sname]

        def cl(e, Pr):
            with e.If_eq(state["rp_gp"], 0):
                fn_e(e).then_inc(s, 16)
            with e.Else():
                fn_o(e).then_inc(s, 16)
        P.raw("gp", cl, deps=deps)
        return tok

    def set_flag(slot_even, deps=()):
        return branch_dma(
            "d_sh",
            lambda e: e.dma_start(out=sh_flags[0:1, slot_even:slot_even + 1], in_=non_s[:, :]),
            lambda e: e.dma_start(out=sh_flags[0:1, slot_even + 1:slot_even + 2], in_=non_s[:, :]),
            deps=deps)

    def poll(slot_even, deps=()):
        P.sem("e_sy")
        P.cnt["e_sy"] += 1
        tok = ("e_sy", P.cnt["e_sy"])
        esy = P.sems["e_sy"]

        def cl(e, Pr):
            def cond():
                e.reg_add(state["rt"], state["rt"], 16)
                e.dma_start(out=flg_s[0:1, :], in_=sh_flags).then_inc(psem, 16)
                e.wait_ge(psem, state["rt"])
                fe = e.value_load(flg_s[0:1, slot_even:slot_even + 1])
                fo = e.value_load(flg_s[0:1, slot_even + 1:slot_even + 2])
                n = e.value_load(non_s[0:1, 0:1])
                return (fe != n) | (fo != n)
            with e.While(cond):
                e.nop(cycle_cnt=500)
            e.nop().then_inc(esy, 1)
        P.raw("sy", cl, deps=deps)
        return tok

    tones_bf = P.comp("gp", lambda e: e.memset(ident[:, :], 1.0))
    tid = P.comp("gp", lambda e: e.affine_select(
        ident[:, :], ident[:, :], [[1, 128]], AL.is_equal, 0.0,
        base=0, channel_multiplier=-1), (tones_bf,))
    tones = P.comp("gp", lambda e: e.memset(ones_f[:, :], 1.0))
    tidf = P.comp("gp", lambda e: e.affine_select(
        idf[:, :], ones_f[:, :], [[1, 128]], AL.is_equal, 0.0,
        base=0, channel_multiplier=-1), (tones,))
    P.comp("sc", lambda e: e.preload_act_table(AF.Silu))

    cols = {"g": None}

    def emit_rsqrt(dst, src, scr, deps):
        t = P.comp("ve", lambda e: e.tensor_scalar(
            dst.bitcast(I32), src.bitcast(I32), 1, None, AL.arith_shift_right), deps)
        t = P.comp("ve", lambda e: e.tensor_scalar(
            dst.bitcast(I32), dst.bitcast(I32), 0, None, AL.bitwise_not), (t,))
        t = P.comp("ve", lambda e: e.tensor_scalar(
            dst.bitcast(I32), dst.bitcast(I32), 0x5F3759E0, None, AL.add), (t,))
        for _ in range(3):
            t = P.comp("ve", lambda e: e.tensor_tensor(scr, dst, dst, AL.mult), (t,))
            t = P.comp("ve", lambda e: e.tensor_tensor(scr, scr, src, AL.mult), (t,))
            t = P.comp("ve", lambda e: e.tensor_scalar(scr, scr, -0.5, 1.5, AL.mult, AL.add), (t,))
            t = P.comp("ve", lambda e: e.tensor_tensor(dst, dst, scr, AL.mult), (t,))
        return t

    def emit_ln_stats(xap, width, deps):
        inv = 1.0 / float(width)
        dd = tuple(deps) + ((cols["g"],) if cols["g"] else ())
        ts1 = P.comp("sc", lambda e: e.activation(
            sc_scr[:, 0:width], xap, AF.Copy, accum_out=col_a[:, :]), dd)
        ts2 = P.comp("sc", lambda e: e.activation(
            sc_scr[:, 0:width], xap, AF.Square, accum_out=col_d[:, :]), (ts1,))
        t1 = P.comp("ve", lambda e: e.tensor_scalar(col_e[:, :], col_a[:, :], inv, None, AL.mult), (ts1,))
        t2 = P.comp("ve", lambda e: e.tensor_scalar(col_d[:, :], col_d[:, :], inv, None, AL.mult), (ts2,))
        t3 = P.comp("ve", lambda e: e.tensor_tensor(col_c[:, :], col_e[:, :], col_e[:, :], AL.mult), (t1,))
        t4 = P.comp("ve", lambda e: e.tensor_tensor(col_d[:, :], col_d[:, :], col_c[:, :], AL.subtract), (t2, t3))
        t5 = P.comp("ve", lambda e: e.tensor_scalar(col_d[:, :], col_d[:, :], EPS, None, AL.add), (t4,))
        return emit_rsqrt(col_b, col_d, col_c, (t5,)), ts2

    wsb = [sb("wsb%d" % i, [128, 2048], F32) for i in range(2)]
    wqb_ = sb("wqb_", [128, 2048], F32)
    wqo = [sb("wqo%d" % i, [128, 2048], BF16) for i in range(2)]
    trb = sb("trb", [128, 2048], BF16)

    markA = len(cms)
    pw = sb("pw", [HLOC, 132], F32)
    ipw = sb("ipw", [HLOC, 132], F32)
    rpw = sb("rpw", [HLOC, 132], F32)
    gam_s = sb("gam_s", [HLOC, 1], F32)
    tri = sb("tri", [128, 128], F32)
    dqt = sb("dqt", [128, HLOC * 128], F32)
    decS = sb("decS", [128, HLOC], F32)
    minv = sb("minv", [128, HLOC], F32)
    gc_cols = sb("gc_cols", [128, HLOC], F32)
    bqc = sb("bqc", [128, HLOC], F32)
    bkc = sb("bkc", [128, HLOC], F32)
    bv_rep = sb("bv_rep", [128, 1024], F32)
    gnw_hd = sb("gnw_hd", [HLOC, 128], F32)
    gnb_hd = sb("gnb_hd", [HLOC, 128], F32)
    ln0f = sb("ln0f", [128, 16 * SUBT], BF16)
    kf = sb("kf", [128, HLOC * SUBT], BF16)
    qdec = sb("qdec", [128, HLOC * SUBT], BF16)
    vtm = sb("vtm", [128, 8 * 1024], BF16)
    xbuf = [sb("xbuf%d" % i, [128, D], F32) for i in range(2)]
    ln0tm = [sb("ln0tm%d" % i, [128, D], BF16) for i in range(2)]
    wkqb = [sb("wkqb%d" % i, [128, 16 * 128], BF16) for i in range(2)]
    wvb = [sb("wvb%d" % i, [128, 16 * 256], BF16) for i in range(2)]
    scbf = [sb("scbf%d" % i, [128, 128], BF16) for i in range(4)]
    ktd = [sb("ktd%d" % i, [128, 128], BF16) for i in range(4)]
    retsc = [sb("retsc%d" % i, [128, 512], F32) for i in range(2)]
    sqsc = sb("sqsc", [128, 512], F32)
    tmp4 = sb("tmp4", [128, 8], F32)
    accS = sb("accS", [128, HLOC], F32)
    accQ = sb("accQ", [128, HLOC], F32)
    Sst = sb("Sst", [128, HLOC * 128], F32)
    Sbf = sb("Sbf", [128, HLOC * 128], BF16)
    qtmp = sb("qtmp", [128, 512], F32)
    wred = sb("wred", [128, 2], F32)
    wacc = sb("wacc", [128, 2], F32)
    wscol = sb("wscol", [2, 2], F32)
    wst2 = sb("wst2", [1, 4], F32)
    gsum = sb("gsum", [HLOC, 2], F32)
    gmrs = sb("gmrs", [HLOC, 4], F32)
    a_hd = sb("a_hd", [HLOC, 128], F32)
    b_hd = sb("b_hd", [HLOC, 128], F32)

    tg = P.dma("gp", "d_c", lambda e: e.dma_start(out=gam_s[:, :], in_=gam_h))
    tp = P.comp("gp", lambda e: e.memset(pw[:, 0:1], 1.0))
    tp = P.comp("ve", lambda e: e.tensor_scalar(pw[:, 1:2], pw[:, 0:1], gam_s[:, :], None, AL.mult), (tg, tp))
    n = 1
    while n < 128:
        def step(e, n=n):
            return e.tensor_scalar(pw[:, n + 1:2 * n + 1], pw[:, 1:n + 1], pw[:, n:n + 1], None, AL.mult)
        tp = P.comp("ve", step, (tp,))
        n *= 2
    tip = P.comp("ve", lambda e: e.reciprocal(ipw[:, 0:129], pw[:, 0:129]), (tp,))
    trp = P.comp("ve", lambda e: e.tensor_scalar(rpw[:, 0:129], ipw[:, 0:129], pw[:, 128:129], None, AL.mult), (tip,))
    tt1 = P.dma("gp", "d_c", lambda e: e.dma_start(out=scr_tab[0:HLOC, 0:129], in_=pw[:, 0:129]), (tp,))
    tt2 = P.dma("gp", "d_c", lambda e: e.dma_start(out=scr_tab[8:8 + HLOC, 0:129], in_=ipw[:, 0:129]), (tip,))
    tt3 = P.dma("gp", "d_c", lambda e: e.dma_start(out=scr_tab[16:16 + HLOC, 0:129], in_=rpw[:, 0:129]), (trp,))
    tt4 = P.dma("gp", "d_c", lambda e: e.dma_start(
        out=scr_gc[0:1, 0:HLOC].rearrange("o p -> p o"), in_=pw[:, 128:129]), (tp,))

    ttri = P.comp("gp", lambda e: e.affine_select(
        tri[:, :], ones_f[:, :], [[1, 128]], AL.is_ge, 0.0,
        base=0, channel_multiplier=-1), (tones,))
    tinv = []
    for h in range(HLOC):
        def ld(e, h=h):
            return e.dma_start(out=decS[:, h:h + 1],
                               in_=scr_tab[16 + h, 1:129].unsqueeze(0).rearrange("o p -> p o"))
        tinv.append(P.dma("sy", "d_c", ld, (tt3,)))
    tmi = []
    for h in range(HLOC):
        def ld(e, h=h):
            return e.dma_start(out=minv[:, h:h + 1],
                               in_=scr_tab[8 + h, 1:129].unsqueeze(0).rearrange("o p -> p o"))
        tmi.append(P.dma("sy", "d_c", ld, (tt2,)))
    tdq = []
    for h in range(HLOC):
        def ld(e, h=h):
            return e.dma_start(out=dqt[:, h * 128:(h + 1) * 128],
                               in_=scr_tab[h, 1:129].unsqueeze(0).to_broadcast((128, 128)))
        tdq.append(P.dma("sy", "d_c", ld, (tt1,)))
    tgc = P.dma("sy", "d_c", lambda e: e.dma_start(
        out=gc_cols[:, :], in_=scr_gc[0:1, 0:HLOC].to_broadcast((128, HLOC))), (tt4,))

    tb = []
    for h in range(HLOC):
        def lq(e, h=h):
            return e.dma_start(out=bqc[:, h:h + 1],
                               in_=bq_h[0, h * 128:(h + 1) * 128].unsqueeze(0).rearrange("o p -> p o"))

        def lk(e, h=h):
            return e.dma_start(out=bkc[:, h:h + 1],
                               in_=bk_h[0, h * 128:(h + 1) * 128].unsqueeze(0).rearrange("o p -> p o"))
        tb.append(P.dma("sy", "d_c", lq))
        tb.append(P.dma("sy", "d_c", lk))
    tbv = P.dma("sy", "d_c", lambda e: e.dma_start(out=bv_rep[:, :], in_=bv_h.to_broadcast((128, 1024))))
    tgw = P.dma("sy", "d_c", lambda e: e.dma_start(
        out=gnw_hd[:, :], in_=gnw_h.rearrange("o (h d) -> (o h) d", h=HLOC)))
    tgb = P.dma("sy", "d_c", lambda e: e.dma_start(
        out=gnb_hd[:, :], in_=gnb_h.rearrange("o (h d) -> (o h) d", h=HLOC)))

    tSz = P.comp("gp", lambda e: e.memset(Sst[:, :], 0.0))
    tSb0 = P.comp("gp", lambda e: e.memset(Sbf[:, :], 0.0), (tSz,))
    tacc = P.comp("gp", lambda e: e.memset(accS[:, :], 0.0))
    tacc2 = P.comp("gp", lambda e: e.memset(accQ[:, :], 0.0))

    twz = P.comp("gp", lambda e: e.memset(wacc[:, :], 0.0))
    stats_state = {"red": twz, "i": 0}
    stats_blocks = [("gv", i) for i in range(HID2 // 128)] + \
                   [("out", i) for i in range(2 * (D // 128))]

    def emit_stats_some(k):
        for _ in range(k):
            i = stats_state["i"]
            if i >= len(stats_blocks):
                return
            stats_state["i"] = i + 1
            kind, j = stats_blocks[i]
            b = i % 2
            if kind == "gv":
                tl = P.dma("sc", "d_st%d" % b, lambda e, j=j, b=b: e.dma_start(
                    out=wsb[b][:, 0:D], in_=wgv_hf[j * 128:(j + 1) * 128, :]),
                    deps=bd(("wsb", b)))
                col = 0
                wdt = D
            else:
                ri, cj = j // 2, j % 2
                tl = P.dma("sc", "d_st%d" % b, lambda e, ri=ri, cj=cj, b=b: e.dma_start(
                    out=wsb[b][:, 0:2048], in_=wout_hf[ri * 128:(ri + 1) * 128, cj * 2048:(cj + 1) * 2048]),
                    deps=bd(("wsb", b)))
                col = 1
                wdt = 2048
            tr = P.comp("ve", lambda e, b=b, col=col, wdt=wdt: e.tensor_reduce(
                wred[:, col:col + 1], wsb[b][:, 0:wdt].rearrange("p (o f) -> p o f", o=1),
                AX.X, AL.add, apply_absolute_value=True), (tl, stats_state["red"]))
            setb(("wsb", b), tr)
            stats_state["red"] = P.comp("ve", lambda e, col=col: e.tensor_tensor(
                wacc[:, col:col + 1], wacc[:, col:col + 1], wred[:, col:col + 1], AL.add), (tr, twz))

    def finish_stats():
        emit_stats_some(len(stats_blocks))
        twt = P.comp("te", lambda e: e.transpose(ps[2][0:2, 0:128], wacc[:, :], idf[:, :]),
                     (stats_state["red"], tidf) + pd(2))
        twe = P.comp("ve", lambda e: e.tensor_reduce(
            wscol[0:2, 0:1], ps[2][0:2, 0:128].rearrange("p (o f) -> p o f", o=1), AX.X, AL.add), (twt,))
        setp(2, twe)
        tst = branch_dma("d_sh",
                         lambda e: e.dma_start(out=sh_stat_e.rearrange("o p -> p o"), in_=wscol[0:2, 0:1]),
                         lambda e: e.dma_start(out=sh_stat_o.rearrange("o p -> p o"), in_=wscol[0:2, 0:1]),
                         deps=(twe,))
        return set_flag(0, deps=(tst,))

    def emit_ln0(st):
        evacs = []
        for tt in range(8):
            b = tt % 2
            row = st * SUBT + tt * 128
            tl = P.dma("sy", "d_x%d" % b, lambda e, row=row, b=b: e.dma_start(
                out=xbuf[b][:, :], in_=x_full[row:row + 128, :]), deps=bd(("x", b)))
            trs, tsq = emit_ln_stats(xbuf[b][:, :], D, (tl,))
            tap = P.comp("ve", lambda e, b=b: e.tensor_scalar(
                ln0tm[b][:, :], xbuf[b][:, :], col_e[:, :], col_b[:, :],
                AL.subtract, AL.mult), (trs,) + bd(("ln0tm", b)))
            cols["g"] = tap
            setb(("x", b), tap, tsq)
            last_tr = None
            for kc in range(16):
                bank = 6 + (kc % 2)
                pb = ps[bank]
                ttr = P.comp("te", lambda e, b=b, kc=kc, pb=pb: e.transpose(
                    pb.bitcast(BF16)[:, 0:128], ln0tm[b][:, kc * 128:(kc + 1) * 128], ident[:, :]),
                    (tap, tid) + pd(bank) + bd("ln0f"))
                tev = P.comp("sc", lambda e, kc=kc, tt=tt, pb=pb: e.copy(
                    ln0f[:, kc * SUBT + tt * 128: kc * SUBT + (tt + 1) * 128],
                    pb.bitcast(BF16)[:, 0:128]), (ttr,))
                setp(bank, tev)
                last_tr = ttr
                evacs.append(tev)
            setb(("ln0tm", b), last_tr)
            if st > 0:
                emit_stats_some(3)
        tsp = P.dma("gp", "d_spw", lambda e, st=st: e.dma_start(
            out=ln0f_sp.rearrange("(c p) t -> p c t", p=128)[:, :, st * SUBT:(st + 1) * SUBT],
            in_=ln0f.rearrange("p (c t) -> p c t", c=16)), deps=tuple(evacs))
        return evacs, tsp

    def emit_proj(st, ln_evacs, tsp):
        lasts = []
        for wmat, is_q in ((wk_h, False), (wq_h, True)):
            for fb in range(HLOC):
                b = fb % 2
                tl = P.dma("gp", "d_wk%d" % b, lambda e, wmat=wmat, fb=fb, b=b: e.dma_start(
                    out=wkqb[b].rearrange("p (c f) -> p c f", c=16),
                    in_=wmat[:, fb * 128:(fb + 1) * 128].rearrange("(c p) f -> p c f", p=128)),
                    deps=bd(("wkq", b)))
                for tt2 in range(2):
                    pb = ps[tt2]
                    mlast = None
                    for kc in range(16):
                        def mm(e, fb=fb, b=b, kc=kc, tt2=tt2, pb=pb):
                            return e.matmul(pb[:, :],
                                            wkqb[b][:, kc * 128:(kc + 1) * 128],
                                            ln0f[:, kc * SUBT + tt2 * 512: kc * SUBT + (tt2 + 1) * 512],
                                            start=(kc == 0), stop=(kc == 15))
                        d = (tl,) + pd(tt2) + tuple(ln_evacs) if kc == 0 else (tl,)
                        mlast = P.comp("te", mm, d)
                    co = fb * SUBT + tt2 * 512
                    if not is_q:
                        tev = P.comp("sc", lambda e, fb=fb, co=co, pb=pb: e.add(
                            kf[:, co:co + 512], pb[:, :], bkc[:, fb:fb + 1]),
                            (mlast,) + tuple(tb) + bd("kf"))
                    else:
                        te1 = P.comp("sc", lambda e, fb=fb, pb=pb: e.add(
                            qtmp[:, :], pb[:, :], bqc[:, fb:fb + 1]),
                            (mlast,) + tuple(tb) + bd("qtmp"))
                        tev = te1
                        for j4 in range(4):
                            tev = P.comp("ve", lambda e, fb=fb, co=co, j4=j4: e.tensor_tensor(
                                qdec[:, co + j4 * 128: co + (j4 + 1) * 128],
                                qtmp[:, j4 * 128:(j4 + 1) * 128],
                                dqt[:, fb * 128:(fb + 1) * 128], AL.mult),
                                (te1,) + tuple(tdq) + bd("qdec"))
                        setb("qtmp", tev)
                    setp(tt2, tev)
                    lasts.append(tev)
                setb(("wkq", b), mlast)
        for ob in range(4):
            b = ob % 2
            tl = P.dma("gp", "d_wv%d" % b, lambda e, ob=ob, b=b: e.dma_start(
                out=wvb[b].rearrange("p (c f) -> p c f", c=16),
                in_=wv_h[:, ob * 256:(ob + 1) * 256].rearrange("(c p) f -> p c f", p=128)),
                deps=bd(("wv", b)))
            for tt3 in range(8):
                bank = 2 + tt3 % 2
                pb = ps[bank]
                mlast = None
                for kc in range(16):
                    def mm(e, ob=ob, b=b, kc=kc, tt3=tt3, pb=pb):
                        return e.matmul(pb[:, 0:256],
                                        ln0f[:, kc * SUBT + tt3 * 128: kc * SUBT + (tt3 + 1) * 128],
                                        wvb[b][:, kc * 256:(kc + 1) * 256],
                                        start=(kc == 0), stop=(kc == 15))
                    d = (tl,) + pd(bank) + tuple(ln_evacs) if kc == 0 else (tl,)
                    mlast = P.comp("te", mm, d)
                tev = P.comp("ve", lambda e, ob=ob, tt3=tt3, pb=pb: e.tensor_tensor(
                    vtm[:, tt3 * 1024 + ob * 256: tt3 * 1024 + (ob + 1) * 256],
                    pb[:, 0:256], bv_rep[:, ob * 256:(ob + 1) * 256], AL.add),
                    (mlast, tbv) + bd("vtm"))
                setp(bank, tev)
                lasts.append(tev)
            setb(("wv", b), mlast)
        return lasts

    ROT = (4, 5, 0, 1)

    def emit_chunks(st, dep_toks):
        for c in range(8):
            tcast = P.comp("sc", lambda e: e.copy(Sbf[:, :], Sst[:, :]),
                           tuple(dep_toks) + bd("Sbf"))
            ret_last = {}
            for half in range(2):
                tscs = {}
                for h in range(half * 4, half * 4 + 4):
                    bank = ROT[h % 4]
                    pb = ps[bank]
                    kcol = h * SUBT + c * 128
                    tsc = P.comp("te", lambda e, kcol=kcol, pb=pb: e.matmul(
                        pb[:, 0:128], kf[:, kcol:kcol + 128], qdec[:, kcol:kcol + 128],
                        start=True, stop=True), pd(bank) + (tcast,))
                    tscs[h] = tsc
                tmks = {}
                for h in range(half * 4, half * 4 + 4):
                    bank = ROT[h % 4]
                    pb = ps[bank]
                    tmk = P.comp("ve", lambda e, h=h, pb=pb: e.scalar_tensor_tensor(
                        scbf[h % 4][:, :], pb[:, 0:128], minv[:, h:h + 1], tri[:, :],
                        AL.mult, AL.mult),
                        (tscs[h], ttri) + tuple(tmi) + bd(("scbf", h % 4)))
                    setp(bank, tmk)
                    tmks[h] = tmk
                for h in range(half * 4, half * 4 + 4):
                    rb = ps[2 + h // 4]
                    rcol = (h % 4) * 128
                    kcol = h * SUBT + c * 128
                    vcol = c * 1024 + h * 128
                    d1 = (tmks[h],) + (pd(2 + h // 4) if h % 4 == 0 else ())
                    tr1 = P.comp("te", lambda e, vcol=vcol, rb=rb, rcol=rcol, h=h: e.matmul(
                        rb[:, rcol:rcol + 128], scbf[h % 4][:, :], vtm[:, vcol:vcol + 128],
                        start=True, stop=False), d1)
                    tr2 = P.comp("te", lambda e, kcol=kcol, rb=rb, rcol=rcol, h=h: e.matmul(
                        rb[:, rcol:rcol + 128], qdec[:, kcol:kcol + 128], Sbf[:, h * 128:(h + 1) * 128],
                        start=False, stop=True), (tr1,))
                    setb(("scbf", h % 4), tr1)
                    ret_last[h // 4] = tr2
            setb("Sbf", ret_last[0], ret_last[1])
            sp_toks = []
            for bnk in range(2):
                rb = ps[2 + bnk]
                rsc = retsc[bnk]
                tev = P.comp("ve", lambda e, rb=rb, rsc=rsc: e.tensor_copy(rsc[:, :], rb[:, :]),
                             (ret_last[bnk],) + bd(("retsc", bnk)))
                setp(2 + bnk, tev)
                tS1 = P.comp("ve", lambda e, rsc=rsc: e.tensor_reduce(
                    tmp4[:, 0:4], rsc[:, :].rearrange("p (h f) -> p h f", h=4), AX.X, AL.add), (tev,))
                tS2 = P.comp("ve", lambda e, bnk=bnk: e.tensor_tensor(
                    accS[:, bnk * 4:bnk * 4 + 4], accS[:, bnk * 4:bnk * 4 + 4], tmp4[:, 0:4], AL.add),
                    (tS1, tacc))
                tq1 = P.comp("sc", lambda e, rsc=rsc: e.square(sqsc[:, :], rsc[:, :]),
                             (tev, tS1) + bd("sqsc"))
                tq2 = P.comp("ve", lambda e: e.tensor_reduce(
                    tmp4[:, 4:8], sqsc[:, :].rearrange("p (h f) -> p h f", h=4), AX.X, AL.add),
                    (tq1, tS2))
                setb("sqsc", tq2)
                tq3 = P.comp("ve", lambda e, bnk=bnk: e.tensor_tensor(
                    accQ[:, bnk * 4:bnk * 4 + 4], accQ[:, bnk * 4:bnk * 4 + 4], tmp4[:, 4:8], AL.add),
                    (tq2, tacc2))
                tsp = P.dma("gp", "d_spw", lambda e, st=st, c=c, bnk=bnk, rsc=rsc: e.dma_start(
                    out=ret_sp[st * SUBT + c * 128: st * SUBT + (c + 1) * 128, bnk * 512:(bnk + 1) * 512],
                    in_=rsc[:, :]), deps=(tq1,))
                setb(("retsc", bnk), tsp, tq2)
                sp_toks += [tq3, tsp]
            ztoks = []
            for half in range(2):
                tkds = {}
                for h in range(half * 4, half * 4 + 4):
                    bank = ROT[h % 4]
                    pb = ps[bank]
                    kcol = h * SUBT + c * 128
                    ttr = P.comp("te", lambda e, kcol=kcol, pb=pb: e.transpose(
                        pb.bitcast(BF16)[:, 0:128], kf[:, kcol:kcol + 128], ident[:, :]),
                        pd(bank) + (tid,))
                    tkd = P.comp("ve", lambda e, h=h, pb=pb: e.tensor_scalar(
                        ktd[h % 4][:, :], pb.bitcast(BF16)[:, 0:128], decS[:, h:h + 1], None, AL.mult),
                        (ttr,) + tuple(tinv) + bd(("ktd", h % 4)))
                    setp(bank, tkd)
                    tkds[h] = tkd
                for h in range(half * 4, half * 4 + 4):
                    zb = ps[6 + h // 4]
                    zcol = (h % 4) * 128
                    vcol = c * 1024 + h * 128
                    d1 = (tkds[h],) + (pd(6 + h // 4) if h % 4 == 0 else ())
                    tz = P.comp("te", lambda e, h=h, vcol=vcol, zb=zb, zcol=zcol: e.matmul(
                        zb[:, zcol:zcol + 128], ktd[h % 4][:, :], vtm[:, vcol:vcol + 128],
                        start=True, stop=True), d1)
                    setb(("ktd", h % 4), tz)
                    ztoks.append(tz)
            tS = tcast
            for h in range(HLOC):
                tS = P.comp("gp", lambda e, h=h: e.tensor_scalar(
                    Sst[:, h * 128:(h + 1) * 128], Sst[:, h * 128:(h + 1) * 128],
                    gc_cols[:, h:h + 1], None, AL.mult), (tS, tgc))
            tS = P.comp("ve", lambda e: e.tensor_tensor(
                Sst[:, 0:512], Sst[:, 0:512], ps[6][:, :], AL.add), (tS,) + tuple(ztoks))
            tS2b = P.comp("ve", lambda e: e.tensor_tensor(
                Sst[:, 512:1024], Sst[:, 512:1024], ps[7][:, :], AL.add), (tS,))
            setp(6, tS)
            setp(7, tS2b)
            dep_toks = [tS2b] + sp_toks
            if st > 0:
                emit_stats_some(2)
        return dep_toks

    tf1 = None
    adeps = []
    for st in range(NSUB):
        evs, tsps = emit_ln0(st)
        pls = emit_proj(st, evs, tsps)
        setb("ln0f", pls[-1], tsps)
        if st == 0:
            adeps = emit_chunks(0, pls[-3:] + [tSz, tSb0])
        else:
            adeps = emit_chunks(st, pls[-3:] + adeps)
        setb("kf", adeps[0])
        setb("qdec", adeps[0])
        setb("vtm", adeps[0])
        if st == 2:
            tf1 = finish_stats()

    tp1 = poll(0, deps=(tf1,))
    tr1 = P.dma("sy", "d_c", lambda e: e.dma_start(out=wst2[0:1, 0:2], in_=sh_stat_e), (tp1,))
    tr2 = P.dma("sy", "d_c", lambda e: e.dma_start(out=wst2[0:1, 2:4], in_=sh_stat_o), (tp1,))
    tws = P.comp("ve", lambda e: e.tensor_tensor(wst2[0:1, 0:2], wst2[0:1, 0:2], wst2[0:1, 2:4], AL.add),
                 (tr1, tr2))
    tws = P.comp("ve", lambda e: e.tensor_scalar(
        wst2[0:1, 0:1], wst2[0:1, 0:1], 1.0 / (16384.0 * 2048.0), EPS, AL.mult, AL.max), (tws,))
    tws = P.comp("ve", lambda e: e.tensor_scalar(
        wst2[0:1, 1:2], wst2[0:1, 1:2], 1.0 / (2048.0 * 8192.0), EPS, AL.mult, AL.max), (tws,))
    tws2 = P.comp("ve", lambda e: e.reciprocal(wst2[0:1, 2:4], wst2[0:1, 0:2]), (tws,))
    twm1 = P.dma("gp", "d_c", lambda e: e.dma_start(out=scr_wm[0:1, 0:2], in_=wst2[0:1, 0:2]), (tws,))
    twm2 = P.dma("gp", "d_c", lambda e: e.dma_start(out=scr_wm[0:1, 2:4], in_=wst2[0:1, 2:4]), (tws2,))
    twmc = P.dma("sy", "d_c", lambda e: e.dma_start(
        out=wmcols[:, :], in_=scr_wm[0:1, 0:4].to_broadcast((128, 4))), (twm1, twm2))

    tt1g = P.comp("te", lambda e: e.transpose(ps[0][0:HLOC, 0:128], accS[:, :], idf[:, :]),
                  tuple(adeps) + (tidf,) + pd(0))
    tg1 = P.comp("ve", lambda e: e.tensor_reduce(
        gsum[:, 0:1], ps[0][0:HLOC, 0:128].rearrange("p (o f) -> p o f", o=1), AX.X, AL.add), (tt1g,))
    setp(0, tg1)
    tt2g = P.comp("te", lambda e: e.transpose(ps[1][0:HLOC, 0:128], accQ[:, :], idf[:, :]),
                  (tg1,) + tuple(adeps) + pd(1))
    tg2 = P.comp("ve", lambda e: e.tensor_reduce(
        gsum[:, 1:2], ps[1][0:HLOC, 0:128].rearrange("p (o f) -> p o f", o=1), AX.X, AL.add), (tt2g,))
    setp(1, tg2)
    NTOT = float(S * 128)
    tg3 = P.comp("ve", lambda e: e.tensor_scalar(gmrs[:, 0:2], gsum[:, 0:2], 1.0 / NTOT, None, AL.mult),
                 (tg2,))
    tg4 = P.comp("ve", lambda e: e.tensor_tensor(gmrs[:, 2:3], gmrs[:, 0:1], gmrs[:, 0:1], AL.mult), (tg3,))
    tg5 = P.comp("ve", lambda e: e.tensor_tensor(gmrs[:, 2:3], gmrs[:, 1:2], gmrs[:, 2:3], AL.subtract), (tg4,))
    tg6 = P.comp("ve", lambda e: e.tensor_scalar(gmrs[:, 2:3], gmrs[:, 2:3], EPS, None, AL.add), (tg5,))
    tg7 = emit_rsqrt(gmrs[:, 3:4], gmrs[:, 2:3], gsum[:, 0:1], (tg6,))
    tg8 = P.comp("ve", lambda e: e.tensor_scalar(a_hd[:, :], gnw_hd[:, :], gmrs[:, 3:4], None, AL.mult),
                 (tg7, tgw))
    tg9 = P.comp("ve", lambda e: e.tensor_scalar(b_hd[:, :], a_hd[:, :], gmrs[:, 0:1], None, AL.mult), (tg8,))
    tg9 = P.comp("ve", lambda e: e.tensor_tensor(b_hd[:, :], gnb_hd[:, :], b_hd[:, :], AL.subtract), (tg9, tgb))
    tga = P.dma("gp", "d_c", lambda e: e.dma_start(
        out=scr_gn[0:1, :].rearrange("o (h d) -> (o h) d", h=HLOC), in_=a_hd[:, :]), (tg8,))
    tgb2 = P.dma("gp", "d_c", lambda e: e.dma_start(
        out=scr_gn[1:2, :].rearrange("o (h d) -> (o h) d", h=HLOC), in_=b_hd[:, :]), (tg9,))

    if DBG:
        P.dma("sy", "d_tap", lambda e: e.dma_start(out=taps["tap_ret"], in_=ret_sp),
              deps=tuple(adeps) + (P.tok("d_spw"),))
        P.dma("sy", "d_tap", lambda e: e.dma_start(out=taps["tap_ln0f"], in_=ln0f_sp),
              deps=(P.tok("d_spw"),))
        P.dma("sy", "d_tap", lambda e: e.dma_start(out=taps["tap_gn"], in_=scr_gn), deps=(tga, tgb2))

    aphase_end = [P.tok("e_ve"), P.tok("e_te"), P.tok("e_sc"), P.tok("e_gp"),
                  P.tok("d_spw"), tga, tgb2]

    def make_quant_emitters(blocks, src, stage, sc_idx, store, out_toks):
        ems = []
        for bi, (r0, c0, ncols) in enumerate(blocks):
            b = bi % 2

            def em(r0=r0, c0=c0, ncols=ncols, b=b, src=src, stage=stage, sc_idx=sc_idx,
                   store=store, out_toks=out_toks):
                tl = P.dma("sc", "d_q%d" % b, lambda e, r0=r0, c0=c0, ncols=ncols, b=b, src=src: e.dma_start(
                    out=wsb[b][:, 0:ncols], in_=src[r0:r0 + 128, c0:c0 + ncols]),
                    deps=bd(("wsb", b)))
                t1 = P.comp("gp", lambda e, b=b, ncols=ncols, sc_idx=sc_idx: e.tensor_scalar(
                    wsb[b][:, 0:ncols], wsb[b][:, 0:ncols], wmcols[:, 2 + sc_idx:3 + sc_idx], MAGIC,
                    AL.mult, AL.add), (tl, twmc))
                t2 = P.comp("ve", lambda e, b=b, ncols=ncols: e.tensor_scalar(
                    wqb_[:, 0:ncols], wsb[b][:, 0:ncols], MAGIC, 1.0, AL.subtract, AL.min), (t1,))
                setb(("wsb", b), t2)
                t3 = P.comp("ve", lambda e, b=b, ncols=ncols: e.tensor_scalar(
                    wqo[b][:, 0:ncols], wqb_[:, 0:ncols], -1.0, None, AL.max),
                    (t2,) + bd(("wqo", b)))
                if store is None:
                    ts = P.dma("gp", "d_q2%d" % b, lambda e, r0=r0, c0=c0, ncols=ncols, b=b, stage=stage: e.dma_start(
                        out=stage[r0:r0 + 128, c0:c0 + ncols], in_=wqo[b][:, 0:ncols]), deps=(t3,))
                else:
                    ts = store(r0, c0, ncols, b, t3)
                setb(("wqo", b), ts)
                out_toks.append(ts)
            ems.append(em)
        return ems

    def gv_store(r0, c0, ncols, b, t3):
        return branch_dma_gp(
            "d_q2%d" % b,
            lambda e, r0=r0, ncols=ncols, b=b: e.dma_start(
                out=sh_wgv[r0:r0 + 128, 0:ncols], in_=wqo[b][:, 0:ncols]),
            lambda e, r0=r0, ncols=ncols, b=b: e.dma_start(
                out=sh_wgv[HID2 + r0:HID2 + r0 + 128, 0:ncols], in_=wqo[b][:, 0:ncols]),
            deps=(t3,))

    tq_gv = []
    tq_out = []
    p1_emitters = (
        make_quant_emitters([(i * 128, 0, D) for i in range(HID2 // 128)], wgv_hf, None, 0,
                            gv_store, tq_gv)
        + make_quant_emitters([(ri * 128, cj * 2048, 2048) for ri in range(D // 128) for cj in range(2)],
                              wout_hf, wstage_out, 1, None, tq_out))

    close_scope(markA)

    markF = len(cms)
    a_rep = sb("a_rep", [128, 1024], F32)
    b_rep = sb("b_rep", [128, 1024], F32)
    bg_rep2 = sb("bg_rep2", [128, 1024], F32)
    wgb = sb("wgb", [128, 16 * 1024], BF16)
    wob = sb("wob", [128, 8 * 2048], BF16)
    lnfb = [sb("lnfb%d" % i, [128, 16 * 256], BF16) for i in range(2)]
    retb = [sb("retb%d" % i, [128, 1024], F32) for i in range(2)]
    rof = sb("rof", [128, 1024], F32)
    ro_bf = [sb("ro_bf%d" % i, [128, 1024], BF16) for i in range(2)]
    gtmp = sb("gtmp", [128, 512], F32)
    g_bf = sb("g_bf", [128, 1024], BF16)
    gro = [sb("gro%d" % i, [128, 1024], BF16) for i in range(2)]
    grof = [sb("grof%d" % i, [128, 8 * 128], BF16) for i in range(2)]
    ysc = [sb("ysc%d" % i, [128, D], F32) for i in range(2)]

    tar = P.dma("sy", "d_c", lambda e: e.dma_start(
        out=a_rep[:, :], in_=scr_gn[0:1, :].to_broadcast((128, 1024))), (tga, tgb2))
    tbr = P.dma("sy", "d_c", lambda e: e.dma_start(
        out=b_rep[:, :], in_=scr_gn[1:2, :].to_broadcast((128, 1024))), (tga, tgb2))
    tbg2 = P.dma("sy", "d_c", lambda e: e.dma_start(
        out=bg_rep2[:, :], in_=bg_h.to_broadcast((128, 1024))))
    twg = P.dma("gp", "d_wg", lambda e: e.dma_start(
        out=wgb.rearrange("p (c f) -> p c f", c=16),
        in_=wg_h.rearrange("(c p) f -> p c f", p=128)), tuple(aphase_end))
    two = P.dma("gp", "d_wg", lambda e: e.dma_start(
        out=wob.rearrange("p (c f) -> p c f", c=8),
        in_=wo_h.rearrange("(c p) f -> p c f", p=128)), tuple(aphase_end))

    ypart_toks = []
    tlf = None
    for f in range(32):
        b = f % 2
        trr = P.dma("sy", "d_rt%d" % b, lambda e, f=f, b=b: e.dma_start(
            out=retb[b][:, :], in_=ret_sp[f * 128:(f + 1) * 128, :]),
            deps=bd(("retb", b)) + tuple(aphase_end))
        if f % 2 == 0:
            lslot = (f // 2) % 2
            tlf = P.dma("sy", "d_lf%d" % lslot, lambda e, f=f, lslot=lslot: e.dma_start(
                out=lnfb[lslot].rearrange("p (c t) -> p c t", c=16),
                in_=ln0f_sp.rearrange("(c p) t -> p c t", p=128)[:, :, f * 128:(f + 2) * 128]),
                deps=bd(("lnfb", lslot)) + tuple(aphase_end))
        lb = lnfb[(f // 2) % 2]
        lofs = (f % 2) * 128
        t1 = P.comp("ve", lambda e, b=b: e.tensor_tensor(rof[:, :], retb[b][:, :], a_rep[:, :], AL.mult),
                    (trr, tar) + bd("rof"))
        t2 = P.comp("ve", lambda e, b=b: e.tensor_tensor(ro_bf[b][:, :], rof[:, :], b_rep[:, :], AL.add),
                    (t1, tbr) + bd(("ro_bf", b)))
        setb("rof", t2)
        setb(("retb", b), t1)
        gts = []
        for ob in range(2):
            pb = ps[ob]
            mlast = None
            for kc in range(16):
                def mm(e, kc=kc, ob=ob, pb=pb, lb=lb, lofs=lofs):
                    return e.matmul(pb[:, :], lb[:, kc * 256 + lofs: kc * 256 + lofs + 128],
                                    wgb[:, kc * 1024 + ob * 512: kc * 1024 + (ob + 1) * 512],
                                    start=(kc == 0), stop=(kc == 15))
                d = (tlf, twg) + pd(ob) if kc == 0 else ()
                mlast = P.comp("te", mm, d)
            t3 = P.comp("ve", lambda e, ob=ob, pb=pb: e.tensor_tensor(
                gtmp[:, :], pb[:, :], bg_rep2[:, ob * 512:(ob + 1) * 512], AL.add),
                (mlast, tbg2) + bd("gtmp"))
            t4 = P.comp("sc", lambda e, ob=ob: e.activation(
                g_bf[:, ob * 512:(ob + 1) * 512], gtmp[:, :], AF.Silu), (t3,) + bd("g_bf"))
            setb("gtmp", t4)
            setp(ob, t4)
            gts.append(t4)
        t5 = P.comp("ve", lambda e, b=b: e.tensor_tensor(gro[b][:, :], g_bf[:, :], ro_bf[b][:, :], AL.mult),
                    (t2,) + tuple(gts) + bd(("gro", b)))
        setb("g_bf", t5)
        setb(("ro_bf", b), t5)
        gtr = []
        for kc in range(8):
            bank = 2 + kc % 2
            pb = ps[bank]
            ttr = P.comp("te", lambda e, kc=kc, pb=pb, b=b: e.transpose(
                pb.bitcast(BF16)[:, 0:128], gro[b][:, kc * 128:(kc + 1) * 128], ident[:, :]),
                (t5,) + pd(bank))
            tev = P.comp("sc", lambda e, kc=kc, b=b, pb=pb: e.copy(
                grof[b][:, kc * 128:(kc + 1) * 128], pb.bitcast(BF16)[:, 0:128]),
                (ttr,) + bd(("grof", b)))
            setp(bank, tev)
            gtr.append(tev)
        setb(("gro", b), gtr[-1])
        yts = []
        mlast = None
        for obq in range(4):
            bank = 4 + obq
            pb = ps[bank]
            for kc in range(8):
                def mm(e, kc=kc, obq=obq, b=b, pb=pb):
                    return e.matmul(pb[:, :], grof[b][:, kc * 128:(kc + 1) * 128],
                                    wob[:, kc * 2048 + obq * 512: kc * 2048 + (obq + 1) * 512],
                                    start=(kc == 0), stop=(kc == 7))
                d = tuple(gtr) + (two,) + pd(bank) if kc == 0 else ()
                mlast = P.comp("te", mm, d)
            tev = P.comp("ve", lambda e, obq=obq, b=b, pb=pb: e.tensor_copy(
                ysc[b][:, obq * 512:(obq + 1) * 512], pb[:, :]), (mlast,) + bd(("ysc", b)))
            setp(bank, tev)
            yts.append(tev)
        setb(("grof", b), mlast)
        tyw = branch_dma("d_yw",
                         lambda e, f=f, b=b: e.dma_start(out=sh_ye[f * 128:(f + 1) * 128, :], in_=ysc[b][:, :]),
                         lambda e, f=f, b=b: e.dma_start(out=sh_yo[f * 128:(f + 1) * 128, :], in_=ysc[b][:, :]),
                         deps=tuple(yts))
        setb(("ysc", b), tyw)
        setb(("lnfb", (f // 2) % 2), mlast)
        ypart_toks.append(tyw)
        for _ in range(3):
            if p1_emitters:
                p1_emitters.pop(0)()
        if f == 21:
            tf2 = set_flag(2, deps=tuple(tq_gv))

    while p1_emitters:
        p1_emitters.pop(0)()

    tfy = set_flag(4, deps=tuple(ypart_toks))
    tpy = poll(4, deps=(tfy,))

    p1o_toks = []
    for cb in range(32):
        tt = P.dma("sc", "d_tr0", lambda e, cb=cb: e.dma_start_transpose(
            trb[:, 0:D], wstage_out[:, cb * 128:(cb + 1) * 128]),
            deps=tuple(tq_out) + bd("trb"))
        tws_ = []
        for ob in range(16):
            tw = branch_dma("d_sh",
                            lambda e, cb=cb, ob=ob: e.dma_start(
                                out=sh_woutT[ob * 128:(ob + 1) * 128, cb * 128:(cb + 1) * 128],
                                in_=trb[:, ob * 128:(ob + 1) * 128]),
                            lambda e, cb=cb, ob=ob: e.dma_start(
                                out=sh_woutT[ob * 128:(ob + 1) * 128, 4096 + cb * 128:4096 + (cb + 1) * 128],
                                in_=trb[:, ob * 128:(ob + 1) * 128]),
                            deps=(tt,))
            tws_.append(tw)
        setb("trb", *tws_)
        p1o_toks += tws_
    tf6 = set_flag(6, deps=tuple(p1o_toks))
    if DBG:
        P.dma("sy", "d_tap", lambda e: e.dma_start(out=taps["tap_y"], in_=sh_ye), deps=(tpy,))

    fphase_end = [P.tok("e_ve"), P.tok("e_te"), P.tok("e_sc"), tpy]
    close_scope(markF)

    markG = len(cms)
    rqf = sb("rqf", [128, 16 * T], BF16)
    markG2 = len(cms)
    bo_rep = sb("bo_rep", [128, D], F32)
    wn_rep = sb("wn_rep", [128, D], F32)
    yab = [sb("yab%d" % i, [128, D], F32) for i in range(2)]
    xhb = [sb("xhb%d" % i, [128, D], F32) for i in range(2)]
    x1b = [sb("x1b%d" % i, [128, D], F32) for i in range(2)]
    h2b = sb("h2b", [128, D], F32)
    q1b = [sb("q1b%d" % i, [128, D], BF16) for i in range(2)]

    tbo = P.dma("sy", "d_c", lambda e: e.dma_start(
        out=bo_rep[:, :], in_=bo_f.to_broadcast((128, D))), tuple(fphase_end))
    twn = P.dma("sy", "d_c", lambda e: e.dma_start(
        out=wn_rep[:, :], in_=wnorm_f.to_broadcast((128, D))), tuple(fphase_end))

    gi_toks = []
    for f in range(16):
        b = f % 2
        tya = branch_dma_gp("d_yg%d" % b,
                            lambda e, f=f, b=b: e.dma_start(
                                out=yab[b][:, :], in_=sh_ye[f * 128:(f + 1) * 128, :]),
                            lambda e, f=f, b=b: e.dma_start(
                                out=yab[b][:, :], in_=sh_ye[T + f * 128:T + (f + 1) * 128, :]),
                            deps=(tpy,) + bd(("yab", b)))
        tyb = branch_dma_gp("d_yg%d" % b,
                            lambda e, f=f, b=b: e.dma_start(
                                out=yab[b][:, :], in_=sh_yo[f * 128:(f + 1) * 128, :], accum_op=AL.add),
                            lambda e, f=f, b=b: e.dma_start(
                                out=yab[b][:, :], in_=sh_yo[T + f * 128:T + (f + 1) * 128, :], accum_op=AL.add),
                            deps=(tya,))
        txh = P.dma("sy", "d_xh%d" % b, lambda e, f=f, b=b: e.dma_start(
            out=xhb[b][:, :], in_=x_half[f * 128:(f + 1) * 128, :]),
            deps=bd(("xhb", b)) + tuple(fphase_end))
        t2 = P.comp("ve", lambda e, b=b: e.tensor_tensor(yab[b][:, :], yab[b][:, :], bo_rep[:, :], AL.add),
                    (tyb, tbo))
        trs, _ = emit_ln_stats(yab[b][:, :], D, (t2,))
        t6 = P.comp("ve", lambda e, b=b: e.tensor_scalar(
            yab[b][:, :], yab[b][:, :], col_e[:, :], col_b[:, :], AL.subtract, AL.mult), (trs,))
        t7 = P.comp("ve", lambda e, b=b: e.tensor_tensor(x1b[b][:, :], xhb[b][:, :], yab[b][:, :], AL.add),
                    (t6, txh) + bd(("x1b", b)))
        setb(("yab", b), t7)
        setb(("xhb", b), t7)
        tx1 = P.dma("gp", "d_x1w", lambda e, f=f, b=b: e.dma_start(
            out=x1_sp[f * 128:(f + 1) * 128, :], in_=x1b[b][:, :]), deps=(t7,))
        trs2, _ = emit_ln_stats(x1b[b][:, :], D, (t7,))
        t9 = P.comp("ve", lambda e, f=f: e.tensor_copy(m2c[:, f:f + 1], col_e[:, :]), (trs2,))
        t12 = P.comp("ve", lambda e, f=f: e.tensor_copy(rs2c[:, f:f + 1], col_b[:, :]), (trs2,))
        t13 = P.comp("ve", lambda e: e.reciprocal(col_c[:, :], col_d[:, :]), (trs2,))
        t14 = P.comp("ve", lambda e: e.tensor_scalar(col_d[:, :], col_d[:, :], -EPS, None, AL.add), (t13,))
        t14 = P.comp("ve", lambda e: e.tensor_tensor(col_c[:, :], col_d[:, :], col_c[:, :], AL.mult), (t14,))
        t15 = P.comp("ve", lambda e: e.tensor_scalar(col_c[:, :], col_c[:, :], EPS, None, AL.add), (t14,))
        t16 = emit_rsqrt(col_f, col_c, col_a, (t15,))
        t17 = P.comp("ve", lambda e, f=f: e.tensor_tensor(srsc[:, f:f + 1], rs2c[:, f:f + 1], col_f[:, :],
                                                          AL.mult), (t16, t12))
        t18 = P.comp("ve", lambda e, b=b, f=f: e.tensor_scalar(
            h2b[:, :], x1b[b][:, :], m2c[:, f:f + 1], srsc[:, f:f + 1], AL.subtract, AL.mult),
            (t17, t9) + bd("h2b"))
        setb(("x1b", b), t18, tx1)
        t19 = P.comp("ve", lambda e: e.tensor_tensor(h2b[:, :], h2b[:, :], wn_rep[:, :], AL.mult), (t18, twn))
        t20 = P.comp("ve", lambda e: e.tensor_reduce(
            col_a[:, :], h2b[:, :].rearrange("p (o f) -> p o f", o=1), AX.X, AL.max,
            apply_absolute_value=True), (t19,))
        t21 = P.comp("ve", lambda e: e.tensor_scalar(col_a[:, :], col_a[:, :], EPS, None, AL.max), (t20,))
        t22 = P.comp("ve", lambda e: e.reciprocal(col_b[:, :], col_a[:, :]), (t21,))
        t23 = P.comp("ve", lambda e: e.tensor_scalar(col_b[:, :], col_b[:, :], QB, None, AL.mult), (t22,))
        t24 = P.comp("ve", lambda e: e.tensor_scalar(h2b[:, :], h2b[:, :], col_b[:, :], MAGIC,
                                                     AL.mult, AL.add), (t23,))
        t25 = P.comp("ve", lambda e, b=b: e.tensor_scalar(q1b[b][:, :], h2b[:, :], MAGIC, None, AL.subtract),
                     (t24,) + bd(("q1b", b)))
        setb("h2b", t25)
        t26 = P.comp("ve", lambda e, f=f: e.tensor_scalar(
            scgvc[:, f:f + 1], col_a[:, :], 1.0 / QB, wmcols[:, 0:1], AL.mult, AL.mult), (t21, twmc))
        cols["g"] = t26
        trql = None
        for kc in range(16):
            bank = kc % 2
            pb = ps[bank]
            ttr = P.comp("te", lambda e, kc=kc, b=b, pb=pb: e.transpose(
                pb.bitcast(BF16)[:, 0:128], q1b[b][:, kc * 128:(kc + 1) * 128], ident[:, :]),
                (t25,) + pd(bank))
            tev = P.comp("sc", lambda e, kc=kc, f=f, pb=pb: e.copy(
                rqf[:, kc * T + f * 128: kc * T + (f + 1) * 128], pb.bitcast(BF16)[:, 0:128]), (ttr,))
            setp(bank, tev)
            trql = ttr
            gi_toks.append(tev)
        setb(("q1b", b), trql)
        gi_toks += [tx1, t26]

    if DBG:
        P.dma("sy", "d_tap", lambda e: e.dma_start(out=taps["tap_x1"], in_=x1_sp),
              deps=tuple(gi_toks) + (P.tok("d_x1w"),))
        P.dma("sy", "d_tap", lambda e: e.dma_start(out=taps["tap_wm"], in_=scr_wm), deps=(twmc,))

    tp2 = poll(2, deps=tuple(gi_toks))
    tp2o = poll(6, deps=(tf6,))

    close_scope(markG2)
    wjv = [sb("wjv%d" % i, [128, 16 * 512], BF16) for i in range(2)]
    wjg = [sb("wjg%d" % i, [128, 16 * 512], BF16) for i in range(2)]
    vsc = sb("vsc", [128, 512], F32)
    gsc = sb("gsc", [128, 512], F32)
    ubf = [sb("ubf%d" % i, [128, 512], F32) for i in range(2)]

    j_toks = []
    for qp in range(16):
        b = qp % 2
        tvk = []
        tgk = []
        for kc in range(16):
            tv = P.dma("sc", "d_jv%d" % b, lambda e, qp=qp, b=b, kc=kc: e.dma_start_transpose(
                wjv[b][:, kc * 512:(kc + 1) * 512],
                sh_wgv[qp * 512:(qp + 1) * 512, kc * 128:(kc + 1) * 128]),
                deps=(tp2,) + (bd(("wjv", b)) if kc == 0 else ()))
            tg = P.dma("sc", "d_jg%d" % b, lambda e, qp=qp, b=b, kc=kc: e.dma_start_transpose(
                wjg[b][:, kc * 512:(kc + 1) * 512],
                sh_wgv[HID2 + qp * 512:HID2 + (qp + 1) * 512, kc * 128:(kc + 1) * 128]),
                deps=(tp2,) + (bd(("wjg", b)) if kc == 0 else ()))
            tvk.append(tv)
            tgk.append(tg)
        mlast = None
        for t in range(16):
            ub = t % 2
            pv = ps[4 + 2 * (t % 2)]
            pg = ps[5 + 2 * (t % 2)]
            for kc in range(16):
                def mmv(e, kc=kc, t=t, b=b, pv=pv):
                    return e.matmul(pv[:, :], rqf[:, kc * T + t * 128: kc * T + (t + 1) * 128],
                                    wjv[b][:, kc * 512:(kc + 1) * 512], start=(kc == 0), stop=(kc == 15))

                def mmg(e, kc=kc, t=t, b=b, pg=pg):
                    return e.matmul(pg[:, :], rqf[:, kc * T + t * 128: kc * T + (t + 1) * 128],
                                    wjg[b][:, kc * 512:(kc + 1) * 512], start=(kc == 0), stop=(kc == 15))
                dv = (tvk[kc],) + (pd(4 + 2 * (t % 2)) if kc == 0 else ())
                dg = (tgk[kc],) + (pd(5 + 2 * (t % 2)) if kc == 0 else ())
                P.comp("te", mmv, dv if t == 0 else (dv if kc == 0 else ()))
                mlast = P.comp("te", mmg, dg if t == 0 else (dg if kc == 0 else ()))
            t1 = P.comp("ve", lambda e, t=t, pv=pv: e.tensor_scalar(
                vsc[:, :], pv[:, :], scgvc[:, t:t + 1], None, AL.mult), (mlast,) + bd("vsc"))
            t2 = P.comp("sc", lambda e, t=t, pg=pg: e.activation(
                gsc[:, :], pg[:, :], AF.Silu, scale=scgvc[:, t:t + 1]), (mlast,) + bd("gsc"))
            t3 = P.comp("ve", lambda e, ub=ub: e.tensor_tensor(ubf[ub][:, :], vsc[:, :], gsc[:, :], AL.mult),
                        (t1, t2) + bd(("ubf", ub)))
            setb("vsc", t3)
            setb("gsc", t3)
            setp(4 + 2 * (t % 2), t3)
            setp(5 + 2 * (t % 2), t3)
            tu = P.dma("gp", "d_uw", lambda e, qp=qp, t=t, ub=ub: e.dma_start(
                out=u_sp[t * 128:(t + 1) * 128, qp * 512:(qp + 1) * 512], in_=ubf[ub][:, :]), deps=(t3,))
            setb(("ubf", ub), tu)
            j_toks.append(tu)
        setb(("wjv", b), mlast)
        setb(("wjg", b), mlast)

    if DBG:
        P.dma("sy", "d_tap", lambda e: e.dma_start(out=taps["tap_u"], in_=u_sp),
              deps=tuple(j_toks) + (P.tok("d_uw"),))

    jphase_end = [P.tok("e_ve"), P.tok("e_te"), P.tok("e_sc"), P.tok("d_uw")]
    close_scope(markG)

    markLM = len(cms)
    idf2 = sb("idf2", [128, 128], F32)
    uqf = sb("uqf", [128, 64 * 512], BF16)
    urb = [sb("urb", [128, HID2], F32)] * 2
    uq_tm = sb("uq_tm", [128, HID2], BF16)
    s2rep = sb("s2rep", [128, 512], F32)
    wmb = [sb("wmb%d" % i, [128, 64 * 128], BF16) for i in range(2)]
    mo = [sb("mo%d" % i, [128, 1024], F32) for i in range(2)]
    mt = [sb("mt%d" % i, [128, 128], F32) for i in range(2)]

    tidf2 = P.comp("gp", lambda e: e.memset(idf2[:, :], 1.0), tuple(jphase_end))
    tidf2 = P.comp("gp", lambda e: e.affine_select(
        idf2[:, :], idf2[:, :], [[1, 128]], AL.is_equal, 0.0, base=0, channel_multiplier=-1), (tidf2,))

    m_toks = []
    for sup in range(4):
        l_toks = []
        for t8 in range(4):
            t = sup * 4 + t8
            b = t8 % 2
            tl = P.dma("sy", "d_ur%d" % b, lambda e, t=t, b=b: e.dma_start(
                out=urb[b][:, :], in_=u_sp[t * 128:(t + 1) * 128, :]),
                deps=tuple(jphase_end) + bd("urb"))
            t1 = P.comp("ve", lambda e, b=b: e.tensor_reduce(
                col_a[:, :], urb[b][:, :].rearrange("p (o f) -> p o f", o=1), AX.X, AL.max,
                apply_absolute_value=True), (tl, cols["g"]))
            t2 = P.comp("ve", lambda e: e.tensor_scalar(col_a[:, :], col_a[:, :], EPS, None, AL.max), (t1,))
            t3 = P.comp("ve", lambda e: e.reciprocal(col_b[:, :], col_a[:, :]), (t2,))
            t4 = P.comp("ve", lambda e: e.tensor_scalar(col_b[:, :], col_b[:, :], QB, None, AL.mult), (t3,))
            tqs = []
            tprev = t4
            for seg in range(8):
                sg = mo[seg % 2]
                tm1 = P.comp("ve", lambda e, b=b, seg=seg, sg=sg: e.tensor_scalar(
                    sg[:, 0:1024], urb[b][:, seg * 1024:(seg + 1) * 1024], col_b[:, :], MAGIC,
                    AL.mult, AL.add), (tprev,) + bd(("mo", seg % 2)))
                tq = P.comp("ve", lambda e, seg=seg, sg=sg: e.tensor_scalar(
                    uq_tm[:, seg * 1024:(seg + 1) * 1024], sg[:, 0:1024], MAGIC, None, AL.subtract),
                    (tm1,) + bd(("uq", seg)))
                setb(("mo", seg % 2), tq)
                tqs.append(tq)
                tprev = tm1
            t6 = P.comp("ve", lambda e: e.tensor_scalar(
                col_c[:, :], col_a[:, :], 1.0 / QB, wmcols[:, 1:2], AL.mult, AL.mult), (t2, twmc))
            cols["g"] = t6
            ts2 = P.dma("gp", "d_s2", lambda e, t=t: e.dma_start(
                out=scr_sc2[0:1, t * 128:(t + 1) * 128].rearrange("o p -> p o"), in_=col_c[:, :]),
                deps=(t6,))
            setb("urb", tqs[-1])
            for kc in range(64):
                bank = kc % 2
                pb = ps[bank]
                ttr = P.comp("te", lambda e, kc=kc, pb=pb: e.transpose(
                    pb.bitcast(BF16)[:, 0:128], uq_tm[:, kc * 128:(kc + 1) * 128], ident[:, :]),
                    (tqs[kc // 8],) + pd(bank))
                tev = P.comp("sc", lambda e, kc=kc, t8=t8, pb=pb: e.copy(
                    uqf[:, kc * 512 + t8 * 128: kc * 512 + (t8 + 1) * 128],
                    pb.bitcast(BF16)[:, 0:128]), (ttr,) + bd("uqf"))
                setp(bank, tev)
                if kc % 8 == 7:
                    setb(("uq", kc // 8), ttr)
                l_toks.append(tev)
            l_toks.append(ts2)
        ts2r = P.dma("sy", "d_c", lambda e, sup=sup: e.dma_start(
            out=s2rep[:, :], in_=scr_sc2[0:1, sup * 512:(sup + 1) * 512].to_broadcast((128, 512))),
            deps=tuple(l_toks[-3:]) + (P.tok("d_s2"),) + bd("s2rep"))
        mq_last = None
        for ob in range(16):
            b = ob % 2
            tlw = P.dma("sc", "d_mw%d" % b, lambda e, ob=ob, b=b: e.dma_start(
                out=wmb[b][:, :], in_=sh_woutT[ob * 128:(ob + 1) * 128, :]),
                deps=(tp2o,) + bd(("wmb", b)))
            bank = 4 + ob % 4
            pb0 = ps[bank]
            mlast = None
            for kc in range(64):
                def mm0(e, kc=kc, b=b, pb0=pb0):
                    return e.matmul(pb0[:, :], wmb[b][:, kc * 128:(kc + 1) * 128],
                                    uqf[:, kc * 512:(kc + 1) * 512],
                                    start=(kc == 0), stop=(kc == 63))
                d = (tlw, ts2r) + pd(bank) + tuple(l_toks[-2:]) if kc == 0 else ()
                mlast = P.comp("te", mm0, d)
            setb(("wmb", b), mlast)
            mq_last = mlast
            tev2 = P.comp("ve", lambda e, b=b, pb0=pb0: e.tensor_tensor(
                mo[b][:, 0:512], pb0[:, :], s2rep[:, 0:512], AL.mult),
                (mlast, ts2r) + bd(("mo", b)))
            setp(bank, tev2)
            for tt in range(4):
                bank2 = 2 + tt % 2
                pb2 = ps[bank2]
                ttr = P.comp("te", lambda e, b=b, tt=tt, pb2=pb2: e.transpose(
                    pb2[:, 0:128], mo[b][:, tt * 128:(tt + 1) * 128], idf2[:, :]),
                    (tev2, tidf2) + pd(bank2))
                tev3 = P.comp("ve", lambda e, tt=tt, pb2=pb2: e.tensor_copy(
                    mt[tt % 2][:, :], pb2[:, 0:128]), (ttr,) + bd(("mt", tt % 2)))
                setp(bank2, tev3)
                tsp = P.dma("gp", "d_flw", lambda e, sup=sup, ob=ob, tt=tt: e.dma_start(
                    out=fl_sp[sup * 512 + tt * 128: sup * 512 + (tt + 1) * 128,
                              ob * 128:(ob + 1) * 128],
                    in_=mt[tt % 2][:, :]), deps=(tev3,))
                setb(("mt", tt % 2), tsp)
                if tt == 3:
                    setb(("mo", b), tev3)
                m_toks.append(tsp)
        setb("uqf", mq_last)
        setb("s2rep", mq_last)

    if DBG:
        P.dma("sy", "d_tap", lambda e: e.dma_start(out=taps["tap_fl"], in_=fl_sp),
              deps=tuple(m_toks) + (P.tok("d_flw"),))
        P.dma("sy", "d_tap", lambda e: e.dma_start(out=taps["tap_sc2"], in_=scr_sc2),
              deps=(P.tok("d_s2"),))

    mphase_end = [P.tok("e_ve"), P.tok("e_te"), P.tok("d_flw"), P.tok("d_x1w")]
    close_scope(markLM)

    x1n = [sb("x1n%d" % i, [128, D], F32) for i in range(2)]
    fln = [sb("fln%d" % i, [128, D], F32) for i in range(2)]
    frt = sb("frt", [128, D], F32)
    x2t = [sb("x2t%d" % i, [128, D], F32) for i in range(2)]

    for f in range(16):
        b = f % 2
        tl1 = P.dma("sy", "d_n1%d" % b, lambda e, f=f, b=b: e.dma_start(
            out=x1n[b][:, :], in_=x1_sp[f * 128:(f + 1) * 128, :]),
            deps=tuple(mphase_end) + bd(("x1n", b)))
        tl2 = P.dma("sy", "d_n2%d" % b, lambda e, f=f, b=b: e.dma_start(
            out=fln[b][:, :], in_=fl_sp[f * 128:(f + 1) * 128, :]),
            deps=tuple(mphase_end) + bd(("fln", b)))
        t1 = P.comp("ve", lambda e, f=f, b=b: e.tensor_scalar(
            frt[:, :], x1n[b][:, :], m2c[:, f:f + 1], rs2c[:, f:f + 1], AL.subtract, AL.mult),
            (tl1,) + bd("frt"))
        t2 = P.comp("ve", lambda e, b=b: e.tensor_tensor(frt[:, :], frt[:, :], fln[b][:, :], AL.add),
                    (tl2, t1))
        setb(("fln", b), t2)
        trs, _ = emit_ln_stats(frt[:, :], D, (t2,))
        t6 = P.comp("ve", lambda e: e.tensor_scalar(
            frt[:, :], frt[:, :], col_e[:, :], col_b[:, :], AL.subtract, AL.mult), (trs,))
        t7 = P.comp("ve", lambda e, b=b: e.tensor_tensor(x2t[b][:, :], x1n[b][:, :], frt[:, :], AL.add),
                    (t6,) + bd(("x2t", b)))
        cols["g"] = t7
        setb("frt", t7)
        setb(("x1n", b), t7)
        to = P.dma("sy", "d_out", lambda e, f=f, b=b: e.dma_start(
            out=y_out[f * 128:(f + 1) * 128, :], in_=x2t[b][:, :]), deps=(t7,))
        setb(("x2t", b), to)

    fin = P.tok("d_out")

    def final_wait(e, Pr):
        e.wait_ge(Pr.sems["d_out"], fin[1])
    P.raw("sy", final_wait)

    with nc.Block() as block:
        @block.tensor
        def _(e):
            P.emit("te", e)

        @block.vector
        def _(e):
            P.emit("ve", e)

        @block.scalar
        def _(e):
            P.emit("sc", e)

        @block.gpsimd
        def _(e):
            P.emit("gp", e)

        @block.sync
        def _(e):
            P.emit("sy", e)

    for cm in reversed(cms):
        cm.__exit__(None, None, None)
    for cm in reversed(P._stack):
        cm.__exit__(None, None, None)

    return nc


_NC_CACHE = {}


def kernel(**inputs):
    x = np.asarray(inputs["L0_attn_x"], np.float32)
    B = x.shape[0]
    nonce = np.uint32(int.from_bytes(os.urandom(4), "little") | 1)

    if "nc" not in _NC_CACHE:
        _NC_CACHE["nc"] = build_program()
    nc = _NC_CACHE["nc"]

    g = lambda k: np.ascontiguousarray(np.asarray(inputs[k], np.float32))
    wq, wk, wv, wg, wo = g("L0_attn_wq"), g("L0_attn_wk"), g("L0_attn_wv"), g("L0_attn_wg"), g("L0_attn_wo")
    bq, bk, bv, bg, bo = g("L0_attn_bq"), g("L0_attn_bk"), g("L0_attn_bv"), g("L0_attn_bg"), g("L0_attn_bo")
    gnw, gnb, gam = g("L0_attn_gn_w"), g("L0_attn_gn_b"), g("L0_attn_gamma")
    wnorm, wgv, wout = g("L0_ffn_w_norm"), g("L0_ffn_w_gv"), g("L0_ffn_w_out")

    in_maps = []
    for c in range(8):
        b, par = c // 2, c % 2
        hs = slice(par * 1024, (par + 1) * 1024)
        in_maps.append({
            "x_full": np.ascontiguousarray(x[b]),
            "x_half": np.ascontiguousarray(x[b, par * T:(par + 1) * T]),
            "wq_h": np.ascontiguousarray(wq[:, hs]),
            "wk_h": np.ascontiguousarray(wk[:, hs]),
            "wv_h": np.ascontiguousarray(wv[:, hs]),
            "wg_h": np.ascontiguousarray(wg[:, hs]),
            "wo_h": np.ascontiguousarray(wo[hs, :]),
            "bq_h": np.ascontiguousarray(bq[hs])[None, :],
            "bk_h": np.ascontiguousarray(bk[hs])[None, :],
            "bv_h": np.ascontiguousarray(bv[hs])[None, :],
            "bg_h": np.ascontiguousarray(bg[hs])[None, :],
            "bo_f": bo[None, :],
            "gnw_h": np.ascontiguousarray(gnw[hs])[None, :],
            "gnb_h": np.ascontiguousarray(gnb[hs])[None, :],
            "gam_h": np.ascontiguousarray(gam[par * 8:(par + 1) * 8])[:, None],
            "wnorm_f": wnorm[None, :],
            "wgv_hf": np.ascontiguousarray(wgv[par * HID2:(par + 1) * HID2, :]),
            "wout_hf": np.ascontiguousarray(wout[:, par * 4096:(par + 1) * 4096]),
            "par_h": np.array([[par]], np.uint32),
            "non_h": np.array([[nonce]], np.uint32),
        })

    res = run_bass_kernel_spmd(nc, in_maps, core_ids=list(range(8)))
    out = np.empty((B, S, D), np.float32)
    for c in range(8):
        b, par = c // 2, c % 2
        out[b, par * T:(par + 1) * T] = res.results[c]["y_out"]
    if DBG:
        kernel._taps = res.results
    return out


# version 10
# speedup vs baseline: 1.0717x; 1.0717x over previous
import os

import numpy as np

import concourse.bass as bass
import concourse.mybir as mybir
from concourse.bass_utils import run_bass_kernel_spmd

F32 = mybir.dt.float32
BF16 = mybir.dt.bfloat16
U32 = mybir.dt.uint32
I32 = mybir.dt.int32
AL = mybir.AluOpType
AF = mybir.ActivationFunctionType
AX = mybir.AxisListType

MAGIC = 12582912.0
EPS = 1e-5
QB = 127.0

D = 2048
S = 4096
T = 2048
HLOC = 8
NSUB = 4
SUBT = 1024
HID2 = 8192
DBG = os.environ.get("MK_TAPS", "")
SERIAL = os.environ.get("MK_SERIAL", "")

ENGS = ("te", "ve", "sc", "gp", "sy")


class Prog:
    def __init__(self, nc):
        self.nc = nc
        self.ops = {e: [] for e in ENGS}
        self.cnt = {}
        self.sems = {}
        self.wm = {e: {} for e in ENGS}
        self._stack = []

    def sem(self, name):
        if name not in self.sems:
            cm = self.nc.semaphore(name)
            self.sems[name] = cm.__enter__()
            self._stack.append(cm)
            self.cnt[name] = 0
        return self.sems[name]

    def _waits(self, eng, deps):
        w = []
        wm = self.wm[eng]
        for d in deps:
            if d is None:
                continue
            s, v = d
            if v > 0 and wm.get(s, 0) < v:
                w.append((self.sems[s], v))
                wm[s] = v
        return w

    def _serial_deps(self):
        return tuple((k, v) for k, v in self.cnt.items()
                     if v > 0 and k != "d_poll")

    def comp(self, eng, fn, deps=(), sem=None):
        if SERIAL:
            deps = tuple(deps) + self._serial_deps()
        sname = sem or ("e_" + eng)
        self.sem(sname)
        self.cnt[sname] += 1
        tok = (sname, self.cnt[sname])
        self.ops[eng].append((self._waits(eng, deps), fn, (self.sems[sname], 1)))
        return tok

    def dma(self, eng, sname, fn, deps=()):
        if SERIAL:
            deps = tuple(deps) + self._serial_deps()
        self.sem(sname)
        self.cnt[sname] += 16
        tok = (sname, self.cnt[sname])
        self.ops[eng].append((self._waits(eng, deps), fn, (self.sems[sname], 16)))
        return tok

    def raw(self, eng, fn, deps=()):
        self.ops[eng].append((self._waits(eng, deps), fn, None))

    def tok(self, sname):
        return (sname, self.cnt.get(sname, 0))

    def emit(self, eng, e):
        for waits, fn, inc in self.ops[eng]:
            for s, v in waits:
                e.wait_ge(s, v)
            if inc is None:
                fn(e, self)
            else:
                fn(e).then_inc(inc[0], inc[1])


def build_program():
    nc = bass.Bass("TRN2", target_bir_lowering=False)

    def inp(name, shape, dt=F32):
        return nc.dram_tensor(name, shape, dt, kind="ExternalInput").ap()

    x_full = inp("x_full", [S, D])
    x_half = inp("x_half", [T, D])
    wq_h = inp("wq_h", [D, 1024])
    wk_h = inp("wk_h", [D, 1024])
    wv_h = inp("wv_h", [D, 1024])
    wg_h = inp("wg_h", [D, 1024])
    wo_h = inp("wo_h", [1024, D])
    bq_h = inp("bq_h", [1, 1024])
    bk_h = inp("bk_h", [1, 1024])
    bv_h = inp("bv_h", [1, 1024])
    bg_h = inp("bg_h", [1, 1024])
    bo_f = inp("bo_f", [1, D])
    gnw_h = inp("gnw_h", [1, 1024])
    gnb_h = inp("gnb_h", [1, 1024])
    gam_h = inp("gam_h", [HLOC, 1])
    wnorm_f = inp("wnorm_f", [1, D])
    wgv_hf = inp("wgv_hf", [HID2, D])
    wout_hf = inp("wout_hf", [D, 4096])
    par_h = inp("par_h", [1, 1], U32)
    non_h = inp("non_h", [1, 1], U32)

    y_out = nc.dram_tensor("y_out", [T, D], F32, kind="ExternalOutput").ap()

    def shd(name, shape, dt=F32):
        return nc.dram_tensor(name, shape, dt, kind="Internal", addr_space="Shared").ap()

    sh_wgv = shd("sh_wgv", [2 * HID2, D], BF16)
    sh_wout = shd("sh_wout", [D, HID2], BF16)
    sh_ye = shd("sh_ye", [S, D])
    sh_yo = shd("sh_yo", [S, D])
    sh_stat_e = shd("sh_stat_e", [1, 2])
    sh_stat_o = shd("sh_stat_o", [1, 2])
    sh_flags = shd("sh_flags", [1, 8], U32)

    def loc(name, shape, dt=F32):
        return nc.dram_tensor(name, shape, dt, kind="Internal").ap()

    scr_tab = loc("scr_tab", [24, 132])
    scr_gc = loc("scr_gc", [1, 8])
    scr_wm = loc("scr_wm", [1, 4])
    scr_gn = loc("scr_gn", [2, 1024])
    ln0f_sp = loc("ln0f_sp", [D, S], BF16)
    ret_sp = loc("ret_sp", [S, 1024])
    u_sp = loc("u_sp", [T, HID2])
    x1_sp = loc("x1_sp", [T, D])
    fl_sp = loc("fl_sp", [T, D])

    taps = {}
    if DBG:
        for nm, shape, dt in [("tap_ret", [S, 1024], F32), ("tap_ln0f", [D, S], BF16),
                              ("tap_y", [S, D], F32), ("tap_x1", [T, D], F32),
                              ("tap_u", [T, HID2], F32), ("tap_fl", [T, D], F32),
                              ("tap_gn", [2, 1024], F32), ("tap_wm", [1, 4], F32),
                              ("tap_sc2", [1, T], F32)]:
            taps[nm] = nc.dram_tensor(nm, shape, dt, kind="ExternalOutput").ap()

    P = Prog(nc)
    cms = []
    bdep = {}
    pdep = [None] * 8

    def bd(key):
        return tuple(bdep.get(key) or ())

    def setb(key, *toks):
        bdep[key] = [t for t in toks if t is not None]

    def pd(i):
        return tuple(pdep[i] or ())

    def setp(i, *toks):
        pdep[i] = [t for t in toks if t is not None]

    def sb(name, shape, dt):
        cm = nc.sbuf_tensor(name, shape, dt)
        t = cm.__enter__()
        cms.append(cm)
        return t.ap() if hasattr(t, "ap") else t

    def close_scope(mark):
        while len(cms) > mark:
            cms.pop().__exit__(None, None, None)

    ps = []
    for i in range(8):
        cm = nc.psum_tensor("pb%d" % i, [128, 512], F32)
        t = cm.__enter__()
        cms.append(cm)
        ps.append(t.ap() if hasattr(t, "ap") else t)

    par_s = sb("par_s", [1, 1], U32)
    non_s = sb("non_s", [1, 1], U32)
    flg_s = sb("flg_s", [1, 8], U32)
    ident = sb("ident", [128, 128], BF16)
    idf = sb("idf", [128, 128], F32)
    ones_f = sb("ones_f", [128, 128], F32)
    col_a = sb("col_a", [128, 1], F32)
    col_b = sb("col_b", [128, 1], F32)
    col_c = sb("col_c", [128, 1], F32)
    col_d = sb("col_d", [128, 1], F32)
    col_e = sb("col_e", [128, 1], F32)
    col_f = sb("col_f", [128, 1], F32)
    m2c = sb("m2c", [128, 16], F32)
    rs2c = sb("rs2c", [128, 16], F32)
    srsc = sb("srsc", [128, 16], F32)
    scgvc = sb("scgvc", [128, 16], F32)
    wmcols = sb("wmcols", [128, 4], F32)
    sc_scr = sb("sc_scr", [128, D], F32)

    t0 = P.dma("gp", "d_c", lambda e: e.dma_start(out=par_s[:, :], in_=par_h))
    t0b = P.dma("gp", "d_c", lambda e: e.dma_start(out=non_s[:, :], in_=non_h))

    state = {}

    def sy_prologue(e, Pr):
        state["rp"] = e.alloc_register("rp")
        state["rt"] = e.alloc_register("rt")
        pv = e.value_load(par_s[0:1, 0:1])
        e.reg_mov(state["rp"], pv)
        e.reg_mov(state["rt"], 0)

    P.raw("sy", sy_prologue, deps=(t0, t0b))

    def gp_prologue(e, Pr):
        state["rp_gp"] = e.alloc_register("rp")
        pv = e.value_load(par_s[0:1, 0:1])
        e.reg_mov(state["rp_gp"], pv)

    P.raw("gp", gp_prologue, deps=(t0, t0b))
    psem = P.sem("d_poll")

    def branch_dma(sname, fn_e, fn_o, deps=()):
        P.sem(sname)
        P.cnt[sname] += 16
        tok = (sname, P.cnt[sname])
        s = P.sems[sname]

        def cl(e, Pr):
            with e.If_eq(state["rp"], 0):
                fn_e(e).then_inc(s, 16)
            with e.Else():
                fn_o(e).then_inc(s, 16)
        P.raw("sy", cl, deps=deps)
        return tok

    def branch_dma_gp(sname, fn_e, fn_o, deps=()):
        P.sem(sname)
        P.cnt[sname] += 16
        tok = (sname, P.cnt[sname])
        s = P.sems[sname]

        def cl(e, Pr):
            with e.If_eq(state["rp_gp"], 0):
                fn_e(e).then_inc(s, 16)
            with e.Else():
                fn_o(e).then_inc(s, 16)
        P.raw("gp", cl, deps=deps)
        return tok

    def set_flag(slot_even, deps=()):
        return branch_dma(
            "d_sh",
            lambda e: e.dma_start(out=sh_flags[0:1, slot_even:slot_even + 1], in_=non_s[:, :]),
            lambda e: e.dma_start(out=sh_flags[0:1, slot_even + 1:slot_even + 2], in_=non_s[:, :]),
            deps=deps)

    def poll(slot_even, deps=()):
        P.sem("e_sy")
        P.cnt["e_sy"] += 1
        tok = ("e_sy", P.cnt["e_sy"])
        esy = P.sems["e_sy"]

        def cl(e, Pr):
            def cond():
                e.reg_add(state["rt"], state["rt"], 16)
                e.dma_start(out=flg_s[0:1, :], in_=sh_flags).then_inc(psem, 16)
                e.wait_ge(psem, state["rt"])
                fe = e.value_load(flg_s[0:1, slot_even:slot_even + 1])
                fo = e.value_load(flg_s[0:1, slot_even + 1:slot_even + 2])
                n = e.value_load(non_s[0:1, 0:1])
                return (fe != n) | (fo != n)
            with e.While(cond):
                e.nop(cycle_cnt=500)
            e.nop().then_inc(esy, 1)
        P.raw("sy", cl, deps=deps)
        return tok

    tones_bf = P.comp("gp", lambda e: e.memset(ident[:, :], 1.0))
    tid = P.comp("gp", lambda e: e.affine_select(
        ident[:, :], ident[:, :], [[1, 128]], AL.is_equal, 0.0,
        base=0, channel_multiplier=-1), (tones_bf,))
    tones = P.comp("gp", lambda e: e.memset(ones_f[:, :], 1.0))
    tidf = P.comp("gp", lambda e: e.affine_select(
        idf[:, :], ones_f[:, :], [[1, 128]], AL.is_equal, 0.0,
        base=0, channel_multiplier=-1), (tones,))
    P.comp("sc", lambda e: e.preload_act_table(AF.Silu))

    cols = {"g": None}

    def emit_rsqrt(dst, src, scr, deps):
        t = P.comp("ve", lambda e: e.tensor_scalar(
            dst.bitcast(I32), src.bitcast(I32), 1, None, AL.arith_shift_right), deps)
        t = P.comp("ve", lambda e: e.tensor_scalar(
            dst.bitcast(I32), dst.bitcast(I32), 0, None, AL.bitwise_not), (t,))
        t = P.comp("ve", lambda e: e.tensor_scalar(
            dst.bitcast(I32), dst.bitcast(I32), 0x5F3759E0, None, AL.add), (t,))
        for _ in range(3):
            t = P.comp("ve", lambda e: e.tensor_tensor(scr, dst, dst, AL.mult), (t,))
            t = P.comp("ve", lambda e: e.tensor_tensor(scr, scr, src, AL.mult), (t,))
            t = P.comp("ve", lambda e: e.tensor_scalar(scr, scr, -0.5, 1.5, AL.mult, AL.add), (t,))
            t = P.comp("ve", lambda e: e.tensor_tensor(dst, dst, scr, AL.mult), (t,))
        return t

    def emit_ln_stats(xap, width, deps):
        inv = 1.0 / float(width)
        dd = tuple(deps) + ((cols["g"],) if cols["g"] else ())
        ts1 = P.comp("sc", lambda e: e.activation(
            sc_scr[:, 0:width], xap, AF.Copy, accum_out=col_a[:, :]), dd)
        ts2 = P.comp("sc", lambda e: e.activation(
            sc_scr[:, 0:width], xap, AF.Square, accum_out=col_d[:, :]), (ts1,))
        t1 = P.comp("ve", lambda e: e.tensor_scalar(col_e[:, :], col_a[:, :], inv, None, AL.mult), (ts1,))
        t2 = P.comp("ve", lambda e: e.tensor_scalar(col_d[:, :], col_d[:, :], inv, None, AL.mult), (ts2,))
        t3 = P.comp("ve", lambda e: e.tensor_tensor(col_c[:, :], col_e[:, :], col_e[:, :], AL.mult), (t1,))
        t4 = P.comp("ve", lambda e: e.tensor_tensor(col_d[:, :], col_d[:, :], col_c[:, :], AL.subtract), (t2, t3))
        t5 = P.comp("ve", lambda e: e.tensor_scalar(col_d[:, :], col_d[:, :], EPS, None, AL.add), (t4,))
        return emit_rsqrt(col_b, col_d, col_c, (t5,)), ts2

    wsb = [sb("wsb%d" % i, [128, 2048], F32) for i in range(2)]
    wqb_ = sb("wqb_", [128, 2048], F32)
    wqo = [sb("wqo%d" % i, [128, 2048], BF16) for i in range(2)]

    markA = len(cms)
    pw = sb("pw", [HLOC, 132], F32)
    ipw = sb("ipw", [HLOC, 132], F32)
    rpw = sb("rpw", [HLOC, 132], F32)
    gam_s = sb("gam_s", [HLOC, 1], F32)
    tri = sb("tri", [128, 128], F32)
    dqt = sb("dqt", [128, HLOC * 128], F32)
    decS = sb("decS", [128, HLOC], F32)
    minv = sb("minv", [128, HLOC], F32)
    gc_cols = sb("gc_cols", [128, HLOC], F32)
    bqc = sb("bqc", [128, HLOC], F32)
    bkc = sb("bkc", [128, HLOC], F32)
    bv_rep = sb("bv_rep", [128, 1024], F32)
    gnw_hd = sb("gnw_hd", [HLOC, 128], F32)
    gnb_hd = sb("gnb_hd", [HLOC, 128], F32)
    ln0f = sb("ln0f", [128, 16 * SUBT], BF16)
    kf = sb("kf", [128, HLOC * SUBT], BF16)
    qdec = sb("qdec", [128, HLOC * SUBT], BF16)
    vtm = sb("vtm", [128, 8 * 1024], BF16)
    xbuf = [sb("xbuf%d" % i, [128, D], F32) for i in range(2)]
    ln0tm = [sb("ln0tm%d" % i, [128, D], BF16) for i in range(2)]
    wkqb = [sb("wkqb%d" % i, [128, 16 * 128], BF16) for i in range(2)]
    wvb = [sb("wvb%d" % i, [128, 16 * 256], BF16) for i in range(2)]
    scbf = [sb("scbf%d" % i, [128, 128], BF16) for i in range(4)]
    ktd = [sb("ktd%d" % i, [128, 128], BF16) for i in range(4)]
    retsc = [sb("retsc%d" % i, [128, 512], F32) for i in range(2)]
    sqsc = sb("sqsc", [128, 512], F32)
    tmp4 = sb("tmp4", [128, 8], F32)
    accS = sb("accS", [128, HLOC], F32)
    accQ = sb("accQ", [128, HLOC], F32)
    Sst = sb("Sst", [128, HLOC * 128], F32)
    Sbf = sb("Sbf", [128, HLOC * 128], BF16)
    qtmp = sb("qtmp", [128, 512], F32)
    wred = sb("wred", [128, 2], F32)
    wacc = sb("wacc", [128, 2], F32)
    wscol = sb("wscol", [2, 2], F32)
    wst2 = sb("wst2", [1, 4], F32)
    gsum = sb("gsum", [HLOC, 2], F32)
    gmrs = sb("gmrs", [HLOC, 4], F32)
    a_hd = sb("a_hd", [HLOC, 128], F32)
    b_hd = sb("b_hd", [HLOC, 128], F32)

    tg = P.dma("gp", "d_c", lambda e: e.dma_start(out=gam_s[:, :], in_=gam_h))
    tp = P.comp("gp", lambda e: e.memset(pw[:, 0:1], 1.0))
    tp = P.comp("ve", lambda e: e.tensor_scalar(pw[:, 1:2], pw[:, 0:1], gam_s[:, :], None, AL.mult), (tg, tp))
    n = 1
    while n < 128:
        def step(e, n=n):
            return e.tensor_scalar(pw[:, n + 1:2 * n + 1], pw[:, 1:n + 1], pw[:, n:n + 1], None, AL.mult)
        tp = P.comp("ve", step, (tp,))
        n *= 2
    tip = P.comp("ve", lambda e: e.reciprocal(ipw[:, 0:129], pw[:, 0:129]), (tp,))
    trp = P.comp("ve", lambda e: e.tensor_scalar(rpw[:, 0:129], ipw[:, 0:129], pw[:, 128:129], None, AL.mult), (tip,))
    tt1 = P.dma("gp", "d_c", lambda e: e.dma_start(out=scr_tab[0:HLOC, 0:129], in_=pw[:, 0:129]), (tp,))
    tt2 = P.dma("gp", "d_c", lambda e: e.dma_start(out=scr_tab[8:8 + HLOC, 0:129], in_=ipw[:, 0:129]), (tip,))
    tt3 = P.dma("gp", "d_c", lambda e: e.dma_start(out=scr_tab[16:16 + HLOC, 0:129], in_=rpw[:, 0:129]), (trp,))
    tt4 = P.dma("gp", "d_c", lambda e: e.dma_start(
        out=scr_gc[0:1, 0:HLOC].rearrange("o p -> p o"), in_=pw[:, 128:129]), (tp,))

    ttri = P.comp("gp", lambda e: e.affine_select(
        tri[:, :], ones_f[:, :], [[1, 128]], AL.is_ge, 0.0,
        base=0, channel_multiplier=-1), (tones,))
    tinv = []
    for h in range(HLOC):
        def ld(e, h=h):
            return e.dma_start(out=decS[:, h:h + 1],
                               in_=scr_tab[16 + h, 1:129].unsqueeze(0).rearrange("o p -> p o"))
        tinv.append(P.dma("sy", "d_c", ld, (tt3,)))
    tmi = []
    for h in range(HLOC):
        def ld(e, h=h):
            return e.dma_start(out=minv[:, h:h + 1],
                               in_=scr_tab[8 + h, 1:129].unsqueeze(0).rearrange("o p -> p o"))
        tmi.append(P.dma("sy", "d_c", ld, (tt2,)))
    tdq = []
    for h in range(HLOC):
        def ld(e, h=h):
            return e.dma_start(out=dqt[:, h * 128:(h + 1) * 128],
                               in_=scr_tab[h, 1:129].unsqueeze(0).to_broadcast((128, 128)))
        tdq.append(P.dma("sy", "d_c", ld, (tt1,)))
    tgc = P.dma("sy", "d_c", lambda e: e.dma_start(
        out=gc_cols[:, :], in_=scr_gc[0:1, 0:HLOC].to_broadcast((128, HLOC))), (tt4,))

    tb = []
    for h in range(HLOC):
        def lq(e, h=h):
            return e.dma_start(out=bqc[:, h:h + 1],
                               in_=bq_h[0, h * 128:(h + 1) * 128].unsqueeze(0).rearrange("o p -> p o"))

        def lk(e, h=h):
            return e.dma_start(out=bkc[:, h:h + 1],
                               in_=bk_h[0, h * 128:(h + 1) * 128].unsqueeze(0).rearrange("o p -> p o"))
        tb.append(P.dma("sy", "d_c", lq))
        tb.append(P.dma("sy", "d_c", lk))
    tbv = P.dma("sy", "d_c", lambda e: e.dma_start(out=bv_rep[:, :], in_=bv_h.to_broadcast((128, 1024))))
    tgw = P.dma("sy", "d_c", lambda e: e.dma_start(
        out=gnw_hd[:, :], in_=gnw_h.rearrange("o (h d) -> (o h) d", h=HLOC)))
    tgb = P.dma("sy", "d_c", lambda e: e.dma_start(
        out=gnb_hd[:, :], in_=gnb_h.rearrange("o (h d) -> (o h) d", h=HLOC)))

    tSz = P.comp("gp", lambda e: e.memset(Sst[:, :], 0.0))
    tSb0 = P.comp("gp", lambda e: e.memset(Sbf[:, :], 0.0), (tSz,))
    tacc = P.comp("gp", lambda e: e.memset(accS[:, :], 0.0))
    tacc2 = P.comp("gp", lambda e: e.memset(accQ[:, :], 0.0))

    twz = P.comp("gp", lambda e: e.memset(wacc[:, :], 0.0))
    stats_state = {"red": twz, "i": 0}
    stats_blocks = [("gv", i) for i in range(HID2 // 128)] + \
                   [("out", i) for i in range(2 * (D // 128))]

    def emit_stats_some(k):
        for _ in range(k):
            i = stats_state["i"]
            if i >= len(stats_blocks):
                return
            stats_state["i"] = i + 1
            kind, j = stats_blocks[i]
            b = i % 2
            if kind == "gv":
                tl = P.dma("sc", "d_st%d" % b, lambda e, j=j, b=b: e.dma_start(
                    out=wsb[b][:, 0:D], in_=wgv_hf[j * 128:(j + 1) * 128, :]),
                    deps=bd(("wsb", b)))
                col = 0
                wdt = D
            else:
                ri, cj = j // 2, j % 2
                tl = P.dma("sc", "d_st%d" % b, lambda e, ri=ri, cj=cj, b=b: e.dma_start(
                    out=wsb[b][:, 0:2048], in_=wout_hf[ri * 128:(ri + 1) * 128, cj * 2048:(cj + 1) * 2048]),
                    deps=bd(("wsb", b)))
                col = 1
                wdt = 2048
            tr = P.comp("ve", lambda e, b=b, col=col, wdt=wdt: e.tensor_reduce(
                wred[:, col:col + 1], wsb[b][:, 0:wdt].rearrange("p (o f) -> p o f", o=1),
                AX.X, AL.add, apply_absolute_value=True), (tl, stats_state["red"]))
            setb(("wsb", b), tr)
            stats_state["red"] = P.comp("ve", lambda e, col=col: e.tensor_tensor(
                wacc[:, col:col + 1], wacc[:, col:col + 1], wred[:, col:col + 1], AL.add), (tr, twz))

    def finish_stats():
        emit_stats_some(len(stats_blocks))
        twt = P.comp("te", lambda e: e.transpose(ps[2][0:2, 0:128], wacc[:, :], idf[:, :]),
                     (stats_state["red"], tidf) + pd(2))
        twe = P.comp("ve", lambda e: e.tensor_reduce(
            wscol[0:2, 0:1], ps[2][0:2, 0:128].rearrange("p (o f) -> p o f", o=1), AX.X, AL.add), (twt,))
        setp(2, twe)
        tst = branch_dma("d_sh",
                         lambda e: e.dma_start(out=sh_stat_e.rearrange("o p -> p o"), in_=wscol[0:2, 0:1]),
                         lambda e: e.dma_start(out=sh_stat_o.rearrange("o p -> p o"), in_=wscol[0:2, 0:1]),
                         deps=(twe,))
        return set_flag(0, deps=(tst,))

    def emit_ln0(st):
        evacs = []
        for tt in range(8):
            b = tt % 2
            row = st * SUBT + tt * 128
            tl = P.dma("sy", "d_x%d" % b, lambda e, row=row, b=b: e.dma_start(
                out=xbuf[b][:, :], in_=x_full[row:row + 128, :]), deps=bd(("x", b)))
            trs, tsq = emit_ln_stats(xbuf[b][:, :], D, (tl,))
            tap = P.comp("ve", lambda e, b=b: e.tensor_scalar(
                ln0tm[b][:, :], xbuf[b][:, :], col_e[:, :], col_b[:, :],
                AL.subtract, AL.mult), (trs,) + bd(("ln0tm", b)))
            cols["g"] = tap
            setb(("x", b), tap, tsq)
            last_tr = None
            for kc in range(16):
                bank = 6 + (kc % 2)
                pb = ps[bank]
                ttr = P.comp("te", lambda e, b=b, kc=kc, pb=pb: e.transpose(
                    pb.bitcast(BF16)[:, 0:128], ln0tm[b][:, kc * 128:(kc + 1) * 128], ident[:, :]),
                    (tap, tid) + pd(bank) + bd("ln0f"))
                tev = P.comp("sc", lambda e, kc=kc, tt=tt, pb=pb: e.copy(
                    ln0f[:, kc * SUBT + tt * 128: kc * SUBT + (tt + 1) * 128],
                    pb.bitcast(BF16)[:, 0:128]), (ttr,))
                setp(bank, tev)
                last_tr = ttr
                evacs.append(tev)
            setb(("ln0tm", b), last_tr)
            if st > 0:
                emit_stats_some(3)
        tsp = P.dma("gp", "d_spw", lambda e, st=st: e.dma_start(
            out=ln0f_sp.rearrange("(c p) t -> p c t", p=128)[:, :, st * SUBT:(st + 1) * SUBT],
            in_=ln0f.rearrange("p (c t) -> p c t", c=16)), deps=tuple(evacs))
        return evacs, tsp

    def emit_proj(st, ln_evacs, tsp):
        lasts = []
        for wmat, is_q in ((wk_h, False), (wq_h, True)):
            for fb in range(HLOC):
                b = fb % 2
                tl = P.dma("gp", "d_wk%d" % b, lambda e, wmat=wmat, fb=fb, b=b: e.dma_start(
                    out=wkqb[b].rearrange("p (c f) -> p c f", c=16),
                    in_=wmat[:, fb * 128:(fb + 1) * 128].rearrange("(c p) f -> p c f", p=128)),
                    deps=bd(("wkq", b)))
                for tt2 in range(2):
                    pb = ps[tt2]
                    mlast = None
                    for kc in range(16):
                        def mm(e, fb=fb, b=b, kc=kc, tt2=tt2, pb=pb):
                            return e.matmul(pb[:, :],
                                            wkqb[b][:, kc * 128:(kc + 1) * 128],
                                            ln0f[:, kc * SUBT + tt2 * 512: kc * SUBT + (tt2 + 1) * 512],
                                            start=(kc == 0), stop=(kc == 15))
                        d = (tl,) + pd(tt2) + tuple(ln_evacs) if kc == 0 else (tl,)
                        mlast = P.comp("te", mm, d)
                    co = fb * SUBT + tt2 * 512
                    if not is_q:
                        tev = P.comp("sc", lambda e, fb=fb, co=co, pb=pb: e.add(
                            kf[:, co:co + 512], pb[:, :], bkc[:, fb:fb + 1]),
                            (mlast,) + tuple(tb) + bd("kf"))
                    else:
                        te1 = P.comp("sc", lambda e, fb=fb, pb=pb: e.add(
                            qtmp[:, :], pb[:, :], bqc[:, fb:fb + 1]),
                            (mlast,) + tuple(tb) + bd("qtmp"))
                        tev = te1
                        for j4 in range(4):
                            tev = P.comp("ve", lambda e, fb=fb, co=co, j4=j4: e.tensor_tensor(
                                qdec[:, co + j4 * 128: co + (j4 + 1) * 128],
                                qtmp[:, j4 * 128:(j4 + 1) * 128],
                                dqt[:, fb * 128:(fb + 1) * 128], AL.mult),
                                (te1,) + tuple(tdq) + bd("qdec"))
                        setb("qtmp", tev)
                    setp(tt2, tev)
                    lasts.append(tev)
                setb(("wkq", b), mlast)
        for ob in range(4):
            b = ob % 2
            tl = P.dma("gp", "d_wv%d" % b, lambda e, ob=ob, b=b: e.dma_start(
                out=wvb[b].rearrange("p (c f) -> p c f", c=16),
                in_=wv_h[:, ob * 256:(ob + 1) * 256].rearrange("(c p) f -> p c f", p=128)),
                deps=bd(("wv", b)))
            for tt3 in range(8):
                bank = 2 + tt3 % 2
                pb = ps[bank]
                mlast = None
                for kc in range(16):
                    def mm(e, ob=ob, b=b, kc=kc, tt3=tt3, pb=pb):
                        return e.matmul(pb[:, 0:256],
                                        ln0f[:, kc * SUBT + tt3 * 128: kc * SUBT + (tt3 + 1) * 128],
                                        wvb[b][:, kc * 256:(kc + 1) * 256],
                                        start=(kc == 0), stop=(kc == 15))
                    d = (tl,) + pd(bank) + tuple(ln_evacs) if kc == 0 else (tl,)
                    mlast = P.comp("te", mm, d)
                tev = P.comp("ve", lambda e, ob=ob, tt3=tt3, pb=pb: e.tensor_tensor(
                    vtm[:, tt3 * 1024 + ob * 256: tt3 * 1024 + (ob + 1) * 256],
                    pb[:, 0:256], bv_rep[:, ob * 256:(ob + 1) * 256], AL.add),
                    (mlast, tbv) + bd("vtm"))
                setp(bank, tev)
                lasts.append(tev)
            setb(("wv", b), mlast)
        return lasts

    ROT = (4, 5, 0, 1)

    def emit_chunks(st, dep_toks):
        for c in range(8):
            tcast = P.comp("sc", lambda e: e.copy(Sbf[:, :], Sst[:, :]),
                           tuple(dep_toks) + bd("Sbf"))
            ret_last = {}
            for half in range(2):
                tscs = {}
                for h in range(half * 4, half * 4 + 4):
                    bank = ROT[h % 4]
                    pb = ps[bank]
                    kcol = h * SUBT + c * 128
                    tsc = P.comp("te", lambda e, kcol=kcol, pb=pb: e.matmul(
                        pb[:, 0:128], kf[:, kcol:kcol + 128], qdec[:, kcol:kcol + 128],
                        start=True, stop=True), pd(bank) + (tcast,))
                    tscs[h] = tsc
                tmks = {}
                for h in range(half * 4, half * 4 + 4):
                    bank = ROT[h % 4]
                    pb = ps[bank]
                    tmk = P.comp("ve", lambda e, h=h, pb=pb: e.scalar_tensor_tensor(
                        scbf[h % 4][:, :], pb[:, 0:128], minv[:, h:h + 1], tri[:, :],
                        AL.mult, AL.mult),
                        (tscs[h], ttri) + tuple(tmi) + bd(("scbf", h % 4)))
                    setp(bank, tmk)
                    tmks[h] = tmk
                for h in range(half * 4, half * 4 + 4):
                    rb = ps[2 + h // 4]
                    rcol = (h % 4) * 128
                    kcol = h * SUBT + c * 128
                    vcol = c * 1024 + h * 128
                    d1 = (tmks[h],) + (pd(2 + h // 4) if h % 4 == 0 else ())
                    tr1 = P.comp("te", lambda e, vcol=vcol, rb=rb, rcol=rcol, h=h: e.matmul(
                        rb[:, rcol:rcol + 128], scbf[h % 4][:, :], vtm[:, vcol:vcol + 128],
                        start=True, stop=False), d1)
                    tr2 = P.comp("te", lambda e, kcol=kcol, rb=rb, rcol=rcol, h=h: e.matmul(
                        rb[:, rcol:rcol + 128], qdec[:, kcol:kcol + 128], Sbf[:, h * 128:(h + 1) * 128],
                        start=False, stop=True), (tr1,))
                    setb(("scbf", h % 4), tr1)
                    ret_last[h // 4] = tr2
            setb("Sbf", ret_last[0], ret_last[1])
            sp_toks = []
            for bnk in range(2):
                rb = ps[2 + bnk]
                rsc = retsc[bnk]
                tev = P.comp("ve", lambda e, rb=rb, rsc=rsc: e.tensor_copy(rsc[:, :], rb[:, :]),
                             (ret_last[bnk],) + bd(("retsc", bnk)))
                setp(2 + bnk, tev)
                tS1 = P.comp("ve", lambda e, rsc=rsc: e.tensor_reduce(
                    tmp4[:, 0:4], rsc[:, :].rearrange("p (h f) -> p h f", h=4), AX.X, AL.add), (tev,))
                tS2 = P.comp("ve", lambda e, bnk=bnk: e.tensor_tensor(
                    accS[:, bnk * 4:bnk * 4 + 4], accS[:, bnk * 4:bnk * 4 + 4], tmp4[:, 0:4], AL.add),
                    (tS1, tacc))
                tq1 = P.comp("sc", lambda e, rsc=rsc: e.square(sqsc[:, :], rsc[:, :]),
                             (tev, tS1) + bd("sqsc"))
                tq2 = P.comp("ve", lambda e: e.tensor_reduce(
                    tmp4[:, 4:8], sqsc[:, :].rearrange("p (h f) -> p h f", h=4), AX.X, AL.add),
                    (tq1, tS2))
                setb("sqsc", tq2)
                tq3 = P.comp("ve", lambda e, bnk=bnk: e.tensor_tensor(
                    accQ[:, bnk * 4:bnk * 4 + 4], accQ[:, bnk * 4:bnk * 4 + 4], tmp4[:, 4:8], AL.add),
                    (tq2, tacc2))
                tsp = P.dma("gp", "d_spw", lambda e, st=st, c=c, bnk=bnk, rsc=rsc: e.dma_start(
                    out=ret_sp[st * SUBT + c * 128: st * SUBT + (c + 1) * 128, bnk * 512:(bnk + 1) * 512],
                    in_=rsc[:, :]), deps=(tq1,))
                setb(("retsc", bnk), tsp, tq2)
                sp_toks += [tq3, tsp]
            ztoks = []
            for half in range(2):
                tkds = {}
                for h in range(half * 4, half * 4 + 4):
                    bank = ROT[h % 4]
                    pb = ps[bank]
                    kcol = h * SUBT + c * 128
                    ttr = P.comp("te", lambda e, kcol=kcol, pb=pb: e.transpose(
                        pb.bitcast(BF16)[:, 0:128], kf[:, kcol:kcol + 128], ident[:, :]),
                        pd(bank) + (tid,))
                    tkd = P.comp("ve", lambda e, h=h, pb=pb: e.tensor_scalar(
                        ktd[h % 4][:, :], pb.bitcast(BF16)[:, 0:128], decS[:, h:h + 1], None, AL.mult),
                        (ttr,) + tuple(tinv) + bd(("ktd", h % 4)))
                    setp(bank, tkd)
                    tkds[h] = tkd
                for h in range(half * 4, half * 4 + 4):
                    zb = ps[6 + h // 4]
                    zcol = (h % 4) * 128
                    vcol = c * 1024 + h * 128
                    d1 = (tkds[h],) + (pd(6 + h // 4) if h % 4 == 0 else ())
                    tz = P.comp("te", lambda e, h=h, vcol=vcol, zb=zb, zcol=zcol: e.matmul(
                        zb[:, zcol:zcol + 128], ktd[h % 4][:, :], vtm[:, vcol:vcol + 128],
                        start=True, stop=True), d1)
                    setb(("ktd", h % 4), tz)
                    ztoks.append(tz)
            tS = tcast
            for h in range(HLOC):
                tS = P.comp("gp", lambda e, h=h: e.tensor_scalar(
                    Sst[:, h * 128:(h + 1) * 128], Sst[:, h * 128:(h + 1) * 128],
                    gc_cols[:, h:h + 1], None, AL.mult), (tS, tgc))
            tS = P.comp("ve", lambda e: e.tensor_tensor(
                Sst[:, 0:512], Sst[:, 0:512], ps[6][:, :], AL.add), (tS,) + tuple(ztoks))
            tS2b = P.comp("ve", lambda e: e.tensor_tensor(
                Sst[:, 512:1024], Sst[:, 512:1024], ps[7][:, :], AL.add), (tS,))
            setp(6, tS)
            setp(7, tS2b)
            dep_toks = [tS2b] + sp_toks
            if st > 0:
                emit_stats_some(2)
        return dep_toks

    tf1 = None
    adeps = []
    for st in range(NSUB):
        evs, tsps = emit_ln0(st)
        pls = emit_proj(st, evs, tsps)
        setb("ln0f", pls[-1], tsps)
        if st == 0:
            adeps = emit_chunks(0, pls[-3:] + [tSz, tSb0])
        else:
            adeps = emit_chunks(st, pls[-3:] + adeps)
        setb("kf", adeps[0])
        setb("qdec", adeps[0])
        setb("vtm", adeps[0])
        if st == 2:
            tf1 = finish_stats()

    tp1 = poll(0, deps=(tf1,))
    tr1 = P.dma("sy", "d_c", lambda e: e.dma_start(out=wst2[0:1, 0:2], in_=sh_stat_e), (tp1,))
    tr2 = P.dma("sy", "d_c", lambda e: e.dma_start(out=wst2[0:1, 2:4], in_=sh_stat_o), (tp1,))
    tws = P.comp("ve", lambda e: e.tensor_tensor(wst2[0:1, 0:2], wst2[0:1, 0:2], wst2[0:1, 2:4], AL.add),
                 (tr1, tr2))
    tws = P.comp("ve", lambda e: e.tensor_scalar(
        wst2[0:1, 0:1], wst2[0:1, 0:1], 1.0 / (16384.0 * 2048.0), EPS, AL.mult, AL.max), (tws,))
    tws = P.comp("ve", lambda e: e.tensor_scalar(
        wst2[0:1, 1:2], wst2[0:1, 1:2], 1.0 / (2048.0 * 8192.0), EPS, AL.mult, AL.max), (tws,))
    tws2 = P.comp("ve", lambda e: e.reciprocal(wst2[0:1, 2:4], wst2[0:1, 0:2]), (tws,))
    twm1 = P.dma("gp", "d_c", lambda e: e.dma_start(out=scr_wm[0:1, 0:2], in_=wst2[0:1, 0:2]), (tws,))
    twm2 = P.dma("gp", "d_c", lambda e: e.dma_start(out=scr_wm[0:1, 2:4], in_=wst2[0:1, 2:4]), (tws2,))
    twmc = P.dma("sy", "d_c", lambda e: e.dma_start(
        out=wmcols[:, :], in_=scr_wm[0:1, 0:4].to_broadcast((128, 4))), (twm1, twm2))

    tt1g = P.comp("te", lambda e: e.transpose(ps[0][0:HLOC, 0:128], accS[:, :], idf[:, :]),
                  tuple(adeps) + (tidf,) + pd(0))
    tg1 = P.comp("ve", lambda e: e.tensor_reduce(
        gsum[:, 0:1], ps[0][0:HLOC, 0:128].rearrange("p (o f) -> p o f", o=1), AX.X, AL.add), (tt1g,))
    setp(0, tg1)
    tt2g = P.comp("te", lambda e: e.transpose(ps[1][0:HLOC, 0:128], accQ[:, :], idf[:, :]),
                  (tg1,) + tuple(adeps) + pd(1))
    tg2 = P.comp("ve", lambda e: e.tensor_reduce(
        gsum[:, 1:2], ps[1][0:HLOC, 0:128].rearrange("p (o f) -> p o f", o=1), AX.X, AL.add), (tt2g,))
    setp(1, tg2)
    NTOT = float(S * 128)
    tg3 = P.comp("ve", lambda e: e.tensor_scalar(gmrs[:, 0:2], gsum[:, 0:2], 1.0 / NTOT, None, AL.mult),
                 (tg2,))
    tg4 = P.comp("ve", lambda e: e.tensor_tensor(gmrs[:, 2:3], gmrs[:, 0:1], gmrs[:, 0:1], AL.mult), (tg3,))
    tg5 = P.comp("ve", lambda e: e.tensor_tensor(gmrs[:, 2:3], gmrs[:, 1:2], gmrs[:, 2:3], AL.subtract), (tg4,))
    tg6 = P.comp("ve", lambda e: e.tensor_scalar(gmrs[:, 2:3], gmrs[:, 2:3], EPS, None, AL.add), (tg5,))
    tg7 = emit_rsqrt(gmrs[:, 3:4], gmrs[:, 2:3], gsum[:, 0:1], (tg6,))
    tg8 = P.comp("ve", lambda e: e.tensor_scalar(a_hd[:, :], gnw_hd[:, :], gmrs[:, 3:4], None, AL.mult),
                 (tg7, tgw))
    tg9 = P.comp("ve", lambda e: e.tensor_scalar(b_hd[:, :], a_hd[:, :], gmrs[:, 0:1], None, AL.mult), (tg8,))
    tg9 = P.comp("ve", lambda e: e.tensor_tensor(b_hd[:, :], gnb_hd[:, :], b_hd[:, :], AL.subtract), (tg9, tgb))
    tga = P.dma("gp", "d_c", lambda e: e.dma_start(
        out=scr_gn[0:1, :].rearrange("o (h d) -> (o h) d", h=HLOC), in_=a_hd[:, :]), (tg8,))
    tgb2 = P.dma("gp", "d_c", lambda e: e.dma_start(
        out=scr_gn[1:2, :].rearrange("o (h d) -> (o h) d", h=HLOC), in_=b_hd[:, :]), (tg9,))

    if DBG:
        P.dma("sy", "d_tap", lambda e: e.dma_start(out=taps["tap_ret"], in_=ret_sp),
              deps=tuple(adeps) + (P.tok("d_spw"),))
        P.dma("sy", "d_tap", lambda e: e.dma_start(out=taps["tap_ln0f"], in_=ln0f_sp),
              deps=(P.tok("d_spw"),))
        P.dma("sy", "d_tap", lambda e: e.dma_start(out=taps["tap_gn"], in_=scr_gn), deps=(tga, tgb2))

    aphase_end = [P.tok("e_ve"), P.tok("e_te"), P.tok("e_sc"), P.tok("e_gp"),
                  P.tok("d_spw"), tga, tgb2]

    def make_quant_emitters(blocks, src, stage, sc_idx, store, out_toks):
        ems = []
        for bi, (r0, c0, ncols) in enumerate(blocks):
            b = bi % 2

            def em(r0=r0, c0=c0, ncols=ncols, b=b, src=src, stage=stage, sc_idx=sc_idx,
                   store=store, out_toks=out_toks):
                tl = P.dma("sc", "d_q%d" % b, lambda e, r0=r0, c0=c0, ncols=ncols, b=b, src=src: e.dma_start(
                    out=wsb[b][:, 0:ncols], in_=src[r0:r0 + 128, c0:c0 + ncols]),
                    deps=bd(("wsb", b)))
                t1 = P.comp("gp", lambda e, b=b, ncols=ncols, sc_idx=sc_idx: e.tensor_scalar(
                    wsb[b][:, 0:ncols], wsb[b][:, 0:ncols], wmcols[:, 2 + sc_idx:3 + sc_idx], MAGIC,
                    AL.mult, AL.add), (tl, twmc))
                t2 = P.comp("ve", lambda e, b=b, ncols=ncols: e.tensor_scalar(
                    wqb_[:, 0:ncols], wsb[b][:, 0:ncols], MAGIC, 1.0, AL.subtract, AL.min), (t1,))
                setb(("wsb", b), t2)
                t3 = P.comp("ve", lambda e, b=b, ncols=ncols: e.tensor_scalar(
                    wqo[b][:, 0:ncols], wqb_[:, 0:ncols], -1.0, None, AL.max),
                    (t2,) + bd(("wqo", b)))
                if store is None:
                    ts = P.dma("gp", "d_q2%d" % b, lambda e, r0=r0, c0=c0, ncols=ncols, b=b, stage=stage: e.dma_start(
                        out=stage[r0:r0 + 128, c0:c0 + ncols], in_=wqo[b][:, 0:ncols]), deps=(t3,))
                else:
                    ts = store(r0, c0, ncols, b, t3)
                setb(("wqo", b), ts)
                out_toks.append(ts)
            ems.append(em)
        return ems

    def gv_store(r0, c0, ncols, b, t3):
        return branch_dma_gp(
            "d_q2%d" % b,
            lambda e, r0=r0, ncols=ncols, b=b: e.dma_start(
                out=sh_wgv[r0:r0 + 128, 0:ncols], in_=wqo[b][:, 0:ncols]),
            lambda e, r0=r0, ncols=ncols, b=b: e.dma_start(
                out=sh_wgv[HID2 + r0:HID2 + r0 + 128, 0:ncols], in_=wqo[b][:, 0:ncols]),
            deps=(t3,))

    def wout_store(r0, c0, ncols, b, t3):
        return branch_dma_gp(
            "d_q2%d" % b,
            lambda e, r0=r0, c0=c0, ncols=ncols, b=b: e.dma_start(
                out=sh_wout[r0:r0 + 128, c0:c0 + ncols], in_=wqo[b][:, 0:ncols]),
            lambda e, r0=r0, c0=c0, ncols=ncols, b=b: e.dma_start(
                out=sh_wout[r0:r0 + 128, 4096 + c0:4096 + c0 + ncols], in_=wqo[b][:, 0:ncols]),
            deps=(t3,))

    tq_gv = []
    tq_out = []
    p1_emitters = (
        make_quant_emitters([(i * 128, 0, D) for i in range(HID2 // 128)], wgv_hf, None, 0,
                            gv_store, tq_gv)
        + make_quant_emitters([(ri * 128, cj * 2048, 2048) for ri in range(D // 128) for cj in range(2)],
                              wout_hf, None, 1, wout_store, tq_out))

    close_scope(markA)

    markF = len(cms)
    a_rep = sb("a_rep", [128, 1024], F32)
    b_rep = sb("b_rep", [128, 1024], F32)
    bg_rep2 = sb("bg_rep2", [128, 1024], F32)
    wgb = sb("wgb", [128, 16 * 1024], BF16)
    wob = sb("wob", [128, 8 * 2048], BF16)
    lnfb = [sb("lnfb%d" % i, [128, 16 * 256], BF16) for i in range(2)]
    retb = [sb("retb%d" % i, [128, 1024], F32) for i in range(2)]
    rof = sb("rof", [128, 1024], F32)
    ro_bf = [sb("ro_bf%d" % i, [128, 1024], BF16) for i in range(2)]
    gtmp = sb("gtmp", [128, 512], F32)
    g_bf = sb("g_bf", [128, 1024], BF16)
    gro = [sb("gro%d" % i, [128, 1024], BF16) for i in range(2)]
    grof = [sb("grof%d" % i, [128, 8 * 128], BF16) for i in range(2)]
    ysc = [sb("ysc%d" % i, [128, D], F32) for i in range(2)]

    tar = P.dma("sy", "d_c", lambda e: e.dma_start(
        out=a_rep[:, :], in_=scr_gn[0:1, :].to_broadcast((128, 1024))), (tga, tgb2))
    tbr = P.dma("sy", "d_c", lambda e: e.dma_start(
        out=b_rep[:, :], in_=scr_gn[1:2, :].to_broadcast((128, 1024))), (tga, tgb2))
    tbg2 = P.dma("sy", "d_c", lambda e: e.dma_start(
        out=bg_rep2[:, :], in_=bg_h.to_broadcast((128, 1024))))
    twg = P.dma("gp", "d_wg", lambda e: e.dma_start(
        out=wgb.rearrange("p (c f) -> p c f", c=16),
        in_=wg_h.rearrange("(c p) f -> p c f", p=128)), tuple(aphase_end))
    two = P.dma("gp", "d_wg", lambda e: e.dma_start(
        out=wob.rearrange("p (c f) -> p c f", c=8),
        in_=wo_h.rearrange("(c p) f -> p c f", p=128)), tuple(aphase_end))

    ypart_toks = []
    tlf = None
    for f in range(32):
        b = f % 2
        trr = P.dma("sy", "d_rt%d" % b, lambda e, f=f, b=b: e.dma_start(
            out=retb[b][:, :], in_=ret_sp[f * 128:(f + 1) * 128, :]),
            deps=bd(("retb", b)) + tuple(aphase_end))
        if f % 2 == 0:
            lslot = (f // 2) % 2
            tlf = P.dma("sy", "d_lf%d" % lslot, lambda e, f=f, lslot=lslot: e.dma_start(
                out=lnfb[lslot].rearrange("p (c t) -> p c t", c=16),
                in_=ln0f_sp.rearrange("(c p) t -> p c t", p=128)[:, :, f * 128:(f + 2) * 128]),
                deps=bd(("lnfb", lslot)) + tuple(aphase_end))
        lb = lnfb[(f // 2) % 2]
        lofs = (f % 2) * 128
        t1 = P.comp("ve", lambda e, b=b: e.tensor_tensor(rof[:, :], retb[b][:, :], a_rep[:, :], AL.mult),
                    (trr, tar) + bd("rof"))
        t2 = P.comp("ve", lambda e, b=b: e.tensor_tensor(ro_bf[b][:, :], rof[:, :], b_rep[:, :], AL.add),
                    (t1, tbr) + bd(("ro_bf", b)))
        setb("rof", t2)
        setb(("retb", b), t1)
        gts = []
        for ob in range(2):
            pb = ps[ob]
            mlast = None
            for kc in range(16):
                def mm(e, kc=kc, ob=ob, pb=pb, lb=lb, lofs=lofs):
                    return e.matmul(pb[:, :], lb[:, kc * 256 + lofs: kc * 256 + lofs + 128],
                                    wgb[:, kc * 1024 + ob * 512: kc * 1024 + (ob + 1) * 512],
                                    start=(kc == 0), stop=(kc == 15))
                d = (tlf, twg) + pd(ob) if kc == 0 else ()
                mlast = P.comp("te", mm, d)
            t3 = P.comp("ve", lambda e, ob=ob, pb=pb: e.tensor_tensor(
                gtmp[:, :], pb[:, :], bg_rep2[:, ob * 512:(ob + 1) * 512], AL.add),
                (mlast, tbg2) + bd("gtmp"))
            t4 = P.comp("sc", lambda e, ob=ob: e.activation(
                g_bf[:, ob * 512:(ob + 1) * 512], gtmp[:, :], AF.Silu), (t3,) + bd("g_bf"))
            setb("gtmp", t4)
            setp(ob, t4)
            gts.append(t4)
        t5 = P.comp("ve", lambda e, b=b: e.tensor_tensor(gro[b][:, :], g_bf[:, :], ro_bf[b][:, :], AL.mult),
                    (t2,) + tuple(gts) + bd(("gro", b)))
        setb("g_bf", t5)
        setb(("ro_bf", b), t5)
        gtr = []
        for kc in range(8):
            bank = 2 + kc % 2
            pb = ps[bank]
            ttr = P.comp("te", lambda e, kc=kc, pb=pb, b=b: e.transpose(
                pb.bitcast(BF16)[:, 0:128], gro[b][:, kc * 128:(kc + 1) * 128], ident[:, :]),
                (t5,) + pd(bank))
            tev = P.comp("sc", lambda e, kc=kc, b=b, pb=pb: e.copy(
                grof[b][:, kc * 128:(kc + 1) * 128], pb.bitcast(BF16)[:, 0:128]),
                (ttr,) + bd(("grof", b)))
            setp(bank, tev)
            gtr.append(tev)
        setb(("gro", b), gtr[-1])
        yts = []
        mlast = None
        for obq in range(4):
            bank = 4 + obq
            pb = ps[bank]
            for kc in range(8):
                def mm(e, kc=kc, obq=obq, b=b, pb=pb):
                    return e.matmul(pb[:, :], grof[b][:, kc * 128:(kc + 1) * 128],
                                    wob[:, kc * 2048 + obq * 512: kc * 2048 + (obq + 1) * 512],
                                    start=(kc == 0), stop=(kc == 7))
                d = tuple(gtr) + (two,) + pd(bank) if kc == 0 else ()
                mlast = P.comp("te", mm, d)
            tev = P.comp("ve", lambda e, obq=obq, b=b, pb=pb: e.tensor_copy(
                ysc[b][:, obq * 512:(obq + 1) * 512], pb[:, :]), (mlast,) + bd(("ysc", b)))
            setp(bank, tev)
            yts.append(tev)
        setb(("grof", b), mlast)
        tyw = branch_dma("d_yw",
                         lambda e, f=f, b=b: e.dma_start(out=sh_ye[f * 128:(f + 1) * 128, :], in_=ysc[b][:, :]),
                         lambda e, f=f, b=b: e.dma_start(out=sh_yo[f * 128:(f + 1) * 128, :], in_=ysc[b][:, :]),
                         deps=tuple(yts))
        setb(("ysc", b), tyw)
        setb(("lnfb", (f // 2) % 2), mlast)
        ypart_toks.append(tyw)
        for _ in range(3):
            if p1_emitters:
                p1_emitters.pop(0)()
        if f == 21:
            tf2 = set_flag(2, deps=tuple(tq_gv))

    while p1_emitters:
        p1_emitters.pop(0)()

    tfy = set_flag(4, deps=tuple(ypart_toks))
    tpy = poll(4, deps=(tfy,))
    tf6 = set_flag(6, deps=tuple(tq_out))
    if DBG:
        P.dma("sy", "d_tap", lambda e: e.dma_start(out=taps["tap_y"], in_=sh_ye), deps=(tpy,))

    fphase_end = [P.tok("e_ve"), P.tok("e_te"), P.tok("e_sc"), tpy]
    close_scope(markF)

    markG = len(cms)
    rqf = sb("rqf", [128, 16 * T], BF16)
    markG2 = len(cms)
    bo_rep = sb("bo_rep", [128, D], F32)
    wn_rep = sb("wn_rep", [128, D], F32)
    yab = [sb("yab%d" % i, [128, D], F32) for i in range(2)]
    xhb = [sb("xhb%d" % i, [128, D], F32) for i in range(2)]
    x1b = [sb("x1b%d" % i, [128, D], F32) for i in range(2)]
    h2b = sb("h2b", [128, D], F32)
    q1b = [sb("q1b%d" % i, [128, D], BF16) for i in range(2)]

    tbo = P.dma("sy", "d_c", lambda e: e.dma_start(
        out=bo_rep[:, :], in_=bo_f.to_broadcast((128, D))), tuple(fphase_end))
    twn = P.dma("sy", "d_c", lambda e: e.dma_start(
        out=wn_rep[:, :], in_=wnorm_f.to_broadcast((128, D))), tuple(fphase_end))

    gi_toks = []
    for f in range(16):
        b = f % 2
        tya = branch_dma_gp("d_yg%d" % b,
                            lambda e, f=f, b=b: e.dma_start(
                                out=yab[b][:, :], in_=sh_ye[f * 128:(f + 1) * 128, :]),
                            lambda e, f=f, b=b: e.dma_start(
                                out=yab[b][:, :], in_=sh_ye[T + f * 128:T + (f + 1) * 128, :]),
                            deps=(tpy,) + bd(("yab", b)))
        tyb = branch_dma_gp("d_yg%d" % b,
                            lambda e, f=f, b=b: e.dma_start(
                                out=yab[b][:, :], in_=sh_yo[f * 128:(f + 1) * 128, :], accum_op=AL.add),
                            lambda e, f=f, b=b: e.dma_start(
                                out=yab[b][:, :], in_=sh_yo[T + f * 128:T + (f + 1) * 128, :], accum_op=AL.add),
                            deps=(tya,))
        txh = P.dma("sy", "d_xh%d" % b, lambda e, f=f, b=b: e.dma_start(
            out=xhb[b][:, :], in_=x_half[f * 128:(f + 1) * 128, :]),
            deps=bd(("xhb", b)) + tuple(fphase_end))
        t2 = P.comp("ve", lambda e, b=b: e.tensor_tensor(yab[b][:, :], yab[b][:, :], bo_rep[:, :], AL.add),
                    (tyb, tbo))
        trs, _ = emit_ln_stats(yab[b][:, :], D, (t2,))
        t6 = P.comp("ve", lambda e, b=b: e.tensor_scalar(
            yab[b][:, :], yab[b][:, :], col_e[:, :], col_b[:, :], AL.subtract, AL.mult), (trs,))
        t7 = P.comp("ve", lambda e, b=b: e.tensor_tensor(x1b[b][:, :], xhb[b][:, :], yab[b][:, :], AL.add),
                    (t6, txh) + bd(("x1b", b)))
        setb(("yab", b), t7)
        setb(("xhb", b), t7)
        tx1 = P.dma("gp", "d_x1w", lambda e, f=f, b=b: e.dma_start(
            out=x1_sp[f * 128:(f + 1) * 128, :], in_=x1b[b][:, :]), deps=(t7,))
        trs2, _ = emit_ln_stats(x1b[b][:, :], D, (t7,))
        t9 = P.comp("ve", lambda e, f=f: e.tensor_copy(m2c[:, f:f + 1], col_e[:, :]), (trs2,))
        t12 = P.comp("ve", lambda e, f=f: e.tensor_copy(rs2c[:, f:f + 1], col_b[:, :]), (trs2,))
        t13 = P.comp("ve", lambda e: e.reciprocal(col_c[:, :], col_d[:, :]), (trs2,))
        t14 = P.comp("ve", lambda e: e.tensor_scalar(col_d[:, :], col_d[:, :], -EPS, None, AL.add), (t13,))
        t14 = P.comp("ve", lambda e: e.tensor_tensor(col_c[:, :], col_d[:, :], col_c[:, :], AL.mult), (t14,))
        t15 = P.comp("ve", lambda e: e.tensor_scalar(col_c[:, :], col_c[:, :], EPS, None, AL.add), (t14,))
        t16 = emit_rsqrt(col_f, col_c, col_a, (t15,))
        t17 = P.comp("ve", lambda e, f=f: e.tensor_tensor(srsc[:, f:f + 1], rs2c[:, f:f + 1], col_f[:, :],
                                                          AL.mult), (t16, t12))
        t18 = P.comp("ve", lambda e, b=b, f=f: e.tensor_scalar(
            h2b[:, :], x1b[b][:, :], m2c[:, f:f + 1], srsc[:, f:f + 1], AL.subtract, AL.mult),
            (t17, t9) + bd("h2b"))
        setb(("x1b", b), t18, tx1)
        t19 = P.comp("ve", lambda e: e.tensor_tensor(h2b[:, :], h2b[:, :], wn_rep[:, :], AL.mult), (t18, twn))
        t20 = P.comp("ve", lambda e: e.tensor_reduce(
            col_a[:, :], h2b[:, :].rearrange("p (o f) -> p o f", o=1), AX.X, AL.max,
            apply_absolute_value=True), (t19,))
        t21 = P.comp("ve", lambda e: e.tensor_scalar(col_a[:, :], col_a[:, :], EPS, None, AL.max), (t20,))
        t22 = P.comp("ve", lambda e: e.reciprocal(col_b[:, :], col_a[:, :]), (t21,))
        t23 = P.comp("ve", lambda e: e.tensor_scalar(col_b[:, :], col_b[:, :], QB, None, AL.mult), (t22,))
        t24 = P.comp("ve", lambda e: e.tensor_scalar(h2b[:, :], h2b[:, :], col_b[:, :], MAGIC,
                                                     AL.mult, AL.add), (t23,))
        t25 = P.comp("ve", lambda e, b=b: e.tensor_scalar(q1b[b][:, :], h2b[:, :], MAGIC, None, AL.subtract),
                     (t24,) + bd(("q1b", b)))
        setb("h2b", t25)
        t26 = P.comp("ve", lambda e, f=f: e.tensor_scalar(
            scgvc[:, f:f + 1], col_a[:, :], 1.0 / QB, wmcols[:, 0:1], AL.mult, AL.mult), (t21, twmc))
        cols["g"] = t26
        trql = None
        for kc in range(16):
            bank = kc % 2
            pb = ps[bank]
            ttr = P.comp("te", lambda e, kc=kc, b=b, pb=pb: e.transpose(
                pb.bitcast(BF16)[:, 0:128], q1b[b][:, kc * 128:(kc + 1) * 128], ident[:, :]),
                (t25,) + pd(bank))
            tev = P.comp("sc", lambda e, kc=kc, f=f, pb=pb: e.copy(
                rqf[:, kc * T + f * 128: kc * T + (f + 1) * 128], pb.bitcast(BF16)[:, 0:128]), (ttr,))
            setp(bank, tev)
            trql = ttr
            gi_toks.append(tev)
        setb(("q1b", b), trql)
        gi_toks += [tx1, t26]

    if DBG:
        P.dma("sy", "d_tap", lambda e: e.dma_start(out=taps["tap_x1"], in_=x1_sp),
              deps=tuple(gi_toks) + (P.tok("d_x1w"),))
        P.dma("sy", "d_tap", lambda e: e.dma_start(out=taps["tap_wm"], in_=scr_wm), deps=(twmc,))

    tp2 = poll(2, deps=tuple(gi_toks))
    tp2o = poll(6, deps=(tf6,))

    close_scope(markG2)
    wjv = [sb("wjv%d" % i, [128, 16 * 512], BF16) for i in range(2)]
    wjg = [sb("wjg%d" % i, [128, 16 * 512], BF16) for i in range(2)]
    vsc = sb("vsc", [128, 512], F32)
    gsc = sb("gsc", [128, 512], F32)
    ubf = [sb("ubf%d" % i, [128, 512], F32) for i in range(2)]

    j_toks = []
    for qp in range(16):
        b = qp % 2
        tvk = []
        tgk = []
        for kc in range(16):
            tv = P.dma("sc", "d_jv%d" % b, lambda e, qp=qp, b=b, kc=kc: e.dma_start_transpose(
                wjv[b][:, kc * 512:(kc + 1) * 512],
                sh_wgv[qp * 512:(qp + 1) * 512, kc * 128:(kc + 1) * 128]),
                deps=(tp2,) + (bd(("wjv", b)) if kc == 0 else ()))
            tg = P.dma("sc", "d_jg%d" % b, lambda e, qp=qp, b=b, kc=kc: e.dma_start_transpose(
                wjg[b][:, kc * 512:(kc + 1) * 512],
                sh_wgv[HID2 + qp * 512:HID2 + (qp + 1) * 512, kc * 128:(kc + 1) * 128]),
                deps=(tp2,) + (bd(("wjg", b)) if kc == 0 else ()))
            tvk.append(tv)
            tgk.append(tg)
        mlast = None
        for t in range(16):
            ub = t % 2
            pv = ps[4 + 2 * (t % 2)]
            pg = ps[5 + 2 * (t % 2)]
            for kc in range(16):
                def mmv(e, kc=kc, t=t, b=b, pv=pv):
                    return e.matmul(pv[:, :], rqf[:, kc * T + t * 128: kc * T + (t + 1) * 128],
                                    wjv[b][:, kc * 512:(kc + 1) * 512], start=(kc == 0), stop=(kc == 15))

                def mmg(e, kc=kc, t=t, b=b, pg=pg):
                    return e.matmul(pg[:, :], rqf[:, kc * T + t * 128: kc * T + (t + 1) * 128],
                                    wjg[b][:, kc * 512:(kc + 1) * 512], start=(kc == 0), stop=(kc == 15))
                dv = (tvk[kc],) + (pd(4 + 2 * (t % 2)) if kc == 0 else ())
                dg = (tgk[kc],) + (pd(5 + 2 * (t % 2)) if kc == 0 else ())
                P.comp("te", mmv, dv if t == 0 else (dv if kc == 0 else ()))
                mlast = P.comp("te", mmg, dg if t == 0 else (dg if kc == 0 else ()))
            t1 = P.comp("ve", lambda e, t=t, pv=pv: e.tensor_scalar(
                vsc[:, :], pv[:, :], scgvc[:, t:t + 1], None, AL.mult), (mlast,) + bd("vsc"))
            t2 = P.comp("sc", lambda e, t=t, pg=pg: e.activation(
                gsc[:, :], pg[:, :], AF.Silu, scale=scgvc[:, t:t + 1]), (mlast,) + bd("gsc"))
            t3 = P.comp("ve", lambda e, ub=ub: e.tensor_tensor(ubf[ub][:, :], vsc[:, :], gsc[:, :], AL.mult),
                        (t1, t2) + bd(("ubf", ub)))
            setb("vsc", t3)
            setb("gsc", t3)
            setp(4 + 2 * (t % 2), t3)
            setp(5 + 2 * (t % 2), t3)
            tu = P.dma("gp", "d_uw", lambda e, qp=qp, t=t, ub=ub: e.dma_start(
                out=u_sp[t * 128:(t + 1) * 128, qp * 512:(qp + 1) * 512], in_=ubf[ub][:, :]), deps=(t3,))
            setb(("ubf", ub), tu)
            j_toks.append(tu)
        setb(("wjv", b), mlast)
        setb(("wjg", b), mlast)

    if DBG:
        P.dma("sy", "d_tap", lambda e: e.dma_start(out=taps["tap_u"], in_=u_sp),
              deps=tuple(j_toks) + (P.tok("d_uw"),))

    jphase_end = [P.tok("e_ve"), P.tok("e_te"), P.tok("e_sc"), P.tok("d_uw")]
    close_scope(markG)

    markLM = len(cms)
    uqf = sb("uqf", [128, 64 * 512], BF16)
    urb = [sb("urb", [128, HID2], F32)] * 2
    uq_tm = sb("uq_tm", [128, HID2], BF16)
    sc2c = sb("sc2c", [128, 16], F32)
    mo = [sb("mo%d" % i, [128, 1024], F32) for i in range(2)]
    wmt = [sb("wmt%d" % i, [128, 512], BF16) for i in range(6)]
    flb = [sb("flb%d" % i, [128, 512], F32) for i in range(2)]

    m_toks = []
    sc2_toks = [None] * 16
    for sup in range(4):
        l_toks = []
        for t8 in range(4):
            t = sup * 4 + t8
            b = t8 % 2
            tl = P.dma("sy", "d_ur%d" % b, lambda e, t=t, b=b: e.dma_start(
                out=urb[b][:, :], in_=u_sp[t * 128:(t + 1) * 128, :]),
                deps=tuple(jphase_end) + bd("urb"))
            t1 = P.comp("ve", lambda e, b=b: e.tensor_reduce(
                col_a[:, :], urb[b][:, :].rearrange("p (o f) -> p o f", o=1), AX.X, AL.max,
                apply_absolute_value=True), (tl, cols["g"]))
            t2 = P.comp("ve", lambda e: e.tensor_scalar(col_a[:, :], col_a[:, :], EPS, None, AL.max), (t1,))
            t3 = P.comp("ve", lambda e: e.reciprocal(col_b[:, :], col_a[:, :]), (t2,))
            t4 = P.comp("ve", lambda e: e.tensor_scalar(col_b[:, :], col_b[:, :], QB, None, AL.mult), (t3,))
            tqs = []
            tprev = t4
            for seg in range(8):
                sg = mo[seg % 2]
                tm1 = P.comp("ve", lambda e, b=b, seg=seg, sg=sg: e.tensor_scalar(
                    sg[:, 0:1024], urb[b][:, seg * 1024:(seg + 1) * 1024], col_b[:, :], MAGIC,
                    AL.mult, AL.add), (tprev,) + bd(("mo", seg % 2)))
                tq = P.comp("ve", lambda e, seg=seg, sg=sg: e.tensor_scalar(
                    uq_tm[:, seg * 1024:(seg + 1) * 1024], sg[:, 0:1024], MAGIC, None, AL.subtract),
                    (tm1,) + bd(("uq", seg)))
                setb(("mo", seg % 2), tq)
                tqs.append(tq)
                tprev = tm1
            t6 = P.comp("ve", lambda e: e.tensor_scalar(
                col_c[:, :], col_a[:, :], 1.0 / QB, wmcols[:, 1:2], AL.mult, AL.mult), (t2, twmc))
            cols["g"] = t6
            tsc = P.comp("ve", lambda e, t=t: e.tensor_copy(sc2c[:, t:t + 1], col_c[:, :]), (t6,))
            sc2_toks[t] = tsc
            setb("urb", tqs[-1])
            for kc in range(64):
                bank = kc % 2
                pb = ps[bank]
                ttr = P.comp("te", lambda e, kc=kc, pb=pb: e.transpose(
                    pb.bitcast(BF16)[:, 0:128], uq_tm[:, kc * 128:(kc + 1) * 128], ident[:, :]),
                    (tqs[kc // 8],) + pd(bank))
                tev = P.comp("sc", lambda e, kc=kc, t8=t8, pb=pb: e.copy(
                    uqf[:, kc * 512 + t8 * 128: kc * 512 + (t8 + 1) * 128],
                    pb.bitcast(BF16)[:, 0:128]), (ttr,) + bd("uqf"))
                setp(bank, tev)
                if kc % 8 == 7:
                    setb(("uq", kc // 8), ttr)
                l_toks.append(tev)
        mq_last = None
        for oc in range(4):
            mlasts = [None] * 4
            for kc in range(64):
                wb = (oc * 64 + kc) % 6
                tlw = P.dma("sc", "d_mw%d" % wb, lambda e, oc=oc, kc=kc, wb=wb: e.dma_start_transpose(
                    wmt[wb][:, :], sh_wout[oc * 512:(oc + 1) * 512, kc * 128:(kc + 1) * 128]),
                    deps=(tp2o,) + bd(("wmt", wb)))
                for t8 in range(4):
                    def mm0(e, kc=kc, t8=t8, wb=wb):
                        return e.matmul(ps[4 + t8][:, :], uqf[:, kc * 512 + t8 * 128: kc * 512 + (t8 + 1) * 128],
                                        wmt[wb][:, :], start=(kc == 0), stop=(kc == 63))
                    if kc == 0:
                        d = (tlw,) + pd(4 + t8) + (tuple(l_toks) if oc == 0 else ())
                    elif t8 == 0:
                        d = (tlw,)
                    else:
                        d = ()
                    mlasts[t8] = P.comp("te", mm0, d)
                setb(("wmt", wb), mlasts[3])
            mq_last = mlasts[3]
            for t8 in range(4):
                t = sup * 4 + t8
                fb = (oc * 4 + t8) % 2
                tev2 = P.comp("ve", lambda e, t8=t8, t=t, fb=fb: e.tensor_scalar(
                    flb[fb][:, :], ps[4 + t8][:, :], sc2c[:, t:t + 1], None, AL.mult),
                    (mlasts[t8], sc2_toks[t]) + bd(("flb", fb)))
                setp(4 + t8, tev2)
                tsp = P.dma("gp", "d_flw", lambda e, sup=sup, oc=oc, t8=t8, fb=fb: e.dma_start(
                    out=fl_sp[sup * 512 + t8 * 128: sup * 512 + (t8 + 1) * 128,
                              oc * 512:(oc + 1) * 512],
                    in_=flb[fb][:, :]), deps=(tev2,))
                setb(("flb", fb), tsp)
                m_toks.append(tsp)
        setb("uqf", mq_last)

    if DBG:
        P.dma("sy", "d_tap", lambda e: e.dma_start(out=taps["tap_fl"], in_=fl_sp),
              deps=tuple(m_toks) + (P.tok("d_flw"),))
        P.dma("sy", "d_tap", lambda e: e.dma_start(
            out=taps["tap_sc2"][0:1, :].rearrange("o (t p) -> p (t o)", p=128),
            in_=sc2c[:, :], allow_slow_non_contiguous=True), deps=tuple(sc2_toks))

    mphase_end = [P.tok("e_ve"), P.tok("e_te"), P.tok("d_flw"), P.tok("d_x1w")]
    close_scope(markLM)

    x1n = [sb("x1n%d" % i, [128, D], F32) for i in range(2)]
    fln = [sb("fln%d" % i, [128, D], F32) for i in range(2)]
    frt = sb("frt", [128, D], F32)
    x2t = [sb("x2t%d" % i, [128, D], F32) for i in range(2)]

    for f in range(16):
        b = f % 2
        tl1 = P.dma("sy", "d_n1%d" % b, lambda e, f=f, b=b: e.dma_start(
            out=x1n[b][:, :], in_=x1_sp[f * 128:(f + 1) * 128, :]),
            deps=tuple(mphase_end) + bd(("x1n", b)))
        tl2 = P.dma("sy", "d_n2%d" % b, lambda e, f=f, b=b: e.dma_start(
            out=fln[b][:, :], in_=fl_sp[f * 128:(f + 1) * 128, :]),
            deps=tuple(mphase_end) + bd(("fln", b)))
        t1 = P.comp("ve", lambda e, f=f, b=b: e.tensor_scalar(
            frt[:, :], x1n[b][:, :], m2c[:, f:f + 1], rs2c[:, f:f + 1], AL.subtract, AL.mult),
            (tl1,) + bd("frt"))
        t2 = P.comp("ve", lambda e, b=b: e.tensor_tensor(frt[:, :], frt[:, :], fln[b][:, :], AL.add),
                    (tl2, t1))
        setb(("fln", b), t2)
        trs, _ = emit_ln_stats(frt[:, :], D, (t2,))
        t6 = P.comp("ve", lambda e: e.tensor_scalar(
            frt[:, :], frt[:, :], col_e[:, :], col_b[:, :], AL.subtract, AL.mult), (trs,))
        t7 = P.comp("ve", lambda e, b=b: e.tensor_tensor(x2t[b][:, :], x1n[b][:, :], frt[:, :], AL.add),
                    (t6,) + bd(("x2t", b)))
        cols["g"] = t7
        setb("frt", t7)
        setb(("x1n", b), t7)
        to = P.dma("sy", "d_out", lambda e, f=f, b=b: e.dma_start(
            out=y_out[f * 128:(f + 1) * 128, :], in_=x2t[b][:, :]), deps=(t7,))
        setb(("x2t", b), to)

    fin = P.tok("d_out")

    def final_wait(e, Pr):
        e.wait_ge(Pr.sems["d_out"], fin[1])
    P.raw("sy", final_wait)

    with nc.Block() as block:
        @block.tensor
        def _(e):
            P.emit("te", e)

        @block.vector
        def _(e):
            P.emit("ve", e)

        @block.scalar
        def _(e):
            P.emit("sc", e)

        @block.gpsimd
        def _(e):
            P.emit("gp", e)

        @block.sync
        def _(e):
            P.emit("sy", e)

    for cm in reversed(cms):
        cm.__exit__(None, None, None)
    for cm in reversed(P._stack):
        cm.__exit__(None, None, None)

    return nc


_NC_CACHE = {}


def kernel(**inputs):
    x = np.asarray(inputs["L0_attn_x"], np.float32)
    B = x.shape[0]
    nonce = np.uint32(int.from_bytes(os.urandom(4), "little") | 1)

    if "nc" not in _NC_CACHE:
        _NC_CACHE["nc"] = build_program()
    nc = _NC_CACHE["nc"]

    g = lambda k: np.ascontiguousarray(np.asarray(inputs[k], np.float32))
    wq, wk, wv, wg, wo = g("L0_attn_wq"), g("L0_attn_wk"), g("L0_attn_wv"), g("L0_attn_wg"), g("L0_attn_wo")
    bq, bk, bv, bg, bo = g("L0_attn_bq"), g("L0_attn_bk"), g("L0_attn_bv"), g("L0_attn_bg"), g("L0_attn_bo")
    gnw, gnb, gam = g("L0_attn_gn_w"), g("L0_attn_gn_b"), g("L0_attn_gamma")
    wnorm, wgv, wout = g("L0_ffn_w_norm"), g("L0_ffn_w_gv"), g("L0_ffn_w_out")

    in_maps = []
    for c in range(8):
        b, par = c // 2, c % 2
        hs = slice(par * 1024, (par + 1) * 1024)
        in_maps.append({
            "x_full": np.ascontiguousarray(x[b]),
            "x_half": np.ascontiguousarray(x[b, par * T:(par + 1) * T]),
            "wq_h": np.ascontiguousarray(wq[:, hs]),
            "wk_h": np.ascontiguousarray(wk[:, hs]),
            "wv_h": np.ascontiguousarray(wv[:, hs]),
            "wg_h": np.ascontiguousarray(wg[:, hs]),
            "wo_h": np.ascontiguousarray(wo[hs, :]),
            "bq_h": np.ascontiguousarray(bq[hs])[None, :],
            "bk_h": np.ascontiguousarray(bk[hs])[None, :],
            "bv_h": np.ascontiguousarray(bv[hs])[None, :],
            "bg_h": np.ascontiguousarray(bg[hs])[None, :],
            "bo_f": bo[None, :],
            "gnw_h": np.ascontiguousarray(gnw[hs])[None, :],
            "gnb_h": np.ascontiguousarray(gnb[hs])[None, :],
            "gam_h": np.ascontiguousarray(gam[par * 8:(par + 1) * 8])[:, None],
            "wnorm_f": wnorm[None, :],
            "wgv_hf": np.ascontiguousarray(wgv[par * HID2:(par + 1) * HID2, :]),
            "wout_hf": np.ascontiguousarray(wout[:, par * 4096:(par + 1) * 4096]),
            "par_h": np.array([[par]], np.uint32),
            "non_h": np.array([[nonce]], np.uint32),
        })

    res = run_bass_kernel_spmd(nc, in_maps, core_ids=list(range(8)))
    out = np.empty((B, S, D), np.float32)
    for c in range(8):
        b, par = c // 2, c % 2
        out[b, par * T:(par + 1) * T] = res.results[c]["y_out"]
    if DBG:
        kernel._taps = res.results
    return out


# version 13
# speedup vs baseline: 1.1548x; 1.0775x over previous
import os

import numpy as np

import concourse.bass as bass
import concourse.mybir as mybir
from concourse.bass_utils import run_bass_kernel_spmd

F32 = mybir.dt.float32
BF16 = mybir.dt.bfloat16
U32 = mybir.dt.uint32
I32 = mybir.dt.int32
AL = mybir.AluOpType
AF = mybir.ActivationFunctionType
AX = mybir.AxisListType

MAGIC = 12582912.0
EPS = 1e-5
QB = 127.0

D = 2048
S = 4096
T = 2048
HLOC = 8
NSUB = 4
SUBT = 1024
HID2 = 8192
DBG = os.environ.get("MK_TAPS", "")
SERIAL = os.environ.get("MK_SERIAL", "")

ENGS = ("te", "ve", "sc", "gp", "sy")


class Prog:
    def __init__(self, nc):
        self.nc = nc
        self.ops = {e: [] for e in ENGS}
        self.cnt = {}
        self.sems = {}
        self.wm = {e: {} for e in ENGS}
        self._stack = []

    def sem(self, name):
        if name not in self.sems:
            cm = self.nc.semaphore(name)
            self.sems[name] = cm.__enter__()
            self._stack.append(cm)
            self.cnt[name] = 0
        return self.sems[name]

    def _waits(self, eng, deps):
        w = []
        wm = self.wm[eng]
        for d in deps:
            if d is None:
                continue
            s, v = d
            if v > 0 and wm.get(s, 0) < v:
                w.append((self.sems[s], v))
                wm[s] = v
        return w

    def _serial_deps(self):
        return tuple((k, v) for k, v in self.cnt.items()
                     if v > 0 and k != "d_poll")

    def comp(self, eng, fn, deps=(), sem=None):
        if SERIAL:
            deps = tuple(deps) + self._serial_deps()
        sname = sem or ("e_" + eng)
        self.sem(sname)
        self.cnt[sname] += 1
        tok = (sname, self.cnt[sname])
        self.ops[eng].append((self._waits(eng, deps), fn, (self.sems[sname], 1)))
        return tok

    def dma(self, eng, sname, fn, deps=()):
        if SERIAL:
            deps = tuple(deps) + self._serial_deps()
        self.sem(sname)
        self.cnt[sname] += 16
        tok = (sname, self.cnt[sname])
        self.ops[eng].append((self._waits(eng, deps), fn, (self.sems[sname], 16)))
        return tok

    def raw(self, eng, fn, deps=()):
        self.ops[eng].append((self._waits(eng, deps), fn, None))

    def tok(self, sname):
        return (sname, self.cnt.get(sname, 0))

    def emit(self, eng, e):
        for waits, fn, inc in self.ops[eng]:
            for s, v in waits:
                e.wait_ge(s, v)
            if inc is None:
                fn(e, self)
            else:
                fn(e).then_inc(inc[0], inc[1])


def build_program():
    nc = bass.Bass("TRN2", target_bir_lowering=False)

    def inp(name, shape, dt=F32):
        return nc.dram_tensor(name, shape, dt, kind="ExternalInput").ap()

    x_full = inp("x_full", [S, D])
    x_half = inp("x_half", [T, D])
    wq_h = inp("wq_h", [D, 1024])
    wk_h = inp("wk_h", [D, 1024])
    wv_h = inp("wv_h", [D, 1024])
    wg_h = inp("wg_h", [D, 1024])
    wo_h = inp("wo_h", [1024, D])
    bq_h = inp("bq_h", [1, 1024])
    bk_h = inp("bk_h", [1, 1024])
    bv_h = inp("bv_h", [1, 1024])
    bg_h = inp("bg_h", [1, 1024])
    bo_f = inp("bo_f", [1, D])
    gnw_h = inp("gnw_h", [1, 1024])
    gnb_h = inp("gnb_h", [1, 1024])
    gam_h = inp("gam_h", [HLOC, 1])
    wnorm_f = inp("wnorm_f", [1, D])
    wgv_hf = inp("wgv_hf", [HID2, D])
    wout_hf = inp("wout_hf", [D, 4096])
    par_h = inp("par_h", [1, 1], U32)
    non_h = inp("non_h", [1, 1], U32)

    y_out = nc.dram_tensor("y_out", [T, D], F32, kind="ExternalOutput").ap()

    def shd(name, shape, dt=F32):
        return nc.dram_tensor(name, shape, dt, kind="Internal", addr_space="Shared").ap()

    sh_wgv = shd("sh_wgv", [2 * HID2, D], BF16)
    sh_wout = shd("sh_wout", [D, HID2], BF16)
    sh_ye = shd("sh_ye", [S, D])
    sh_yo = shd("sh_yo", [S, D])
    sh_stat_e = shd("sh_stat_e", [1, 2])
    sh_stat_o = shd("sh_stat_o", [1, 2])
    sh_flags = shd("sh_flags", [1, 8], U32)

    def loc(name, shape, dt=F32):
        return nc.dram_tensor(name, shape, dt, kind="Internal").ap()

    scr_tab = loc("scr_tab", [24, 132])
    scr_gc = loc("scr_gc", [1, 8])
    scr_wm = loc("scr_wm", [1, 4])
    scr_gn = loc("scr_gn", [2, 1024])
    ln0f_sp = loc("ln0f_sp", [D, S], BF16)
    ret_sp = loc("ret_sp", [S, 1024])
    u_sp = loc("u_sp", [T, HID2])
    x1_sp = loc("x1_sp", [T, D])
    fl_sp = loc("fl_sp", [T, D])

    taps = {}
    if DBG:
        for nm, shape, dt in [("tap_ret", [S, 1024], F32), ("tap_ln0f", [D, S], BF16),
                              ("tap_y", [S, D], F32), ("tap_x1", [T, D], F32),
                              ("tap_u", [T, HID2], F32), ("tap_fl", [T, D], F32),
                              ("tap_gn", [2, 1024], F32), ("tap_wm", [1, 4], F32),
                              ("tap_sc2", [1, T], F32)]:
            taps[nm] = nc.dram_tensor(nm, shape, dt, kind="ExternalOutput").ap()

    P = Prog(nc)
    cms = []
    bdep = {}
    pdep = [None] * 8

    def bd(key):
        return tuple(bdep.get(key) or ())

    def setb(key, *toks):
        bdep[key] = [t for t in toks if t is not None]

    def pd(i):
        return tuple(pdep[i] or ())

    def setp(i, *toks):
        pdep[i] = [t for t in toks if t is not None]

    def sb(name, shape, dt):
        cm = nc.sbuf_tensor(name, shape, dt)
        t = cm.__enter__()
        cms.append(cm)
        return t.ap() if hasattr(t, "ap") else t

    def close_scope(mark):
        while len(cms) > mark:
            cms.pop().__exit__(None, None, None)

    ps = []
    for i in range(8):
        cm = nc.psum_tensor("pb%d" % i, [128, 512], F32)
        t = cm.__enter__()
        cms.append(cm)
        ps.append(t.ap() if hasattr(t, "ap") else t)

    par_s = sb("par_s", [1, 1], U32)
    non_s = sb("non_s", [1, 1], U32)
    flg_s = sb("flg_s", [1, 8], U32)
    ident = sb("ident", [128, 128], BF16)
    idf = sb("idf", [128, 128], F32)
    ones_f = sb("ones_f", [128, 128], F32)
    col_a = sb("col_a", [128, 1], F32)
    col_b = sb("col_b", [128, 1], F32)
    col_c = sb("col_c", [128, 1], F32)
    col_d = sb("col_d", [128, 1], F32)
    col_e = sb("col_e", [128, 1], F32)
    col_f = sb("col_f", [128, 1], F32)
    m2c = sb("m2c", [128, 16], F32)
    rs2c = sb("rs2c", [128, 16], F32)
    srsc = sb("srsc", [128, 16], F32)
    scgvc = sb("scgvc", [128, 16], F32)
    wmcols = sb("wmcols", [128, 4], F32)
    sc_scr = sb("sc_scr", [128, D], F32)

    t0 = P.dma("gp", "d_c", lambda e: e.dma_start(out=par_s[:, :], in_=par_h))
    t0b = P.dma("gp", "d_c", lambda e: e.dma_start(out=non_s[:, :], in_=non_h))

    state = {}

    def sy_prologue(e, Pr):
        state["rp"] = e.alloc_register("rp")
        state["rt"] = e.alloc_register("rt")
        pv = e.value_load(par_s[0:1, 0:1])
        e.reg_mov(state["rp"], pv)
        e.reg_mov(state["rt"], 0)

    P.raw("sy", sy_prologue, deps=(t0, t0b))

    def gp_prologue(e, Pr):
        state["rp_gp"] = e.alloc_register("rp")
        pv = e.value_load(par_s[0:1, 0:1])
        e.reg_mov(state["rp_gp"], pv)

    P.raw("gp", gp_prologue, deps=(t0, t0b))
    psem = P.sem("d_poll")

    def branch_dma(sname, fn_e, fn_o, deps=()):
        P.sem(sname)
        P.cnt[sname] += 16
        tok = (sname, P.cnt[sname])
        s = P.sems[sname]

        def cl(e, Pr):
            with e.If_eq(state["rp"], 0):
                fn_e(e).then_inc(s, 16)
            with e.Else():
                fn_o(e).then_inc(s, 16)
        P.raw("sy", cl, deps=deps)
        return tok

    def branch_dma_gp(sname, fn_e, fn_o, deps=()):
        P.sem(sname)
        P.cnt[sname] += 16
        tok = (sname, P.cnt[sname])
        s = P.sems[sname]

        def cl(e, Pr):
            with e.If_eq(state["rp_gp"], 0):
                fn_e(e).then_inc(s, 16)
            with e.Else():
                fn_o(e).then_inc(s, 16)
        P.raw("gp", cl, deps=deps)
        return tok

    def set_flag(slot_even, deps=()):
        return branch_dma(
            "d_sh",
            lambda e: e.dma_start(out=sh_flags[0:1, slot_even:slot_even + 1], in_=non_s[:, :]),
            lambda e: e.dma_start(out=sh_flags[0:1, slot_even + 1:slot_even + 2], in_=non_s[:, :]),
            deps=deps)

    def poll(slot_even, deps=()):
        P.sem("e_sy")
        P.cnt["e_sy"] += 1
        tok = ("e_sy", P.cnt["e_sy"])
        esy = P.sems["e_sy"]

        def cl(e, Pr):
            def cond():
                e.reg_add(state["rt"], state["rt"], 16)
                e.dma_start(out=flg_s[0:1, :], in_=sh_flags).then_inc(psem, 16)
                e.wait_ge(psem, state["rt"])
                fe = e.value_load(flg_s[0:1, slot_even:slot_even + 1])
                fo = e.value_load(flg_s[0:1, slot_even + 1:slot_even + 2])
                n = e.value_load(non_s[0:1, 0:1])
                return (fe != n) | (fo != n)
            with e.While(cond):
                e.nop(cycle_cnt=500)
            e.nop().then_inc(esy, 1)
        P.raw("sy", cl, deps=deps)
        return tok

    tones_bf = P.comp("gp", lambda e: e.memset(ident[:, :], 1.0))
    tid = P.comp("gp", lambda e: e.affine_select(
        ident[:, :], ident[:, :], [[1, 128]], AL.is_equal, 0.0,
        base=0, channel_multiplier=-1), (tones_bf,))
    tones = P.comp("gp", lambda e: e.memset(ones_f[:, :], 1.0))
    tidf = P.comp("gp", lambda e: e.affine_select(
        idf[:, :], ones_f[:, :], [[1, 128]], AL.is_equal, 0.0,
        base=0, channel_multiplier=-1), (tones,))
    P.comp("sc", lambda e: e.preload_act_table(AF.Silu))

    cols = {"g": None}

    def emit_rsqrt(dst, src, scr, deps):
        t = P.comp("ve", lambda e: e.tensor_scalar(
            dst.bitcast(I32), src.bitcast(I32), 1, None, AL.arith_shift_right), deps)
        t = P.comp("ve", lambda e: e.tensor_scalar(
            dst.bitcast(I32), dst.bitcast(I32), 0, None, AL.bitwise_not), (t,))
        t = P.comp("ve", lambda e: e.tensor_scalar(
            dst.bitcast(I32), dst.bitcast(I32), 0x5F3759E0, None, AL.add), (t,))
        for _ in range(3):
            t = P.comp("ve", lambda e: e.tensor_tensor(scr, dst, dst, AL.mult), (t,))
            t = P.comp("ve", lambda e: e.tensor_tensor(scr, scr, src, AL.mult), (t,))
            t = P.comp("ve", lambda e: e.tensor_scalar(scr, scr, -0.5, 1.5, AL.mult, AL.add), (t,))
            t = P.comp("ve", lambda e: e.tensor_tensor(dst, dst, scr, AL.mult), (t,))
        return t

    def emit_ln_stats(xap, width, deps):
        inv = 1.0 / float(width)
        dd = tuple(deps) + ((cols["g"],) if cols["g"] else ())
        ts1 = P.comp("sc", lambda e: e.activation(
            sc_scr[:, 0:width], xap, AF.Copy, accum_out=col_a[:, :]), dd)
        ts2 = P.comp("sc", lambda e: e.activation(
            sc_scr[:, 0:width], xap, AF.Square, accum_out=col_d[:, :]), (ts1,))
        t1 = P.comp("ve", lambda e: e.tensor_scalar(col_e[:, :], col_a[:, :], inv, None, AL.mult), (ts1,))
        t2 = P.comp("ve", lambda e: e.tensor_scalar(col_d[:, :], col_d[:, :], inv, None, AL.mult), (ts2,))
        t3 = P.comp("ve", lambda e: e.tensor_tensor(col_c[:, :], col_e[:, :], col_e[:, :], AL.mult), (t1,))
        t4 = P.comp("ve", lambda e: e.tensor_tensor(col_d[:, :], col_d[:, :], col_c[:, :], AL.subtract), (t2, t3))
        t5 = P.comp("ve", lambda e: e.tensor_scalar(col_d[:, :], col_d[:, :], EPS, None, AL.add), (t4,))
        return emit_rsqrt(col_b, col_d, col_c, (t5,)), ts2

    wsb = [sb("wsb%d" % i, [128, 2048], F32) for i in range(2)]
    wqb_ = sb("wqb_", [128, 2048], F32)
    wqo = [sb("wqo%d" % i, [128, 2048], BF16) for i in range(2)]

    markA = len(cms)
    pw = sb("pw", [HLOC, 132], F32)
    ipw = sb("ipw", [HLOC, 132], F32)
    rpw = sb("rpw", [HLOC, 132], F32)
    gam_s = sb("gam_s", [HLOC, 1], F32)
    tri = sb("tri", [128, 128], F32)
    dqt = sb("dqt", [128, HLOC * 128], F32)
    decS = sb("decS", [128, HLOC], F32)
    minv = sb("minv", [128, HLOC], F32)
    gc_cols = sb("gc_cols", [128, HLOC], F32)
    gct = sb("gct", [128, HLOC * 128], F32)
    bqc = sb("bqc", [128, HLOC], F32)
    bkc = sb("bkc", [128, HLOC], F32)
    bv_rep = sb("bv_rep", [128, 1024], F32)
    gnw_hd = sb("gnw_hd", [HLOC, 128], F32)
    gnb_hd = sb("gnb_hd", [HLOC, 128], F32)
    ln0f = sb("ln0f", [128, 16 * SUBT], BF16)
    kf = sb("kf", [128, HLOC * SUBT], BF16)
    qdec = sb("qdec", [128, HLOC * SUBT], BF16)
    vtm = sb("vtm", [128, 8 * 1024], BF16)
    xbuf = [sb("xbuf%d" % i, [128, D], F32) for i in range(2)]
    ln0tm = [sb("ln0tm%d" % i, [128, D], BF16) for i in range(2)]
    wkqb = [sb("wkqb%d" % i, [128, 16 * 128], BF16) for i in range(2)]
    wvb = [sb("wvb%d" % i, [128, 16 * 256], BF16) for i in range(2)]
    scbf = [sb("scbf%d" % i, [128, 128], BF16) for i in range(4)]
    ktd = [sb("ktd%d" % i, [128, 128], BF16) for i in range(4)]
    retsc = [sb("retsc%d" % i, [128, 512], F32) for i in range(2)]
    sqsc = sb("sqsc", [128, 512], F32)
    tmp4 = sb("tmp4", [128, 8], F32)
    accS = sb("accS", [128, HLOC], F32)
    accQ = sb("accQ", [128, HLOC], F32)
    Sst = sb("Sst", [128, HLOC * 128], F32)
    Sbf = sb("Sbf", [128, HLOC * 128], BF16)
    qtmp = sb("qtmp", [128, 512], F32)
    wred = sb("wred", [128, 2], F32)
    wacc = sb("wacc", [128, 2], F32)
    wscol = sb("wscol", [2, 2], F32)
    wst2 = sb("wst2", [1, 4], F32)
    gsum = sb("gsum", [HLOC, 2], F32)
    gmrs = sb("gmrs", [HLOC, 4], F32)
    a_hd = sb("a_hd", [HLOC, 128], F32)
    b_hd = sb("b_hd", [HLOC, 128], F32)

    tg = P.dma("gp", "d_c", lambda e: e.dma_start(out=gam_s[:, :], in_=gam_h))
    tp = P.comp("gp", lambda e: e.memset(pw[:, 0:1], 1.0))
    tp = P.comp("ve", lambda e: e.tensor_scalar(pw[:, 1:2], pw[:, 0:1], gam_s[:, :], None, AL.mult), (tg, tp))
    n = 1
    while n < 128:
        def step(e, n=n):
            return e.tensor_scalar(pw[:, n + 1:2 * n + 1], pw[:, 1:n + 1], pw[:, n:n + 1], None, AL.mult)
        tp = P.comp("ve", step, (tp,))
        n *= 2
    tip = P.comp("ve", lambda e: e.reciprocal(ipw[:, 0:129], pw[:, 0:129]), (tp,))
    trp = P.comp("ve", lambda e: e.tensor_scalar(rpw[:, 0:129], ipw[:, 0:129], pw[:, 128:129], None, AL.mult), (tip,))
    tt1 = P.dma("gp", "d_c", lambda e: e.dma_start(out=scr_tab[0:HLOC, 0:129], in_=pw[:, 0:129]), (tp,))
    tt2 = P.dma("gp", "d_c", lambda e: e.dma_start(out=scr_tab[8:8 + HLOC, 0:129], in_=ipw[:, 0:129]), (tip,))
    tt3 = P.dma("gp", "d_c", lambda e: e.dma_start(out=scr_tab[16:16 + HLOC, 0:129], in_=rpw[:, 0:129]), (trp,))
    tt4 = P.dma("gp", "d_c", lambda e: e.dma_start(
        out=scr_gc[0:1, 0:HLOC].rearrange("o p -> p o"), in_=pw[:, 128:129]), (tp,))

    ttri = P.comp("gp", lambda e: e.affine_select(
        tri[:, :], ones_f[:, :], [[1, 128]], AL.is_ge, 0.0,
        base=0, channel_multiplier=-1), (tones,))
    tinv = []
    for h in range(HLOC):
        def ld(e, h=h):
            return e.dma_start(out=decS[:, h:h + 1],
                               in_=scr_tab[16 + h, 1:129].unsqueeze(0).rearrange("o p -> p o"))
        tinv.append(P.dma("sy", "d_c", ld, (tt3,)))
    tmi = []
    for h in range(HLOC):
        def ld(e, h=h):
            return e.dma_start(out=minv[:, h:h + 1],
                               in_=scr_tab[8 + h, 1:129].unsqueeze(0).rearrange("o p -> p o"))
        tmi.append(P.dma("sy", "d_c", ld, (tt2,)))
    tdq = []
    for h in range(HLOC):
        def ld(e, h=h):
            return e.dma_start(out=dqt[:, h * 128:(h + 1) * 128],
                               in_=scr_tab[h, 1:129].unsqueeze(0).to_broadcast((128, 128)))
        tdq.append(P.dma("sy", "d_c", ld, (tt1,)))
    tgc = P.dma("sy", "d_c", lambda e: e.dma_start(
        out=gc_cols[:, :], in_=scr_gc[0:1, 0:HLOC].to_broadcast((128, HLOC))), (tt4,))
    tgct = tgc
    for h in range(HLOC):
        tgct = P.comp("gp", lambda e, h=h: e.tensor_scalar(
            gct[:, h * 128:(h + 1) * 128], ones_f[:, :], gc_cols[:, h:h + 1], None, AL.mult),
            (tgct, tones))

    tb = []
    for h in range(HLOC):
        def lq(e, h=h):
            return e.dma_start(out=bqc[:, h:h + 1],
                               in_=bq_h[0, h * 128:(h + 1) * 128].unsqueeze(0).rearrange("o p -> p o"))

        def lk(e, h=h):
            return e.dma_start(out=bkc[:, h:h + 1],
                               in_=bk_h[0, h * 128:(h + 1) * 128].unsqueeze(0).rearrange("o p -> p o"))
        tb.append(P.dma("sy", "d_c", lq))
        tb.append(P.dma("sy", "d_c", lk))
    tbv = P.dma("sy", "d_c", lambda e: e.dma_start(out=bv_rep[:, :], in_=bv_h.to_broadcast((128, 1024))))
    tgw = P.dma("sy", "d_c", lambda e: e.dma_start(
        out=gnw_hd[:, :], in_=gnw_h.rearrange("o (h d) -> (o h) d", h=HLOC)))
    tgb = P.dma("sy", "d_c", lambda e: e.dma_start(
        out=gnb_hd[:, :], in_=gnb_h.rearrange("o (h d) -> (o h) d", h=HLOC)))

    tSz = P.comp("gp", lambda e: e.memset(Sst[:, :], 0.0))
    tSb0 = P.comp("gp", lambda e: e.memset(Sbf[:, :], 0.0), (tSz,))
    tacc = P.comp("gp", lambda e: e.memset(accS[:, :], 0.0))
    tacc2 = P.comp("gp", lambda e: e.memset(accQ[:, :], 0.0))

    twz = P.comp("gp", lambda e: e.memset(wacc[:, :], 0.0))
    stats_state = {"red": twz, "i": 0}
    stats_blocks = [("gv", i) for i in range(HID2 // 128)] + \
                   [("out", i) for i in range(2 * (D // 128))]

    def emit_stats_some(k):
        for _ in range(k):
            i = stats_state["i"]
            if i >= len(stats_blocks):
                return
            stats_state["i"] = i + 1
            kind, j = stats_blocks[i]
            b = i % 2
            if kind == "gv":
                tl = P.dma("sc", "d_st%d" % b, lambda e, j=j, b=b: e.dma_start(
                    out=wsb[b][:, 0:D], in_=wgv_hf[j * 128:(j + 1) * 128, :]),
                    deps=bd(("wsb", b)))
                col = 0
                wdt = D
            else:
                ri, cj = j // 2, j % 2
                tl = P.dma("sc", "d_st%d" % b, lambda e, ri=ri, cj=cj, b=b: e.dma_start(
                    out=wsb[b][:, 0:2048], in_=wout_hf[ri * 128:(ri + 1) * 128, cj * 2048:(cj + 1) * 2048]),
                    deps=bd(("wsb", b)))
                col = 1
                wdt = 2048
            tr = P.comp("ve", lambda e, b=b, col=col, wdt=wdt: e.tensor_reduce(
                wred[:, col:col + 1], wsb[b][:, 0:wdt].rearrange("p (o f) -> p o f", o=1),
                AX.X, AL.add, apply_absolute_value=True), (tl, stats_state["red"]))
            setb(("wsb", b), tr)
            stats_state["red"] = P.comp("ve", lambda e, col=col: e.tensor_tensor(
                wacc[:, col:col + 1], wacc[:, col:col + 1], wred[:, col:col + 1], AL.add), (tr, twz))

    def finish_stats():
        emit_stats_some(len(stats_blocks))
        twt = P.comp("te", lambda e: e.transpose(ps[2][0:2, 0:128], wacc[:, :], idf[:, :]),
                     (stats_state["red"], tidf) + pd(2))
        twe = P.comp("ve", lambda e: e.tensor_reduce(
            wscol[0:2, 0:1], ps[2][0:2, 0:128].rearrange("p (o f) -> p o f", o=1), AX.X, AL.add), (twt,))
        setp(2, twe)
        tst = branch_dma("d_sh",
                         lambda e: e.dma_start(out=sh_stat_e.rearrange("o p -> p o"), in_=wscol[0:2, 0:1]),
                         lambda e: e.dma_start(out=sh_stat_o.rearrange("o p -> p o"), in_=wscol[0:2, 0:1]),
                         deps=(twe,))
        return set_flag(0, deps=(tst,))

    def emit_ln0(st):
        evacs = []
        for tt in range(8):
            b = tt % 2
            row = st * SUBT + tt * 128
            tl = P.dma("sy", "d_x%d" % b, lambda e, row=row, b=b: e.dma_start(
                out=xbuf[b][:, :], in_=x_full[row:row + 128, :]), deps=bd(("x", b)))
            trs, tsq = emit_ln_stats(xbuf[b][:, :], D, (tl,))
            tap = P.comp("ve", lambda e, b=b: e.tensor_scalar(
                ln0tm[b][:, :], xbuf[b][:, :], col_e[:, :], col_b[:, :],
                AL.subtract, AL.mult), (trs,) + bd(("ln0tm", b)))
            cols["g"] = tap
            setb(("x", b), tap, tsq)
            last_tr = None
            for kc in range(16):
                bank = 6 + (kc % 2)
                pb = ps[bank]
                ttr = P.comp("te", lambda e, b=b, kc=kc, pb=pb: e.transpose(
                    pb.bitcast(BF16)[:, 0:128], ln0tm[b][:, kc * 128:(kc + 1) * 128], ident[:, :]),
                    (tap, tid) + pd(bank) + bd("ln0f"))
                tev = P.comp("sc", lambda e, kc=kc, tt=tt, pb=pb: e.copy(
                    ln0f[:, kc * SUBT + tt * 128: kc * SUBT + (tt + 1) * 128],
                    pb.bitcast(BF16)[:, 0:128]), (ttr,))
                setp(bank, tev)
                last_tr = ttr
                evacs.append(tev)
            setb(("ln0tm", b), last_tr)
            if st > 0:
                emit_stats_some(3)
        tsp = P.dma("gp", "d_spw", lambda e, st=st: e.dma_start(
            out=ln0f_sp.rearrange("(c p) t -> p c t", p=128)[:, :, st * SUBT:(st + 1) * SUBT],
            in_=ln0f.rearrange("p (c t) -> p c t", c=16)), deps=tuple(evacs))
        return evacs, tsp

    def emit_proj(st, ln_evacs, tsp):
        lasts = []
        for wmat, is_q in ((wk_h, False), (wq_h, True)):
            for fb in range(HLOC):
                b = fb % 2
                tl = P.dma("gp", "d_wk%d" % b, lambda e, wmat=wmat, fb=fb, b=b: e.dma_start(
                    out=wkqb[b].rearrange("p (c f) -> p c f", c=16),
                    in_=wmat[:, fb * 128:(fb + 1) * 128].rearrange("(c p) f -> p c f", p=128)),
                    deps=bd(("wkq", b)))
                for tt2 in range(2):
                    pb = ps[tt2]
                    mlast = None
                    for kc in range(16):
                        def mm(e, fb=fb, b=b, kc=kc, tt2=tt2, pb=pb):
                            return e.matmul(pb[:, :],
                                            wkqb[b][:, kc * 128:(kc + 1) * 128],
                                            ln0f[:, kc * SUBT + tt2 * 512: kc * SUBT + (tt2 + 1) * 512],
                                            start=(kc == 0), stop=(kc == 15))
                        d = (tl,) + pd(tt2) + tuple(ln_evacs) if kc == 0 else (tl,)
                        mlast = P.comp("te", mm, d)
                    co = fb * SUBT + tt2 * 512
                    if not is_q:
                        tev = P.comp("sc", lambda e, fb=fb, co=co, pb=pb: e.add(
                            kf[:, co:co + 512], pb[:, :], bkc[:, fb:fb + 1]),
                            (mlast,) + tuple(tb) + bd("kf"))
                    else:
                        te1 = P.comp("sc", lambda e, fb=fb, pb=pb: e.add(
                            qtmp[:, :], pb[:, :], bqc[:, fb:fb + 1]),
                            (mlast,) + tuple(tb) + bd("qtmp"))
                        tev = te1
                        for j4 in range(4):
                            tev = P.comp("ve", lambda e, fb=fb, co=co, j4=j4: e.tensor_tensor(
                                qdec[:, co + j4 * 128: co + (j4 + 1) * 128],
                                qtmp[:, j4 * 128:(j4 + 1) * 128],
                                dqt[:, fb * 128:(fb + 1) * 128], AL.mult),
                                (te1,) + tuple(tdq) + bd("qdec"))
                        setb("qtmp", tev)
                    setp(tt2, tev)
                    lasts.append(tev)
                setb(("wkq", b), mlast)
        for ob in range(4):
            b = ob % 2
            tl = P.dma("gp", "d_wv%d" % b, lambda e, ob=ob, b=b: e.dma_start(
                out=wvb[b].rearrange("p (c f) -> p c f", c=16),
                in_=wv_h[:, ob * 256:(ob + 1) * 256].rearrange("(c p) f -> p c f", p=128)),
                deps=bd(("wv", b)))
            for tt3 in range(8):
                bank = 2 + tt3 % 2
                pb = ps[bank]
                mlast = None
                for kc in range(16):
                    def mm(e, ob=ob, b=b, kc=kc, tt3=tt3, pb=pb):
                        return e.matmul(pb[:, 0:256],
                                        ln0f[:, kc * SUBT + tt3 * 128: kc * SUBT + (tt3 + 1) * 128],
                                        wvb[b][:, kc * 256:(kc + 1) * 256],
                                        start=(kc == 0), stop=(kc == 15))
                    d = (tl,) + pd(bank) + tuple(ln_evacs) if kc == 0 else (tl,)
                    mlast = P.comp("te", mm, d)
                tev = P.comp("ve", lambda e, ob=ob, tt3=tt3, pb=pb: e.tensor_tensor(
                    vtm[:, tt3 * 1024 + ob * 256: tt3 * 1024 + (ob + 1) * 256],
                    pb[:, 0:256], bv_rep[:, ob * 256:(ob + 1) * 256], AL.add),
                    (mlast, tbv) + bd("vtm"))
                setp(bank, tev)
                lasts.append(tev)
            setb(("wv", b), mlast)
        return lasts

    ROT = (4, 5, 0, 1)

    def emit_chunks(st, dep_toks):
        for c in range(8):
            tcast = P.comp("sc", lambda e: e.copy(Sbf[:, :], Sst[:, :]),
                           tuple(dep_toks) + bd("Sbf"))
            ret_last = {}
            for half in range(2):
                tscs = {}
                for h in range(half * 4, half * 4 + 4):
                    bank = ROT[h % 4]
                    pb = ps[bank]
                    kcol = h * SUBT + c * 128
                    tsc = P.comp("te", lambda e, kcol=kcol, pb=pb: e.matmul(
                        pb[:, 0:128], kf[:, kcol:kcol + 128], qdec[:, kcol:kcol + 128],
                        start=True, stop=True), pd(bank) + (tcast,))
                    tscs[h] = tsc
                tmks = {}
                for h in range(half * 4, half * 4 + 4):
                    bank = ROT[h % 4]
                    pb = ps[bank]
                    tmk = P.comp("ve", lambda e, h=h, pb=pb: e.scalar_tensor_tensor(
                        scbf[h % 4][:, :], pb[:, 0:128], minv[:, h:h + 1], tri[:, :],
                        AL.mult, AL.mult),
                        (tscs[h], ttri) + tuple(tmi) + bd(("scbf", h % 4)))
                    setp(bank, tmk)
                    tmks[h] = tmk
                for h in range(half * 4, half * 4 + 4):
                    rb = ps[2 + h // 4]
                    rcol = (h % 4) * 128
                    kcol = h * SUBT + c * 128
                    vcol = c * 1024 + h * 128
                    d1 = (tmks[h],) + (pd(2 + h // 4) if h % 4 == 0 else ())
                    tr1 = P.comp("te", lambda e, vcol=vcol, rb=rb, rcol=rcol, h=h: e.matmul(
                        rb[:, rcol:rcol + 128], scbf[h % 4][:, :], vtm[:, vcol:vcol + 128],
                        start=True, stop=False), d1)
                    tr2 = P.comp("te", lambda e, kcol=kcol, rb=rb, rcol=rcol, h=h: e.matmul(
                        rb[:, rcol:rcol + 128], qdec[:, kcol:kcol + 128], Sbf[:, h * 128:(h + 1) * 128],
                        start=False, stop=True), (tr1,))
                    setb(("scbf", h % 4), tr1)
                    ret_last[h // 4] = tr2
            setb("Sbf", ret_last[0], ret_last[1])
            sp_toks = []
            for bnk in range(2):
                rb = ps[2 + bnk]
                rsc = retsc[bnk]
                tev = P.comp("ve", lambda e, rb=rb, rsc=rsc: e.tensor_copy(rsc[:, :], rb[:, :]),
                             (ret_last[bnk],) + bd(("retsc", bnk)))
                setp(2 + bnk, tev)
                tS1 = P.comp("ve", lambda e, rsc=rsc: e.tensor_reduce(
                    tmp4[:, 0:4], rsc[:, :].rearrange("p (h f) -> p h f", h=4), AX.X, AL.add), (tev,))
                tS2 = P.comp("ve", lambda e, bnk=bnk: e.tensor_tensor(
                    accS[:, bnk * 4:bnk * 4 + 4], accS[:, bnk * 4:bnk * 4 + 4], tmp4[:, 0:4], AL.add),
                    (tS1, tacc))
                tq1 = P.comp("sc", lambda e, rsc=rsc: e.square(sqsc[:, :], rsc[:, :]),
                             (tev, tS1) + bd("sqsc"))
                tq2 = P.comp("ve", lambda e: e.tensor_reduce(
                    tmp4[:, 4:8], sqsc[:, :].rearrange("p (h f) -> p h f", h=4), AX.X, AL.add),
                    (tq1, tS2))
                setb("sqsc", tq2)
                tq3 = P.comp("ve", lambda e, bnk=bnk: e.tensor_tensor(
                    accQ[:, bnk * 4:bnk * 4 + 4], accQ[:, bnk * 4:bnk * 4 + 4], tmp4[:, 4:8], AL.add),
                    (tq2, tacc2))
                tsp = P.dma("gp", "d_spw", lambda e, st=st, c=c, bnk=bnk, rsc=rsc: e.dma_start(
                    out=ret_sp[st * SUBT + c * 128: st * SUBT + (c + 1) * 128, bnk * 512:(bnk + 1) * 512],
                    in_=rsc[:, :]), deps=(tq1,))
                setb(("retsc", bnk), tsp, tq2)
                sp_toks += [tq3, tsp]
            ztoks = []
            for half in range(2):
                tkds = {}
                for h in range(half * 4, half * 4 + 4):
                    bank = ROT[h % 4]
                    pb = ps[bank]
                    kcol = h * SUBT + c * 128
                    ttr = P.comp("te", lambda e, kcol=kcol, pb=pb: e.transpose(
                        pb.bitcast(BF16)[:, 0:128], kf[:, kcol:kcol + 128], ident[:, :]),
                        pd(bank) + (tid,))
                    tkd = P.comp("ve", lambda e, h=h, pb=pb: e.tensor_scalar(
                        ktd[h % 4][:, :], pb.bitcast(BF16)[:, 0:128], decS[:, h:h + 1], None, AL.mult),
                        (ttr,) + tuple(tinv) + bd(("ktd", h % 4)))
                    setp(bank, tkd)
                    tkds[h] = tkd
                for h in range(half * 4, half * 4 + 4):
                    zb = ps[6 + h // 4]
                    zcol = (h % 4) * 128
                    vcol = c * 1024 + h * 128
                    d1 = (tkds[h],) + (pd(6 + h // 4) if h % 4 == 0 else ())
                    tz = P.comp("te", lambda e, h=h, vcol=vcol, zb=zb, zcol=zcol: e.matmul(
                        zb[:, zcol:zcol + 128], ktd[h % 4][:, :], vtm[:, vcol:vcol + 128],
                        start=True, stop=True), d1)
                    setb(("ktd", h % 4), tz)
                    ztoks.append(tz)
            tS = tcast
            for h in range(HLOC):
                tS = P.comp("gp", lambda e, h=h: e.tensor_scalar(
                    Sst[:, h * 128:(h + 1) * 128], Sst[:, h * 128:(h + 1) * 128],
                    gc_cols[:, h:h + 1], None, AL.mult), (tS, tgc))
            tS = P.comp("ve", lambda e: e.tensor_tensor(
                Sst[:, 0:512], Sst[:, 0:512], ps[6][:, :], AL.add), (tS,) + tuple(ztoks))
            tS2b = P.comp("ve", lambda e: e.tensor_tensor(
                Sst[:, 512:1024], Sst[:, 512:1024], ps[7][:, :], AL.add), (tS,))
            setp(6, tS)
            setp(7, tS2b)
            dep_toks = [tS2b] + sp_toks
            if st > 0:
                emit_stats_some(2)
        return dep_toks

    tf1 = None
    adeps = []
    for st in range(NSUB):
        evs, tsps = emit_ln0(st)
        pls = emit_proj(st, evs, tsps)
        setb("ln0f", pls[-1], tsps)
        if st == 0:
            adeps = emit_chunks(0, pls[-3:] + [tSz, tSb0])
        else:
            adeps = emit_chunks(st, pls[-3:] + adeps)
        setb("kf", adeps[0])
        setb("qdec", adeps[0])
        setb("vtm", adeps[0])
        if st == 2:
            tf1 = finish_stats()

    tp1 = poll(0, deps=(tf1,))
    tr1 = P.dma("sy", "d_c", lambda e: e.dma_start(out=wst2[0:1, 0:2], in_=sh_stat_e), (tp1,))
    tr2 = P.dma("sy", "d_c", lambda e: e.dma_start(out=wst2[0:1, 2:4], in_=sh_stat_o), (tp1,))
    tws = P.comp("ve", lambda e: e.tensor_tensor(wst2[0:1, 0:2], wst2[0:1, 0:2], wst2[0:1, 2:4], AL.add),
                 (tr1, tr2))
    tws = P.comp("ve", lambda e: e.tensor_scalar(
        wst2[0:1, 0:1], wst2[0:1, 0:1], 1.0 / (16384.0 * 2048.0), EPS, AL.mult, AL.max), (tws,))
    tws = P.comp("ve", lambda e: e.tensor_scalar(
        wst2[0:1, 1:2], wst2[0:1, 1:2], 1.0 / (2048.0 * 8192.0), EPS, AL.mult, AL.max), (tws,))
    tws2 = P.comp("ve", lambda e: e.reciprocal(wst2[0:1, 2:4], wst2[0:1, 0:2]), (tws,))
    twm1 = P.dma("gp", "d_c", lambda e: e.dma_start(out=scr_wm[0:1, 0:2], in_=wst2[0:1, 0:2]), (tws,))
    twm2 = P.dma("gp", "d_c", lambda e: e.dma_start(out=scr_wm[0:1, 2:4], in_=wst2[0:1, 2:4]), (tws2,))
    twmc = P.dma("sy", "d_c", lambda e: e.dma_start(
        out=wmcols[:, :], in_=scr_wm[0:1, 0:4].to_broadcast((128, 4))), (twm1, twm2))

    tt1g = P.comp("te", lambda e: e.transpose(ps[0][0:HLOC, 0:128], accS[:, :], idf[:, :]),
                  tuple(adeps) + (tidf,) + pd(0))
    tg1 = P.comp("ve", lambda e: e.tensor_reduce(
        gsum[:, 0:1], ps[0][0:HLOC, 0:128].rearrange("p (o f) -> p o f", o=1), AX.X, AL.add), (tt1g,))
    setp(0, tg1)
    tt2g = P.comp("te", lambda e: e.transpose(ps[1][0:HLOC, 0:128], accQ[:, :], idf[:, :]),
                  (tg1,) + tuple(adeps) + pd(1))
    tg2 = P.comp("ve", lambda e: e.tensor_reduce(
        gsum[:, 1:2], ps[1][0:HLOC, 0:128].rearrange("p (o f) -> p o f", o=1), AX.X, AL.add), (tt2g,))
    setp(1, tg2)
    NTOT = float(S * 128)
    tg3 = P.comp("ve", lambda e: e.tensor_scalar(gmrs[:, 0:2], gsum[:, 0:2], 1.0 / NTOT, None, AL.mult),
                 (tg2,))
    tg4 = P.comp("ve", lambda e: e.tensor_tensor(gmrs[:, 2:3], gmrs[:, 0:1], gmrs[:, 0:1], AL.mult), (tg3,))
    tg5 = P.comp("ve", lambda e: e.tensor_tensor(gmrs[:, 2:3], gmrs[:, 1:2], gmrs[:, 2:3], AL.subtract), (tg4,))
    tg6 = P.comp("ve", lambda e: e.tensor_scalar(gmrs[:, 2:3], gmrs[:, 2:3], EPS, None, AL.add), (tg5,))
    tg7 = emit_rsqrt(gmrs[:, 3:4], gmrs[:, 2:3], gsum[:, 0:1], (tg6,))
    tg8 = P.comp("ve", lambda e: e.tensor_scalar(a_hd[:, :], gnw_hd[:, :], gmrs[:, 3:4], None, AL.mult),
                 (tg7, tgw))
    tg9 = P.comp("ve", lambda e: e.tensor_scalar(b_hd[:, :], a_hd[:, :], gmrs[:, 0:1], None, AL.mult), (tg8,))
    tg9 = P.comp("ve", lambda e: e.tensor_tensor(b_hd[:, :], gnb_hd[:, :], b_hd[:, :], AL.subtract), (tg9, tgb))
    tga = P.dma("gp", "d_c", lambda e: e.dma_start(
        out=scr_gn[0:1, :].rearrange("o (h d) -> (o h) d", h=HLOC), in_=a_hd[:, :]), (tg8,))
    tgb2 = P.dma("gp", "d_c", lambda e: e.dma_start(
        out=scr_gn[1:2, :].rearrange("o (h d) -> (o h) d", h=HLOC), in_=b_hd[:, :]), (tg9,))

    if DBG:
        P.dma("sy", "d_tap", lambda e: e.dma_start(out=taps["tap_ret"], in_=ret_sp),
              deps=tuple(adeps) + (P.tok("d_spw"),))
        P.dma("sy", "d_tap", lambda e: e.dma_start(out=taps["tap_ln0f"], in_=ln0f_sp),
              deps=(P.tok("d_spw"),))
        P.dma("sy", "d_tap", lambda e: e.dma_start(out=taps["tap_gn"], in_=scr_gn), deps=(tga, tgb2))

    aphase_end = [P.tok("e_ve"), P.tok("e_te"), P.tok("e_sc"), P.tok("e_gp"),
                  P.tok("d_spw"), tga, tgb2]

    def make_quant_emitters(blocks, src, stage, sc_idx, store, out_toks):
        ems = []
        for bi, (r0, c0, ncols) in enumerate(blocks):
            b = bi % 2

            def em(r0=r0, c0=c0, ncols=ncols, b=b, src=src, stage=stage, sc_idx=sc_idx,
                   store=store, out_toks=out_toks):
                tl = P.dma("sc", "d_q%d" % b, lambda e, r0=r0, c0=c0, ncols=ncols, b=b, src=src: e.dma_start(
                    out=wsb[b][:, 0:ncols], in_=src[r0:r0 + 128, c0:c0 + ncols]),
                    deps=bd(("wsb", b)))
                t1 = P.comp("gp", lambda e, b=b, ncols=ncols, sc_idx=sc_idx: e.tensor_scalar(
                    wsb[b][:, 0:ncols], wsb[b][:, 0:ncols], wmcols[:, 2 + sc_idx:3 + sc_idx], MAGIC,
                    AL.mult, AL.add), (tl, twmc))
                t2 = P.comp("ve", lambda e, b=b, ncols=ncols: e.tensor_scalar(
                    wqb_[:, 0:ncols], wsb[b][:, 0:ncols], MAGIC, 1.0, AL.subtract, AL.min), (t1,))
                setb(("wsb", b), t2)
                t3 = P.comp("ve", lambda e, b=b, ncols=ncols: e.tensor_scalar(
                    wqo[b][:, 0:ncols], wqb_[:, 0:ncols], -1.0, None, AL.max),
                    (t2,) + bd(("wqo", b)))
                if store is None:
                    ts = P.dma("gp", "d_q2%d" % b, lambda e, r0=r0, c0=c0, ncols=ncols, b=b, stage=stage: e.dma_start(
                        out=stage[r0:r0 + 128, c0:c0 + ncols], in_=wqo[b][:, 0:ncols]), deps=(t3,))
                else:
                    ts = store(r0, c0, ncols, b, t3)
                setb(("wqo", b), ts)
                out_toks.append(ts)
            ems.append(em)
        return ems

    def gv_store(r0, c0, ncols, b, t3):
        return branch_dma_gp(
            "d_q2%d" % b,
            lambda e, r0=r0, ncols=ncols, b=b: e.dma_start(
                out=sh_wgv[r0:r0 + 128, 0:ncols], in_=wqo[b][:, 0:ncols]),
            lambda e, r0=r0, ncols=ncols, b=b: e.dma_start(
                out=sh_wgv[HID2 + r0:HID2 + r0 + 128, 0:ncols], in_=wqo[b][:, 0:ncols]),
            deps=(t3,))

    def wout_store(r0, c0, ncols, b, t3):
        return branch_dma_gp(
            "d_q2%d" % b,
            lambda e, r0=r0, c0=c0, ncols=ncols, b=b: e.dma_start(
                out=sh_wout[r0:r0 + 128, c0:c0 + ncols], in_=wqo[b][:, 0:ncols]),
            lambda e, r0=r0, c0=c0, ncols=ncols, b=b: e.dma_start(
                out=sh_wout[r0:r0 + 128, 4096 + c0:4096 + c0 + ncols], in_=wqo[b][:, 0:ncols]),
            deps=(t3,))

    tq_gv = []
    tq_out = []
    p1_emitters = (
        make_quant_emitters([(i * 128, 0, D) for i in range(HID2 // 128)], wgv_hf, None, 0,
                            gv_store, tq_gv)
        + make_quant_emitters([(ri * 128, cj * 2048, 2048) for ri in range(D // 128) for cj in range(2)],
                              wout_hf, None, 1, wout_store, tq_out))

    close_scope(markA)

    markF = len(cms)
    a_rep = sb("a_rep", [128, 1024], F32)
    b_rep = sb("b_rep", [128, 1024], F32)
    bg_rep2 = sb("bg_rep2", [128, 1024], F32)
    wgb = sb("wgb", [128, 16 * 1024], BF16)
    wob = sb("wob", [128, 8 * 2048], BF16)
    lnfb = [sb("lnfb%d" % i, [128, 16 * 256], BF16) for i in range(2)]
    retb = [sb("retb%d" % i, [128, 1024], F32) for i in range(2)]
    rof = sb("rof", [128, 1024], F32)
    ro_bf = [sb("ro_bf%d" % i, [128, 1024], BF16) for i in range(2)]
    gtmp = sb("gtmp", [128, 512], F32)
    g_bf = sb("g_bf", [128, 1024], BF16)
    gro = [sb("gro%d" % i, [128, 1024], BF16) for i in range(2)]
    grof = [sb("grof%d" % i, [128, 8 * 128], BF16) for i in range(2)]
    ysc = [sb("ysc%d" % i, [128, D], F32) for i in range(2)]
    bo_rep = sb("bo_rep", [128, D], F32)

    tar = P.dma("sy", "d_c", lambda e: e.dma_start(
        out=a_rep[:, :], in_=scr_gn[0:1, :].to_broadcast((128, 1024))), (tga, tgb2))
    tbr = P.dma("sy", "d_c", lambda e: e.dma_start(
        out=b_rep[:, :], in_=scr_gn[1:2, :].to_broadcast((128, 1024))), (tga, tgb2))
    tbg2 = P.dma("sy", "d_c", lambda e: e.dma_start(
        out=bg_rep2[:, :], in_=bg_h.to_broadcast((128, 1024))))
    tbo = P.dma("sy", "d_c", lambda e: e.dma_start(
        out=bo_rep[:, :], in_=bo_f.to_broadcast((128, D))))
    tbo2 = P.comp("ve", lambda e: e.tensor_scalar(
        bo_rep[:, :], bo_rep[:, :], 0.5, None, AL.mult), (tbo,))
    twg = P.dma("gp", "d_wg", lambda e: e.dma_start(
        out=wgb.rearrange("p (c f) -> p c f", c=16),
        in_=wg_h.rearrange("(c p) f -> p c f", p=128)), tuple(aphase_end))
    two = P.dma("gp", "d_wg", lambda e: e.dma_start(
        out=wob.rearrange("p (c f) -> p c f", c=8),
        in_=wo_h.rearrange("(c p) f -> p c f", p=128)), tuple(aphase_end))

    ypart_toks = []
    tlf = None
    for f in range(32):
        b = f % 2
        trr = P.dma("sy", "d_rt%d" % b, lambda e, f=f, b=b: e.dma_start(
            out=retb[b][:, :], in_=ret_sp[f * 128:(f + 1) * 128, :]),
            deps=bd(("retb", b)) + tuple(aphase_end))
        if f % 2 == 0:
            lslot = (f // 2) % 2
            tlf = P.dma("sy", "d_lf%d" % lslot, lambda e, f=f, lslot=lslot: e.dma_start(
                out=lnfb[lslot].rearrange("p (c t) -> p c t", c=16),
                in_=ln0f_sp.rearrange("(c p) t -> p c t", p=128)[:, :, f * 128:(f + 2) * 128]),
                deps=bd(("lnfb", lslot)) + tuple(aphase_end))
        lb = lnfb[(f // 2) % 2]
        lofs = (f % 2) * 128
        t1 = P.comp("ve", lambda e, b=b: e.tensor_tensor(rof[:, :], retb[b][:, :], a_rep[:, :], AL.mult),
                    (trr, tar) + bd("rof"))
        t2 = P.comp("ve", lambda e, b=b: e.tensor_tensor(ro_bf[b][:, :], rof[:, :], b_rep[:, :], AL.add),
                    (t1, tbr) + bd(("ro_bf", b)))
        setb("rof", t2)
        setb(("retb", b), t1)
        gts = []
        for ob in range(2):
            pb = ps[ob]
            mlast = None
            for kc in range(16):
                def mm(e, kc=kc, ob=ob, pb=pb, lb=lb, lofs=lofs):
                    return e.matmul(pb[:, :], lb[:, kc * 256 + lofs: kc * 256 + lofs + 128],
                                    wgb[:, kc * 1024 + ob * 512: kc * 1024 + (ob + 1) * 512],
                                    start=(kc == 0), stop=(kc == 15))
                d = (tlf, twg) + pd(ob) if kc == 0 else ()
                mlast = P.comp("te", mm, d)
            t3 = P.comp("ve", lambda e, ob=ob, pb=pb: e.tensor_tensor(
                gtmp[:, :], pb[:, :], bg_rep2[:, ob * 512:(ob + 1) * 512], AL.add),
                (mlast, tbg2) + bd("gtmp"))
            t4 = P.comp("sc", lambda e, ob=ob: e.activation(
                g_bf[:, ob * 512:(ob + 1) * 512], gtmp[:, :], AF.Silu), (t3,) + bd("g_bf"))
            setb("gtmp", t4)
            setp(ob, t4)
            gts.append(t4)
        t5 = P.comp("ve", lambda e, b=b: e.tensor_tensor(gro[b][:, :], g_bf[:, :], ro_bf[b][:, :], AL.mult),
                    (t2,) + tuple(gts) + bd(("gro", b)))
        setb("g_bf", t5)
        setb(("ro_bf", b), t5)
        gtr = []
        for kc in range(8):
            bank = 2 + kc % 2
            pb = ps[bank]
            ttr = P.comp("te", lambda e, kc=kc, pb=pb, b=b: e.transpose(
                pb.bitcast(BF16)[:, 0:128], gro[b][:, kc * 128:(kc + 1) * 128], ident[:, :]),
                (t5,) + pd(bank))
            tev = P.comp("sc", lambda e, kc=kc, b=b, pb=pb: e.copy(
                grof[b][:, kc * 128:(kc + 1) * 128], pb.bitcast(BF16)[:, 0:128]),
                (ttr,) + bd(("grof", b)))
            setp(bank, tev)
            gtr.append(tev)
        setb(("gro", b), gtr[-1])
        yts = []
        mlast = None
        for obq in range(4):
            bank = 4 + obq
            pb = ps[bank]
            for kc in range(8):
                def mm(e, kc=kc, obq=obq, b=b, pb=pb):
                    return e.matmul(pb[:, :], grof[b][:, kc * 128:(kc + 1) * 128],
                                    wob[:, kc * 2048 + obq * 512: kc * 2048 + (obq + 1) * 512],
                                    start=(kc == 0), stop=(kc == 7))
                d = tuple(gtr) + (two,) + pd(bank) if kc == 0 else ()
                mlast = P.comp("te", mm, d)
            tev = P.comp("ve", lambda e, obq=obq, b=b, pb=pb: e.tensor_tensor(
                ysc[b][:, obq * 512:(obq + 1) * 512], pb[:, :],
                bo_rep[:, obq * 512:(obq + 1) * 512], AL.add), (mlast, tbo2) + bd(("ysc", b)))
            setp(bank, tev)
            yts.append(tev)
        setb(("grof", b), mlast)
        tyw = branch_dma("d_yw",
                         lambda e, f=f, b=b: e.dma_start(out=sh_ye[f * 128:(f + 1) * 128, :], in_=ysc[b][:, :]),
                         lambda e, f=f, b=b: e.dma_start(out=sh_yo[f * 128:(f + 1) * 128, :], in_=ysc[b][:, :]),
                         deps=tuple(yts))
        setb(("ysc", b), tyw)
        setb(("lnfb", (f // 2) % 2), mlast)
        ypart_toks.append(tyw)
        for _ in range(3):
            if p1_emitters:
                p1_emitters.pop(0)()
        if f == 21:
            tf2 = set_flag(2, deps=tuple(tq_gv))

    while p1_emitters:
        p1_emitters.pop(0)()

    tfy = set_flag(4, deps=tuple(ypart_toks))
    tpy = poll(4, deps=(tfy,))
    tf6 = set_flag(6, deps=tuple(tq_out))
    if DBG:
        P.dma("sy", "d_tap", lambda e: e.dma_start(out=taps["tap_y"], in_=sh_ye), deps=(tpy,))

    fphase_end = [P.tok("e_ve"), P.tok("e_te"), P.tok("e_sc"), tpy]
    close_scope(markF)

    markG = len(cms)
    rqf = sb("rqf", [128, 16 * T], BF16)
    markG2 = len(cms)
    wn_rep = sb("wn_rep", [128, D], F32)
    yab = [sb("yab%d" % i, [128, D], F32) for i in range(2)]
    h2b = sb("h2b", [128, D], F32)
    q1b = [sb("q1b%d" % i, [128, D], BF16) for i in range(2)]

    twn = P.dma("sy", "d_c", lambda e: e.dma_start(
        out=wn_rep[:, :], in_=wnorm_f.to_broadcast((128, D))), tuple(fphase_end))

    gi_toks = []
    for f in range(16):
        b = f % 2
        tya = branch_dma_gp("d_yg%d" % b,
                            lambda e, f=f, b=b: e.dma_start(
                                out=yab[b][:, :], in_=sh_ye[f * 128:(f + 1) * 128, :]),
                            lambda e, f=f, b=b: e.dma_start(
                                out=yab[b][:, :], in_=sh_ye[T + f * 128:T + (f + 1) * 128, :]),
                            deps=(tpy,) + bd(("yab", b)))
        tyb = branch_dma_gp("d_yg%d" % b,
                            lambda e, f=f, b=b: e.dma_start(
                                out=yab[b][:, :], in_=sh_yo[f * 128:(f + 1) * 128, :], accum_op=AL.add),
                            lambda e, f=f, b=b: e.dma_start(
                                out=yab[b][:, :], in_=sh_yo[T + f * 128:T + (f + 1) * 128, :], accum_op=AL.add),
                            deps=(tya,))
        trs, _ = emit_ln_stats(yab[b][:, :], D, (tyb,))
        t6 = P.comp("ve", lambda e, b=b: e.tensor_scalar(
            yab[b][:, :], yab[b][:, :], col_e[:, :], col_b[:, :], AL.subtract, AL.mult), (trs,))
        tx2 = P.dma("gp", "d_xa%d" % b, lambda e, f=f, b=b: e.dma_start(
            out=yab[b][:, :], in_=x_half[f * 128:(f + 1) * 128, :], accum_op=AL.add), deps=(t6,))
        tx1 = P.dma("gp", "d_x1w", lambda e, f=f, b=b: e.dma_start(
            out=x1_sp[f * 128:(f + 1) * 128, :], in_=yab[b][:, :]), deps=(tx2,))
        trs2, _ = emit_ln_stats(yab[b][:, :], D, (tx2,))
        t9 = P.comp("ve", lambda e, f=f: e.tensor_copy(m2c[:, f:f + 1], col_e[:, :]), (trs2,))
        t12 = P.comp("ve", lambda e, f=f: e.tensor_copy(rs2c[:, f:f + 1], col_b[:, :]), (trs2,))
        t13 = P.comp("ve", lambda e: e.reciprocal(col_c[:, :], col_d[:, :]), (trs2,))
        t14 = P.comp("ve", lambda e: e.tensor_scalar(col_d[:, :], col_d[:, :], -EPS, None, AL.add), (t13,))
        t14 = P.comp("ve", lambda e: e.tensor_tensor(col_c[:, :], col_d[:, :], col_c[:, :], AL.mult), (t14,))
        t15 = P.comp("ve", lambda e: e.tensor_scalar(col_c[:, :], col_c[:, :], EPS, None, AL.add), (t14,))
        t16 = emit_rsqrt(col_f, col_c, col_a, (t15,))
        t17 = P.comp("ve", lambda e, f=f: e.tensor_tensor(srsc[:, f:f + 1], rs2c[:, f:f + 1], col_f[:, :],
                                                          AL.mult), (t16, t12))
        t18 = P.comp("ve", lambda e, b=b, f=f: e.tensor_scalar(
            h2b[:, :], yab[b][:, :], m2c[:, f:f + 1], srsc[:, f:f + 1], AL.subtract, AL.mult),
            (t17, t9) + bd("h2b"))
        setb(("yab", b), t18, tx1)
        t19 = P.comp("gp", lambda e: e.tensor_tensor(h2b[:, :], h2b[:, :], wn_rep[:, :], AL.mult), (t18, twn))
        t20 = P.comp("ve", lambda e: e.tensor_reduce(
            col_a[:, :], h2b[:, :].rearrange("p (o f) -> p o f", o=1), AX.X, AL.max,
            apply_absolute_value=True), (t19,))
        t21 = P.comp("ve", lambda e: e.tensor_scalar(col_a[:, :], col_a[:, :], EPS, None, AL.max), (t20,))
        t22 = P.comp("ve", lambda e: e.reciprocal(col_b[:, :], col_a[:, :]), (t21,))
        t23 = P.comp("ve", lambda e: e.tensor_scalar(col_b[:, :], col_b[:, :], QB, None, AL.mult), (t22,))
        t24 = P.comp("gp", lambda e: e.tensor_scalar(h2b[:, :], h2b[:, :], col_b[:, :], MAGIC,
                                                     AL.mult, AL.add), (t23,))
        t25 = P.comp("ve", lambda e, b=b: e.tensor_scalar(q1b[b][:, :], h2b[:, :], MAGIC, None, AL.subtract),
                     (t24,) + bd(("q1b", b)))
        setb("h2b", t25)
        t26 = P.comp("ve", lambda e, f=f: e.tensor_scalar(
            scgvc[:, f:f + 1], col_a[:, :], 1.0 / QB, wmcols[:, 0:1], AL.mult, AL.mult), (t21, twmc))
        cols["g"] = t26
        trql = None
        for kc in range(16):
            bank = kc % 2
            pb = ps[bank]
            ttr = P.comp("te", lambda e, kc=kc, b=b, pb=pb: e.transpose(
                pb.bitcast(BF16)[:, 0:128], q1b[b][:, kc * 128:(kc + 1) * 128], ident[:, :]),
                (t25,) + pd(bank))
            tev = P.comp("sc", lambda e, kc=kc, f=f, pb=pb: e.copy(
                rqf[:, kc * T + f * 128: kc * T + (f + 1) * 128], pb.bitcast(BF16)[:, 0:128]), (ttr,))
            setp(bank, tev)
            trql = ttr
            gi_toks.append(tev)
        setb(("q1b", b), trql)
        gi_toks += [tx1, t26]

    if DBG:
        P.dma("sy", "d_tap", lambda e: e.dma_start(out=taps["tap_x1"], in_=x1_sp),
              deps=tuple(gi_toks) + (P.tok("d_x1w"),))
        P.dma("sy", "d_tap", lambda e: e.dma_start(out=taps["tap_wm"], in_=scr_wm), deps=(twmc,))

    tp2 = poll(2, deps=tuple(gi_toks))
    tp2o = poll(6, deps=(tf6,))

    close_scope(markG2)
    wjv = [sb("wjv%d" % i, [128, 16 * 512], BF16) for i in range(2)]
    wjg = [sb("wjg%d" % i, [128, 16 * 512], BF16) for i in range(2)]
    vsc = sb("vsc", [128, 512], F32)
    gsc = sb("gsc", [128, 512], F32)
    ubf = [sb("ubf%d" % i, [128, 512], F32) for i in range(2)]

    j_toks = []
    for qp in range(16):
        b = qp % 2
        tvk = []
        tgk = []
        for kc in range(16):
            tv = P.dma("sc", "d_jv%d" % b, lambda e, qp=qp, b=b, kc=kc: e.dma_start_transpose(
                wjv[b][:, kc * 512:(kc + 1) * 512],
                sh_wgv[qp * 512:(qp + 1) * 512, kc * 128:(kc + 1) * 128]),
                deps=(tp2,) + (bd(("wjv", b)) if kc == 0 else ()))
            tg = P.dma("sc", "d_jg%d" % b, lambda e, qp=qp, b=b, kc=kc: e.dma_start_transpose(
                wjg[b][:, kc * 512:(kc + 1) * 512],
                sh_wgv[HID2 + qp * 512:HID2 + (qp + 1) * 512, kc * 128:(kc + 1) * 128]),
                deps=(tp2,) + (bd(("wjg", b)) if kc == 0 else ()))
            tvk.append(tv)
            tgk.append(tg)
        mlast = None
        for t in range(16):
            ub = t % 2
            pv = ps[4 + 2 * (t % 2)]
            pg = ps[5 + 2 * (t % 2)]
            for kc in range(16):
                def mmv(e, kc=kc, t=t, b=b, pv=pv):
                    return e.matmul(pv[:, :], rqf[:, kc * T + t * 128: kc * T + (t + 1) * 128],
                                    wjv[b][:, kc * 512:(kc + 1) * 512], start=(kc == 0), stop=(kc == 15))

                def mmg(e, kc=kc, t=t, b=b, pg=pg):
                    return e.matmul(pg[:, :], rqf[:, kc * T + t * 128: kc * T + (t + 1) * 128],
                                    wjg[b][:, kc * 512:(kc + 1) * 512], start=(kc == 0), stop=(kc == 15))
                dv = (tvk[kc],) + (pd(4 + 2 * (t % 2)) if kc == 0 else ())
                dg = (tgk[kc],) + (pd(5 + 2 * (t % 2)) if kc == 0 else ())
                P.comp("te", mmv, dv if t == 0 else (dv if kc == 0 else ()))
                mlast = P.comp("te", mmg, dg if t == 0 else (dg if kc == 0 else ()))
            t1 = P.comp("sc", lambda e, t=t, pv=pv: e.mul(
                vsc[:, :], pv[:, :], scgvc[:, t:t + 1]), (mlast,) + bd("vsc"))
            t2 = P.comp("sc", lambda e, t=t, pg=pg: e.activation(
                gsc[:, :], pg[:, :], AF.Silu, scale=scgvc[:, t:t + 1]), (mlast,) + bd("gsc"))
            t3 = P.comp("gp", lambda e, ub=ub: e.tensor_tensor(ubf[ub][:, :], vsc[:, :], gsc[:, :], AL.mult),
                        (t1, t2) + bd(("ubf", ub)))
            setb("vsc", t3)
            setb("gsc", t3)
            setp(4 + 2 * (t % 2), t3)
            setp(5 + 2 * (t % 2), t3)
            tu = P.dma("gp", "d_uw", lambda e, qp=qp, t=t, ub=ub: e.dma_start(
                out=u_sp[t * 128:(t + 1) * 128, qp * 512:(qp + 1) * 512], in_=ubf[ub][:, :]), deps=(t3,))
            setb(("ubf", ub), tu)
            j_toks.append(tu)
        setb(("wjv", b), mlast)
        setb(("wjg", b), mlast)

    if DBG:
        P.dma("sy", "d_tap", lambda e: e.dma_start(out=taps["tap_u"], in_=u_sp),
              deps=tuple(j_toks) + (P.tok("d_uw"),))

    jphase_end = [P.tok("e_ve"), P.tok("e_te"), P.tok("e_sc"), P.tok("d_uw")]
    close_scope(markG)

    markLM = len(cms)
    uqf = sb("uqf", [128, 64 * 512], BF16)
    urb = [sb("urb", [128, HID2], F32)] * 2
    uq_tm = sb("uq_tm", [128, HID2], BF16)
    sc2c = sb("sc2c", [128, 16], F32)
    wmt = [sb("wmt%d" % i, [128, 512], BF16) for i in range(6)]
    flb = [sb("flb%d" % i, [128, 512], F32) for i in range(2)]

    x1n = [sb("x1n%d" % i, [128, D], F32) for i in range(2)]
    fln = [sb("fln%d" % i, [128, D], F32) for i in range(2)]
    frt = sb("frt", [128, D], F32)
    x1w_tok = P.tok("d_x1w")

    def emit_n_tile(f, dsup):
        b = f % 2
        tl1 = P.dma("sy", "d_n1%d" % b, lambda e, f=f, b=b: e.dma_start(
            out=x1n[b][:, :], in_=x1_sp[f * 128:(f + 1) * 128, :]),
            deps=(dsup, x1w_tok) + bd(("x1n", b)))
        tl2 = P.dma("sy", "d_n2%d" % b, lambda e, f=f, b=b: e.dma_start(
            out=fln[b][:, :], in_=fl_sp[f * 128:(f + 1) * 128, :]),
            deps=(dsup,) + bd(("fln", b)))
        t1 = P.comp("ve", lambda e, f=f, b=b: e.tensor_scalar(
            frt[:, :], x1n[b][:, :], m2c[:, f:f + 1], rs2c[:, f:f + 1], AL.subtract, AL.mult),
            (tl1,) + bd("frt"))
        t2 = P.comp("ve", lambda e, b=b: e.tensor_tensor(frt[:, :], frt[:, :], fln[b][:, :], AL.add),
                    (tl2, t1))
        trs, _ = emit_ln_stats(frt[:, :], D, (t2,))
        t6 = P.comp("ve", lambda e: e.tensor_scalar(
            frt[:, :], frt[:, :], col_e[:, :], col_b[:, :], AL.subtract, AL.mult), (trs,))
        t7 = P.comp("gp", lambda e, b=b: e.tensor_tensor(fln[b][:, :], x1n[b][:, :], frt[:, :], AL.add),
                    (t6,))
        cols["g"] = t7
        setb("frt", t7)
        setb(("x1n", b), t7)
        to = P.dma("sy", "d_out", lambda e, f=f, b=b: e.dma_start(
            out=y_out[f * 128:(f + 1) * 128, :], in_=fln[b][:, :]), deps=(t7,))
        setb(("fln", b), to)

    m_toks = []
    sc2_toks = [None] * 16
    for sup in range(4):
        l_toks = []
        for t8 in range(4):
            t = sup * 4 + t8
            b = t8 % 2
            tl = P.dma("sy", "d_ur%d" % b, lambda e, t=t, b=b: e.dma_start(
                out=urb[b][:, :], in_=u_sp[t * 128:(t + 1) * 128, :]),
                deps=tuple(jphase_end) + bd("urb"))
            t1 = P.comp("ve", lambda e, b=b: e.tensor_reduce(
                col_a[:, :], urb[b][:, :].rearrange("p (o f) -> p o f", o=1), AX.X, AL.max,
                apply_absolute_value=True), (tl, cols["g"]))
            t2 = P.comp("ve", lambda e: e.tensor_scalar(col_a[:, :], col_a[:, :], EPS, None, AL.max), (t1,))
            t3 = P.comp("ve", lambda e: e.reciprocal(col_b[:, :], col_a[:, :]), (t2,))
            t4 = P.comp("ve", lambda e: e.tensor_scalar(col_b[:, :], col_b[:, :], QB, None, AL.mult), (t3,))
            tqs = []
            tprev = t4
            for seg in range(8):
                tm1 = P.comp("ve", lambda e, b=b, seg=seg: e.tensor_scalar(
                    urb[b][:, seg * 1024:(seg + 1) * 1024], urb[b][:, seg * 1024:(seg + 1) * 1024],
                    col_b[:, :], MAGIC, AL.mult, AL.add), (tprev,))
                tq = P.comp("ve", lambda e, b=b, seg=seg: e.tensor_scalar(
                    uq_tm[:, seg * 1024:(seg + 1) * 1024], urb[b][:, seg * 1024:(seg + 1) * 1024],
                    MAGIC, None, AL.subtract),
                    (tm1,) + bd(("uq", seg)))
                tqs.append(tq)
                tprev = tm1
            t6 = P.comp("ve", lambda e: e.tensor_scalar(
                col_c[:, :], col_a[:, :], 1.0 / QB, wmcols[:, 1:2], AL.mult, AL.mult), (t2, twmc))
            cols["g"] = t6
            tsc = P.comp("ve", lambda e, t=t: e.tensor_copy(sc2c[:, t:t + 1], col_c[:, :]), (t6,))
            sc2_toks[t] = tsc
            setb("urb", tqs[-1])
            for kc in range(64):
                bank = kc % 2
                pb = ps[bank]
                ttr = P.comp("te", lambda e, kc=kc, pb=pb: e.transpose(
                    pb.bitcast(BF16)[:, 0:128], uq_tm[:, kc * 128:(kc + 1) * 128], ident[:, :]),
                    (tqs[kc // 8],) + pd(bank))
                tev = P.comp("sc", lambda e, kc=kc, t8=t8, pb=pb: e.copy(
                    uqf[:, kc * 512 + t8 * 128: kc * 512 + (t8 + 1) * 128],
                    pb.bitcast(BF16)[:, 0:128]), (ttr,) + bd("uqf"))
                setp(bank, tev)
                if kc % 8 == 7:
                    setb(("uq", kc // 8), ttr)
                l_toks.append(tev)
        mq_last = None
        for oc in range(4):
            mlasts = [None] * 4
            for kc in range(64):
                wb = (oc * 64 + kc) % 6
                tlw = P.dma("sc", "d_mw%d" % wb, lambda e, oc=oc, kc=kc, wb=wb: e.dma_start_transpose(
                    wmt[wb][:, :], sh_wout[oc * 512:(oc + 1) * 512, kc * 128:(kc + 1) * 128]),
                    deps=(tp2o,) + bd(("wmt", wb)))
                for t8 in range(4):
                    def mm0(e, kc=kc, t8=t8, wb=wb):
                        return e.matmul(ps[4 + t8][:, :], uqf[:, kc * 512 + t8 * 128: kc * 512 + (t8 + 1) * 128],
                                        wmt[wb][:, :], start=(kc == 0), stop=(kc == 63))
                    if kc == 0:
                        d = (tlw,) + pd(4 + t8) + (tuple(l_toks) if oc == 0 else ())
                    elif t8 == 0:
                        d = (tlw,)
                    else:
                        d = ()
                    mlasts[t8] = P.comp("te", mm0, d)
                setb(("wmt", wb), mlasts[3])
            mq_last = mlasts[3]
            for t8 in range(4):
                t = sup * 4 + t8
                fb = (oc * 4 + t8) % 2
                tev2 = P.comp("ve", lambda e, t8=t8, t=t, fb=fb: e.tensor_scalar(
                    flb[fb][:, :], ps[4 + t8][:, :], sc2c[:, t:t + 1], None, AL.mult),
                    (mlasts[t8], sc2_toks[t]) + bd(("flb", fb)))
                setp(4 + t8, tev2)
                tsp = P.dma("gp", "d_flw", lambda e, sup=sup, oc=oc, t8=t8, fb=fb: e.dma_start(
                    out=fl_sp[sup * 512 + t8 * 128: sup * 512 + (t8 + 1) * 128,
                              oc * 512:(oc + 1) * 512],
                    in_=flb[fb][:, :]), deps=(tev2,))
                setb(("flb", fb), tsp)
                m_toks.append(tsp)
        setb("uqf", mq_last)
        for f in range(sup * 4, sup * 4 + 4):
            emit_n_tile(f, m_toks[-1])

    if DBG:
        P.dma("sy", "d_tap", lambda e: e.dma_start(out=taps["tap_fl"], in_=fl_sp),
              deps=tuple(m_toks) + (P.tok("d_flw"),))
        P.dma("sy", "d_tap", lambda e: e.dma_start(
            out=taps["tap_sc2"][0:1, :].rearrange("o (t p) -> p (t o)", p=128),
            in_=sc2c[:, :], allow_slow_non_contiguous=True), deps=tuple(sc2_toks))

    mphase_end = [P.tok("e_ve"), P.tok("e_te"), P.tok("d_flw"), P.tok("d_x1w")]
    close_scope(markLM)

    fin = P.tok("d_out")

    def final_wait(e, Pr):
        e.wait_ge(Pr.sems["d_out"], fin[1])
    P.raw("sy", final_wait)

    with nc.Block() as block:
        @block.tensor
        def _(e):
            P.emit("te", e)

        @block.vector
        def _(e):
            P.emit("ve", e)

        @block.scalar
        def _(e):
            P.emit("sc", e)

        @block.gpsimd
        def _(e):
            P.emit("gp", e)

        @block.sync
        def _(e):
            P.emit("sy", e)

    for cm in reversed(cms):
        cm.__exit__(None, None, None)
    for cm in reversed(P._stack):
        cm.__exit__(None, None, None)

    return nc


_NC_CACHE = {}


def kernel(**inputs):
    x = np.asarray(inputs["L0_attn_x"], np.float32)
    B = x.shape[0]
    nonce = np.uint32(int.from_bytes(os.urandom(4), "little") | 1)

    if "nc" not in _NC_CACHE:
        _NC_CACHE["nc"] = build_program()
    nc = _NC_CACHE["nc"]

    g = lambda k: np.ascontiguousarray(np.asarray(inputs[k], np.float32))
    wq, wk, wv, wg, wo = g("L0_attn_wq"), g("L0_attn_wk"), g("L0_attn_wv"), g("L0_attn_wg"), g("L0_attn_wo")
    bq, bk, bv, bg, bo = g("L0_attn_bq"), g("L0_attn_bk"), g("L0_attn_bv"), g("L0_attn_bg"), g("L0_attn_bo")
    gnw, gnb, gam = g("L0_attn_gn_w"), g("L0_attn_gn_b"), g("L0_attn_gamma")
    wnorm, wgv, wout = g("L0_ffn_w_norm"), g("L0_ffn_w_gv"), g("L0_ffn_w_out")

    in_maps = []
    for c in range(8):
        b, par = c // 2, c % 2
        hs = slice(par * 1024, (par + 1) * 1024)
        in_maps.append({
            "x_full": np.ascontiguousarray(x[b]),
            "x_half": np.ascontiguousarray(x[b, par * T:(par + 1) * T]),
            "wq_h": np.ascontiguousarray(wq[:, hs]),
            "wk_h": np.ascontiguousarray(wk[:, hs]),
            "wv_h": np.ascontiguousarray(wv[:, hs]),
            "wg_h": np.ascontiguousarray(wg[:, hs]),
            "wo_h": np.ascontiguousarray(wo[hs, :]),
            "bq_h": np.ascontiguousarray(bq[hs])[None, :],
            "bk_h": np.ascontiguousarray(bk[hs])[None, :],
            "bv_h": np.ascontiguousarray(bv[hs])[None, :],
            "bg_h": np.ascontiguousarray(bg[hs])[None, :],
            "bo_f": bo[None, :],
            "gnw_h": np.ascontiguousarray(gnw[hs])[None, :],
            "gnb_h": np.ascontiguousarray(gnb[hs])[None, :],
            "gam_h": np.ascontiguousarray(gam[par * 8:(par + 1) * 8])[:, None],
            "wnorm_f": wnorm[None, :],
            "wgv_hf": np.ascontiguousarray(wgv[par * HID2:(par + 1) * HID2, :]),
            "wout_hf": np.ascontiguousarray(wout[:, par * 4096:(par + 1) * 4096]),
            "par_h": np.array([[par]], np.uint32),
            "non_h": np.array([[nonce]], np.uint32),
        })

    res = run_bass_kernel_spmd(nc, in_maps, core_ids=list(range(8)))
    out = np.empty((B, S, D), np.float32)
    for c in range(8):
        b, par = c // 2, c % 2
        out[b, par * T:(par + 1) * T] = res.results[c]["y_out"]
    if DBG:
        kernel._taps = res.results
    return out
